# version 04
# baseline (speedup 1.0000x reference)
import numpy as np
import ml_dtypes
from contextlib import ExitStack

import concourse.bass as bass
import concourse.mybir as mybir
from concourse.bass_utils import run_bass_kernel_spmd
from concourse.bass_core_rust import IndirectOffsetOnAxis

F32 = mybir.dt.float32
BF16 = mybir.dt.bfloat16
I32 = mybir.dt.int32
AF = mybir.ActivationFunctionType
ALU = mybir.AluOpType
AXX = mybir.AxisListType.X
BF16NP = ml_dtypes.bfloat16

NC = 8
BB = 2
SEQ = 4096
D = 2048
DF = 6144
IM = 8192
SLOC = 512
TLOC = BB * SLOC
HALO = 3
KSEL = 512
NSLOT = BB * KSEL
SLOTC = NSLOT // NC
HD = 128
NH = 16
NKV = 8
DCH = D // 128
FCH = DF // 128
ICH = IM // 128
KVCH = (NKV * HD) // 128
XNROWS = 1152
HS = HALO + SLOC
EPS_LN = 1e-5
EPS_RMS_A = 1e-8
EPS_RMS_B = 1e-6
C_RGLRU = 8.0
ROPE_THETA = 10000.0
INV_SQRT_HD = float(1.0 / np.sqrt(HD))
NEG_BIG = -1.0e30
BISECT_ITERS = 40
PAYW = 2048
KVPACK = SLOTC * HD * NKV * 2

_DEBUG_TAPS = ()
_LAST_RES = None
_PHASES = 8
_P8STOP = 99



def _block_major(w, kdim, ndim):
    kch, nblk = kdim // 128, ndim // 128
    return np.ascontiguousarray(
        w.reshape(kch, 128, nblk, 128).transpose(2, 1, 0, 3).reshape(nblk, 128, kch * 128))


_BIGW = [
    ("L0_attn_left_W", "a_left_W", D, D),
    ("L0_attn_right_W", "a_right_W", D, D),
    ("L0_attn_ga_W", "a_ga_W", D, D),
    ("L0_attn_gx_W", "a_gx_W", D, D),
    ("L0_attn_out_W", "a_out_W", D, D),
    ("L0_attn_up1_W", "a_up1_W", D, DF),
    ("L0_attn_up2_W", "a_up2_W", D, DF),
    ("L0_attn_down_W", "a_down_W", DF, D),
    ("L0_attn_down_W", "a_down_W_r", D, DF),
    ("L0_ffn_wq", "f_wq", D, D),
    ("L0_ffn_wk", "f_wk", D, NKV * HD),
    ("L0_ffn_wv", "f_wv", D, NKV * HD),
    ("L0_ffn_wo", "f_wo", D, D),
    ("L0_ffn_w_gate", "f_w_gate", D, IM),
    ("L0_ffn_w_up", "f_w_up", D, IM),
    ("L0_ffn_w_down", "f_w_down", IM, D),
]

_VECW = [
    ("L0_attn_norm1_w", "a_norm1_w"), ("L0_attn_left_b", "a_left_b"),
    ("L0_attn_right_b", "a_right_b"), ("L0_attn_conv_b", "a_conv_b"),
    ("L0_attn_ga_b", "a_ga_b"), ("L0_attn_gx_b", "a_gx_b"),
    ("L0_attn_a_param", "a_a_param"), ("L0_attn_out_b", "a_out_b"),
    ("L0_attn_norm2_w", "a_norm2_w"), ("L0_attn_up1_b", "a_up1_b"),
    ("L0_attn_up2_b", "a_up2_b"), ("L0_attn_down_b", "a_down_b"),
    ("L0_ffn_w_router", "f_w_router"), ("L0_ffn_bq", "f_bq"),
    ("L0_ffn_bk", "f_bk"), ("L0_ffn_bv", "f_bv"),
    ("L0_ffn_ln1_w", "f_ln1_w"), ("L0_ffn_ln2_w", "f_ln2_w"),
]


def _prepare_in_maps(inp):
    x = np.asarray(inp["L0_attn_x"], np.float32)
    shared = {}
    for iname, lname, kdim, ndim in _BIGW:
        if lname == "a_down_W_r":
            shared[lname] = np.ascontiguousarray(
                np.asarray(inp[iname], np.float32).reshape(DF // 128, 128, D))
        else:
            shared[lname] = _block_major(np.asarray(inp[iname], np.float32), kdim, ndim)
    for iname, lname in _VECW:
        shared[lname] = np.ascontiguousarray(np.asarray(inp[iname], np.float32))
    shared["eye_f"] = np.eye(128, dtype=np.float32)
    shared["eye_h"] = np.eye(128, dtype=np.float32).astype(BF16NP)
    shared["ones_col"] = np.ones((128, 1), np.float32)
    shared["ones_row"] = np.ones((1, 128), np.float32)
    shared["ltri"] = (np.arange(128)[:, None] < np.arange(128)[None, :]).astype(np.float32)
    invf = 1.0 / ROPE_THETA ** (np.arange(0, HD, 2, dtype=np.float32) / HD)
    shared["invf_dup"] = np.concatenate([invf, invf])[None, :].astype(np.float32)
    rot = np.zeros((128, 128), np.float32)
    for p in range(64):
        rot[p + 64, p] = -1.0
        rot[p, p + 64] = 1.0
    shared["rot_mat"] = rot
    conv_w = np.asarray(inp["L0_attn_conv_w"], np.float32)
    conv_diag = np.zeros((64, 128, 128), np.float32)
    for ch in range(DCH):
        for k in range(4):
            np.fill_diagonal(conv_diag[ch * 4 + k], conv_w[k, ch * 128:(ch + 1) * 128])
    shared["conv_diag"] = conv_diag
    zeros_rb = np.zeros(D, np.float32)

    in_maps = []
    p128 = np.arange(128)
    for c in range(NC):
        m = dict(shared)
        xs = np.zeros((BB, HS, D), np.float32)
        lo = c * SLOC
        h0 = max(0, lo - HALO)
        xs[:, HALO - (lo - h0):HALO] = x[:, h0:lo]
        xs[:, HALO:] = x[:, lo:lo + SLOC]
        m["x"] = xs
        m["right_b_halo"] = shared["a_right_b"] if c > 0 else zeros_rb
        m["rank_oh_col"] = np.tile((np.arange(NC) == c).astype(np.float32), (128, 1))
        tau = np.zeros((128, 8), np.float32)
        rowid = np.zeros((128, 8), np.float32)
        for tt in range(8):
            b, q = tt // 4, tt % 4
            tv = c * SLOC + q * 128 + p128
            tau[:, tt] = tv
            rowid[:, tt] = b * SEQ + tv
        m["taucol"] = tau
        m["rowidcol"] = rowid
        m["gath_rows"] = rowid.astype(np.int32)
        m["slotcol"] = (c * SLOTC + p128)[:, None].astype(np.float32)
        in_maps.append(m)
    return in_maps


def _assemble(results):
    out = np.zeros((BB, SEQ, D), np.float32)
    for c in range(NC):
        ob = results[c]["out_base"]
        out[0, c * SLOC:(c + 1) * SLOC] = ob[:SLOC]
        out[1, c * SLOC:(c + 1) * SLOC] = ob[SLOC:]
    flat = out.reshape(BB * SEQ, D)
    for c in range(NC):
        rid = results[c]["out_rowid"][:, 0].astype(np.int64)
        flat[rid] = results[c]["out_sel"]
    return out



ENGS = ("sync", "tensor", "vector", "scalar", "gpsimd")


class Plan:
    def __init__(self, nc):
        self.nc = nc
        self.items = []
        self.semh = {}
        self.cnt = {}

    def new_sem(self, es, name):
        self.semh[name] = es.enter_context(self.nc.semaphore(name))
        self.cnt[name] = 0

    SELF_SEM = {"vector": "dv", "scalar": "ac", "gpsimd": "gp"}

    def op(self, eng, fn, inc=None, inc_n=1):
        own = self.SELF_SEM.get(eng)
        if own is not None and inc == own:
            self.wait(eng, own)
        if inc is not None:
            self.cnt[inc] += inc_n
            sem, n = self.semh[inc], inc_n
            self.items.append((eng, lambda e: fn(e).then_inc(sem, n)))
            return self.cnt[inc]
        self.items.append((eng, lambda e: fn(e)))
        return None

    def wait(self, eng, sem, val=None):
        v = self.cnt[sem] if val is None else val
        if v <= 0:
            return
        s = self.semh[sem]
        self.items.append((eng, lambda e: e.wait_ge(s, v)))

    def val(self, sem):
        return self.cnt[sem]

    def replay(self, block):
        def mk(engname):
            def body(engobj):
                for eng, fn in self.items:
                    if eng == engname:
                        fn(engobj)
            return body
        for engname in ENGS:
            getattr(block, engname)(mk(engname))


def dma(P, queue, sem, out, in_, transpose=False, **kw):
    if transpose:
        return P.op(queue, lambda e: e.dma_start(out=out, in_=in_, transpose=True, **kw),
                    inc=sem, inc_n=16)
    return P.op(queue, lambda e: e.dma_start(out=out, in_=in_, **kw), inc=sem, inc_n=16)


class WPool:
    def __init__(self, P, g, names, sems):
        self.P, self.g = P, g
        self.names = list(names)
        self.sems = dict(zip(names, sems))
        self.free_after = {n: None for n in names}
        self.i = 0
        self.pending = []

    def load(self, wap_block, width):
        n = self.names[self.i % len(self.names)]
        self.i += 1
        fa = self.free_after[n]
        if fa is not None:
            self.P.wait("gpsimd", fa[0], fa[1])
        v = dma(self.P, "gpsimd", self.sems[n], self.g[n][:, 0:width], wap_block)
        return n, v

    def prefetch(self, wap_block, width):
        self.pending.append(self.load(wap_block, width))

    def take(self):
        return self.pending.pop(0)

    def consumed(self, bufname, sem, val):
        self.free_after[bufname] = (sem, val)



def _build_program(debug=()):
    nc = bass.Bass("TRN2", target_bir_lowering=False)
    g = {}

    def din(name, shape, dt=F32):
        g[name] = nc.dram_tensor(name, shape, dt, kind="ExternalInput").ap()
    def dout(name, shape, dt=F32):
        g[name] = nc.dram_tensor(name, shape, dt, kind="ExternalOutput").ap()
    def dint(name, shape, dt=F32):
        g[name] = nc.dram_tensor(name, shape, dt).ap()

    din("x", [BB, HS, D])
    for iname, lname, kdim, ndim in _BIGW:
        din(lname, [ndim // 128, 128, (kdim // 128) * 128])
    for iname, lname in _VECW:
        n = {"f_bk": NKV * HD, "f_bv": NKV * HD, "a_up1_b": DF, "a_up2_b": DF}.get(lname, D)
        din(lname, [n])
    din("right_b_halo", [D])
    din("eye_f", [128, 128]); din("eye_h", [128, 128], BF16)
    din("ones_col", [128, 1]); din("ones_row", [1, 128])
    din("ltri", [128, 128]); din("invf_dup", [1, 128]); din("rot_mat", [128, 128])
    din("conv_diag", [64, 128, 128])
    din("rank_oh_col", [128, 8]); din("taucol", [128, 8]); din("rowidcol", [128, 8])
    din("gath_rows", [128, 8], I32); din("slotcol", [128, 1])

    dout("out_base", [TLOC, D])
    dout("out_sel", [SLOTC, D])
    dout("out_rowid", [SLOTC, 1], I32)

    dint("d_xn_park", [XNROWS, D], BF16)
    dint("d_xln_park", [TLOC, D], BF16)
    dint("d_xln_lo", [TLOC, D], BF16)
    dint("d_left_park", [128, DCH * TLOC], BF16)
    dint("d_atn_park", [128, DCH * TLOC])
    dint("d_glu", [FCH, 128, TLOC], BF16)
    dint("d_xa_park", [TLOC, D], BF16)
    dint("d_ex_park", [TLOC, 8])
    dint("d_slotrec", [BB * SEQ, 4], I32)
    dint("d_send", [NSLOT, PAYW], BF16)
    dint("d_recv", [NSLOT, PAYW], BF16)
    dint("d_send_ex", [NSLOT, 8])
    dint("d_recv_ex", [NSLOT, 8])
    dint("d_scin", [TLOC, 1])
    dint("d_scag", [NC * TLOC, 1])
    dint("d_cin", [128, 64])
    dint("d_cout", [NC * 128, 64])
    dint("d_kvin", [1, KVPACK], BF16)
    dint("d_kvout", [NC, KVPACK], BF16)

    taps = []
    def tap(name, shape, dt=F32):
        if name in debug:
            g["tap_" + name] = nc.dram_tensor("tap_" + name, shape, dt,
                                              kind="ExternalOutput").ap()
            taps.append(name)
            return True
        return False

    es = ExitStack()
    with es:
        def sb(name, shape, dt=F32):
            g[name] = es.enter_context(nc.sbuf_tensor(name, shape, dt))

        def psm(name, shape, dt=F32):
            g[name] = es.enter_context(nc.psum_tensor(name, shape, dt))

        sb("cEYEf", [128, 128]); sb("cEYEh", [128, 128], BF16)
        sb("cONEc", [128, 1]); sb("cONEr", [1, 128]); sb("cLTRI", [128, 128])
        sb("cINVF", [1, 128]); sb("cROT", [128, 128], BF16)
        sb("cROH", [128, 8]); sb("cTAU", [128, 8]); sb("cROW", [128, 8])
        sb("cGRW", [128, 8], I32); sb("cSLOT", [128, 1])
        sb("BIAS", [128, 384])
        sb("ST1", [128, 16]); sb("ST2", [128, 16]); sb("ST3", [128, 16])
        sb("ROW1", [1, 128])
        sb("CST", [128, 8])
        sb("SCR", [128, SLOC]); sb("SCR2", [128, SLOC]); sb("SCR3", [128, SLOC])
        sb("WB0", [128, DCH * 128], BF16); sb("WB1", [128, DCH * 128], BF16)
        sb("WC0", [128, DCH * 128], BF16); sb("WC1", [128, DCH * 128], BF16)

        sb("ARA", [128, 9216])
        sb("ARB", [128, 8320])
        sb("ARC", [128, 8192])
        sb("ARD", [128, 8192])
        sb("ARE", [128, 8192])
        sb("ARF", [128, 4096])

        def carve(arena, f32_off, f32_len, dt=F32):
            ap = g[arena][:, f32_off:f32_off + f32_len]
            if dt is F32:
                return ap
            return ap.bitcast(dt)

        g["XN"] = carve("ARA", 0, 9216, BF16)
        g["IH"] = carve("ARB", 0, 8192, BF16)
        g["X1a"] = carve("ARA", 0, 8192)
        g["UEXT"] = carve("ARB", 0, 8240, BF16)
        g["X1b"] = carve("ARB", 0, 8192)
        g["W1BC"] = carve("ARC", 0, 2048)
        g["XTOK"] = carve("ARC", 2048, 2048)
        g["TOK2"] = carve("ARC", 4096, 2048)
        g["YB"] = carve("ARC", 0, 8192, BF16)
        g["RGT"] = carve("ARA", 0, 8192, BF16)
        g["WD0"] = carve("ARC", 0, 4096, BF16)
        g["WD1"] = carve("ARC", 4096, 4096, BF16)
        g["XTOK6"] = carve("ARC", 0, 2048)
        g["TOK26"] = carve("ARC", 2048, 2048)
        g["WRBC"] = carve("ARC", 4096, 2048)
        g["LAB"] = carve("ARD", 0, 8192, BF16)
        g["XLT0"] = carve("ARD", 0, 512, BF16)
        g["XLT1"] = carve("ARD", 512, 512, BF16)
        g["SQ4a"] = carve("ARD", 1024, 1024)
        g["SQ4b"] = carve("ARD", 2048, 1024)
        g["SS4"] = carve("ARD", 3072, 1024)
        g["RSB"] = carve("ARD", 4096, 1024)
        g["GB0"] = carve("ARD", 0, 256, BF16)
        g["GB1"] = carve("ARD", 256, 256, BF16)
        g["GSTG0"] = carve("ARD", 512, 512, BF16)
        g["GSTG1"] = carve("ARD", 1024, 512, BF16)
        g["DSTG0"] = carve("ARD", 6144, 512)
        g["DSTG1"] = carve("ARD", 6656, 512)
        g["ATNTa"] = carve("ARA", 0, 8192)
        g["ATNTb"] = carve("ARB", 0, 8192)
        g["ATHF0"] = carve("ARF", 1024, 1024)
        g["ATHF1"] = carve("ARF", 2048, 1024)
        g["BBT"] = carve("ARE", 0, 8192, BF16)
        g["XN2"] = carve("ARE", 0, 8192, BF16)
        g["SQ6"] = carve("ARE", 0, 2048)
        g["HID6"] = carve("ARE", 2048, 2048)
        g["SCO6"] = carve("ARE", 4096, 8)
        g["EXT6"] = carve("ARE", 4104, 8)
        g["CDIAG"] = carve("ARF", 0, 4096, BF16)
        g["AB"] = carve("ARF", 0, 64)
        g["CAR"] = carve("ARF", 64, 512)
        g["HPRE"] = carve("ARF", 576, 288)
        g["HMINE"] = carve("ARF", 864, 32)
        g["TOKH"] = carve("ARF", 1024, 1024, BF16)
        g["STAGE"] = carve("ARF", 2048, 128)
        g["HROW"] = carve("ARF", 2048, 2048)[0:1, :]
        g["SC7"] = carve("ARE", 4104, 64)
        g["MG7"] = carve("ARE", 4168, 32)
        g["MEQ7"] = carve("ARE", 4200, 32)
        g["MASK7"] = carve("ARE", 4232, 32)
        g["CEX7"] = carve("ARE", 4264, 32)
        g["INCL7"] = carve("ARE", 4296, 32)
        g["T7a"] = carve("ARE", 4328, 32)
        g["T7b"] = carve("ARE", 4360, 32)
        g["BIG7"] = carve("ARE", 4392, 32)
        g["ZERO7"] = carve("ARE", 4424, 32)
        g["STAG7"] = carve("ARE", 4456, 128).bitcast(I32)
        g["IDXI7"] = carve("ARE", 4584, 32).bitcast(I32)
        g["IDG7"] = carve("ARE", 4616, 4).bitcast(I32)
        g["EOFF7"] = carve("ARE", 4620, 1)
        g["MIDB7"] = carve("ARE", 4621, 1)
        g["HIB7"] = carve("ARE", 4622, 1)
        g["NEEDB7"] = carve("ARE", 4623, 1)
        g["CNTC7"] = carve("ARE", 4624, 1)
        g["P7A"] = carve("ARE", 3072, 1024, BF16)
        g["P7E"] = carve("ARE", 4626, 8)
        g["SCO8R"] = carve("ARE", 4640, 128)
        g["HID"] = carve("ARA", 0, 2048)
        g["XSEL"] = carve("ARA", 2048, 1024, BF16)
        g["EXF"] = carve("ARA", 3080, 8)
        g["EXI"] = carve("ARA", 3088, 8).bitcast(I32)
        g["QR"] = carve("ARA", 3096, 1024, BF16)
        g["KR"] = carve("ARA", 4120, 512, BF16)
        g["VTt"] = carve("ARA", 4632, 512, BF16)
        g["ANG"] = carve("ARA", 5144, 128)
        g["SINt"] = carve("ARA", 5272, 128)
        g["COSt"] = carve("ARA", 5400, 128)
        g["X1B"] = carve("ARA", 5528, 2048)
        g["QKVf"] = carve("ARA", 7576, 1024, BF16)
        g["T8"] = carve("ARA", 8600, 128)
        g["T8b"] = carve("ARA", 8728, 128)
        g["MHB8"] = carve("ARA", 8856, 1)
        g["KBH"] = carve("ARB", 0, 512, BF16)
        g["VBH"] = carve("ARB", 512, 512, BF16)
        g["ATTS"] = carve("ARB", 1024, 1024)
        g["ETS"] = carve("ARB", 2048, 512, BF16)
        g["EHS"] = carve("ARB", 2560, 512, BF16)
        g["AVO"] = carve("ARB", 3072, 1024, BF16)
        g["MASKM"] = carve("ARB", 4096, 1024)
        g["NEGT"] = carve("ARB", 5120, 1024)
        g["MTCH"] = carve("ARB", 6144, 512, BF16)
        g["QROW8"] = carve("ARB", 6656, 128)
        g["RBC8"] = carve("ARB", 6784, 128)
        g["KS8"] = carve("ARB", 6912, 8)
        g["DEN8"] = carve("ARB", 6920, 16)
        g["REC8"] = carve("ARB", 6936, 16)
        g["OSEL"] = carve("ARB", 6952, 1).bitcast(I32)
        g["H2"] = carve("ARD", 0, 1024, BF16)
        g["GLB"] = carve("ARD", 1024, 4096, BF16)
        g["PROC"] = carve("ARD", 5120, 2048)
        g["AOF"] = carve("ARD", 7168, 1024, BF16)
        g["MLPf"] = carve("ARE", 0, 1024, BF16)
        g["ATTV"] = carve("ARE", 1024, 2048)

        psm("PMA", [128, 1024]); psm("PMB", [128, 1024])
        psm("PMC", [128, 1024]); psm("PMD", [128, 1024])
        g["PM0"] = g["PMA"][:, :]
        g["PM1"] = g["PMB"][:, :]
        g["PN0"] = g["PMC"][:, 0:512]
        g["PN1"] = g["PMC"][:, 512:1024]
        g["PX"] = g["PMD"][:, 0:128]
        g["PS1"] = g["PMD"][0:1, 512:1024]

        P = Plan(nc)
        for s in ("mm", "dv", "ac", "gp", "cc", "qc", "qx", "qp", "qn",
                  "w0", "w1", "c0", "c1", "d0", "d1",
                  "q1", "q2", "q3", "q4", "q5", "q6", "q7", "q8", "q9",
                  "qa", "qb", "qd", "qe", "qf", "qg",
                  "qo1", "qo2", "qo3", "qtap"):
            P.new_sem(es, s)

        block = es.enter_context(nc.Block())
        _emit_all(nc, P, g, tap)
        P.replay(block)
    return nc, taps


def _bias_ap(g, name):
    off = g["_bias_off"][name]
    return g["BIAS"][:, off:off + g["_bias_n"][name]]



def _emit_all(nc, P, g, tap):
    phases = [_p1, _p2, _p3, _p4, _p5, _p6, _p7, _p8]
    _pre(nc, P, g)
    for i in range(min(_PHASES, 8)):
        phases[i](nc, P, g, tap)
    for sem in ("mm", "dv", "ac", "gp", "cc", "qc", "qx", "qp", "qn",
                "w0", "w1", "c0", "c1", "d0", "d1",
                "q1", "q2", "q3", "q4", "q5", "q6", "q7", "q8", "q9",
                "qa", "qb", "qd", "qe", "qf", "qg",
                "qo1", "qo2", "qo3", "qtap"):
        P.wait("gpsimd", sem)


CST_COLS = {}

def _cst(g, val, base=0, n=128):
    return g["CST"][base:base + n, CST_COLS[val]:CST_COLS[val] + 1]


def _pre(nc, P, g):
    for i, v in enumerate([EPS_LN, EPS_RMS_A, EPS_RMS_B, 1.0, 0.0]):
        CST_COLS[v] = i
        P.op("vector", lambda e, i_=i, v_=v: e.memset(g["CST"][:, i_:i_ + 1], v_),
             inc="dv")
    for n, dst in [("eye_f", "cEYEf"), ("eye_h", "cEYEh"), ("ones_col", "cONEc"),
                   ("ones_row", "cONEr"), ("ltri", "cLTRI"), ("invf_dup", "cINVF"),
                   ("rank_oh_col", "cROH"), ("taucol", "cTAU"), ("rowidcol", "cROW"),
                   ("gath_rows", "cGRW"), ("slotcol", "cSLOT")]:
        dma(P, "sync", "qc", g[dst][:], g[n])
    dma(P, "gpsimd", "q9", g["cROT"][:], g["rot_mat"])

    bias_vecs = [("a_norm1_w", DCH), ("a_left_b", DCH), ("a_right_b", DCH),
                 ("right_b_halo", DCH), ("a_conv_b", DCH), ("a_ga_b", DCH),
                 ("a_gx_b", DCH), ("a_a_param", DCH), ("a_out_b", DCH),
                 ("a_norm2_w", DCH), ("a_up1_b", FCH), ("a_up2_b", FCH),
                 ("a_down_b", DCH), ("f_bq", DCH), ("f_bk", NKV), ("f_bv", NKV),
                 ("f_ln1_w", DCH), ("f_ln2_w", DCH)]
    g["_bias_off"], g["_bias_n"] = {}, {}
    off = 0
    vz = P.op("vector", lambda e: e.memset(g["STAGE"][:], 0.0), inc="dv")
    P.wait("sync", "dv", vz)
    for name, nch in bias_vecs:
        g["_bias_off"][name], g["_bias_n"][name] = off, nch
        v = dma(P, "sync", "qc", g["STAGE"][0:nch, 0:128],
                g[name].rearrange("(a b) -> a b", b=128))
        P.wait("tensor", "qc", v)
        P.op("tensor", lambda e: e.transpose(g["PX"][:], g["STAGE"][:], g["cEYEf"][:]),
             inc="mm")
        P.wait("vector", "mm")
        v2 = P.op("vector", lambda e, o_=off, n_=nch:
                  e.tensor_copy(out=g["BIAS"][:, o_:o_ + n_], in_=g["PX"][:, 0:n_]),
                  inc="dv")
        P.wait("tensor", "dv", v2)
        P.wait("sync", "dv", v2)
        off += nch
    assert off <= 384

    ap = _bias_ap(g, "a_a_param")
    s1, s2 = g["ST1"][:, 0:DCH], g["ST2"][:, 0:DCH]
    P.wait("scalar", "dv")
    P.op("scalar", lambda e: e.activation(out=s1, in_=ap, func=AF.Abs), inc="ac")
    P.wait("scalar", "ac")
    P.op("scalar", lambda e: e.activation(out=s1, in_=s1, func=AF.Exp, scale=-1.0),
         inc="ac")
    P.wait("scalar", "ac")
    P.op("scalar", lambda e: e.activation(out=s1, in_=s1, func=AF.Ln, bias=_cst(g, 1.0)),
         inc="ac")
    P.wait("vector", "ac")
    P.op("vector", lambda e: e.tensor_scalar_max(s2, ap, 0.0), inc="dv")
    P.wait("vector", "dv")
    P.op("vector", lambda e: e.tensor_add(s2, s2, s1), inc="dv")
    P.wait("vector", "dv")
    P.op("vector", lambda e: e.tensor_scalar_mul(ap, s2, -C_RGLRU), inc="dv")

    _bcast_row(nc, P, g, g["a_norm1_w"], g["W1BC"], D)


def _bcast_row(nc, P, g, dram_vec, dst, n):
    v = dma(P, "sync", "qc", g["HROW"][0:1, 0:n], dram_vec.rearrange("(a b) -> a b", a=1))
    P.wait("tensor", "qc", v)
    for h in range(0, n, 1024):
        hw = min(1024, n - h)
        for j in range(0, hw, 512):
            jw = min(512, hw - j)
            P.op("tensor", lambda e, h_=h, j_=j, jw_=jw:
                 e.matmul(g["PM1"][:, j_:j_ + jw_], g["cONEr"][:],
                          g["HROW"][0:1, h_ + j_:h_ + j_ + jw_],
                          start=True, stop=True), inc="mm")
        P.wait("vector", "mm")
        v2 = P.op("vector", lambda e, h_=h, hw_=hw:
                  e.tensor_copy(out=dst[:, h_:h_ + hw_], in_=g["PM1"][:, 0:hw_]),
                  inc="dv")
        P.wait("tensor", "dv", v2)
    P.wait("sync", "dv", P.val("dv"))


def _ln_tile(nc, P, g, xap, base, n, eps, rms_eps=None, sq=None, out=None):
    st_m = g["ST1"][base:base + n, 0:1]
    st_r = g["ST2"][base:base + n, 0:1]
    st_3 = g["ST3"][base:base + n, 0:1]
    sq = g["TOK2"][base:base + n, :] if sq is None else sq
    P.op("vector", lambda e: e.reduce_sum(st_m, xap, axis=AXX), inc="dv")
    P.wait("vector", "dv")
    P.op("vector", lambda e: e.tensor_scalar_mul(st_m, st_m, 1.0 / D), inc="dv")
    P.wait("scalar", "dv")
    P.op("scalar", lambda e: e.activation(out=sq, in_=xap, func=AF.Square,
                                          accum_out=st_r), inc="ac")
    P.wait("vector", "ac")
    P.op("vector", lambda e: e.tensor_scalar_mul(st_r, st_r, 1.0 / D), inc="dv")
    P.op("vector", lambda e: e.tensor_mul(st_3, st_m, st_m), inc="dv")
    P.wait("vector", "dv")
    P.op("vector", lambda e: e.tensor_sub(st_r, st_r, st_3), inc="dv")
    P.wait("scalar", "dv")
    P.op("scalar", lambda e, e_=eps: e.activation(out=st_r, in_=st_r, func=AF.Ln,
                                                  bias=_cst(g, e_, base, n)), inc="ac")
    P.wait("scalar", "ac")
    P.op("scalar", lambda e: e.activation(out=st_r, in_=st_r, func=AF.Exp, scale=-0.5),
         inc="ac")
    P.wait("vector", "ac")
    out = xap if out is None else out
    P.op("vector", lambda e, o_=out: e.tensor_scalar(out=o_, in0=xap, scalar1=st_m,
                                                     scalar2=st_r, op0=ALU.subtract,
                                                     op1=ALU.mult), inc="dv")
    if rms_eps is not None:
        P.wait("scalar", "dv")
        P.op("scalar", lambda e, o_=out: e.activation(out=sq, in_=o_, func=AF.Square,
                                                      accum_out=st_3), inc="ac")
        P.wait("vector", "ac")
        P.op("vector", lambda e: e.tensor_scalar_mul(st_3, st_3, 1.0 / D), inc="dv")
        P.wait("scalar", "dv")
        P.op("scalar", lambda e, re_=rms_eps: e.activation(out=st_3, in_=st_3,
                                                           func=AF.Ln,
                                                           bias=_cst(g, re_, base, n)),
             inc="ac")
        P.wait("scalar", "ac")
        P.op("scalar", lambda e: e.activation(out=st_3, in_=st_3, func=AF.Exp,
                                              scale=-0.5), inc="ac")
        P.wait("vector", "ac")


def _p1(nc, P, g, tap):
    x = g["x"]
    tap
    tokh_free = 0
    xtok_free_dv = 0
    for tt in range(8):
        b, q = tt // 4, tt % 4
        if xtok_free_dv:
            P.wait("sync", "dv", xtok_free_dv)
        vld = dma(P, "sync", "qx", g["XTOK"][:], x[b, HALO + q * 128: HALO + (q + 1) * 128, :])
        P.wait("vector", "qx", vld)
        P.wait("scalar", "qx", vld)
        _ln_tile(nc, P, g, g["XTOK"][:], 0, 128, EPS_LN, EPS_RMS_A)
        if tokh_free:
            P.wait("vector", "qp", tokh_free)
        P.op("vector", lambda e: e.tensor_copy(out=g["TOKH"][:], in_=g["XTOK"][:]),
             inc="dv")
        vxl = P.val("dv")
        P.wait("scalar", "dv", vxl)
        dma(P, "scalar", "qp", g["d_xln_park"][tt * 128:(tt + 1) * 128, :], g["TOKH"][:])
        tokh_free = P.val("qp")
        P.op("vector", lambda e: e.tensor_sub(
            g["TOK2"].bitcast(BF16)[:, 0:D], g["XTOK"][:], g["TOKH"][:]), inc="dv")
        vlo = P.val("dv")
        P.wait("scalar", "dv", vlo)
        dma(P, "scalar", "qp", g["d_xln_lo"][tt * 128:(tt + 1) * 128, :],
            g["TOK2"].bitcast(BF16)[:, 0:D])
        tokh_free = P.val("qp")
        P.wait("vector", "qp", tokh_free)
        P.op("vector", lambda e: e.scalar_tensor_tensor(
            out=g["TOKH"][:], in0=g["XTOK"][:], scalar=g["ST3"][:, 0:1],
            in1=g["W1BC"][:], op0=ALU.mult, op1=ALU.mult), inc="dv")
        xtok_free_dv = P.val("dv")
        P.wait("scalar", "dv", xtok_free_dv)
        dma(P, "scalar", "qp", g["d_xn_park"][tt * 128:(tt + 1) * 128, :], g["TOKH"][:])
        tokh_free = P.val("qp")
    P.wait("sync", "dv", xtok_free_dv)
    dma(P, "sync", "qx", g["XTOK"][0:HALO, :], x[0, 0:HALO, :])
    vld = dma(P, "sync", "qx", g["XTOK"][32:32 + HALO, :], x[1, 0:HALO, :])
    P.wait("vector", "qx", vld)
    P.wait("scalar", "qx", vld)
    P.wait("vector", "qp", tokh_free)
    for base in (0, 32):
        _ln_tile(nc, P, g, g["XTOK"][base:base + HALO, :], base, HALO, EPS_LN,
                 EPS_RMS_A, sq=g["TOK2"][base:base + HALO, :])
        P.op("vector", lambda e, b_=base: e.scalar_tensor_tensor(
            out=g["TOKH"][b_:b_ + HALO, :], in0=g["XTOK"][b_:b_ + HALO, :],
            scalar=g["ST3"][b_:b_ + HALO, 0:1], in1=g["W1BC"][b_:b_ + HALO, :],
            op0=ALU.mult, op1=ALU.mult), inc="dv")
    vh = P.val("dv")
    P.wait("scalar", "dv", vh)
    dma(P, "scalar", "qp", g["d_xn_park"][1024:1024 + HALO, :], g["TOKH"][0:HALO, :])
    vhp = dma(P, "scalar", "qp", g["d_xn_park"][1027:1027 + HALO, :], g["TOKH"][32:32 + HALO, :])
    P.wait("vector", "qp", vhp)
    vz = P.op("vector", lambda e: e.memset(g["TOKH"][:], 0.0), inc="dv")
    P.wait("scalar", "dv", vz)
    dma(P, "scalar", "qp", g["d_xn_park"][1030:XNROWS, :], g["TOKH"][0:XNROWS - 1030, :])
    g["_p1_qt"] = P.val("qp")
    g["_p1_dv"] = P.val("dv")
    if tap("xlnpark", [TLOC, D], BF16):
        P.wait("gpsimd", "qp", g["_p1_qt"])
        dma(P, "gpsimd", "qtap", g["tap_xlnpark"], g["d_xln_park"])
    if tap("xnpark", [XNROWS, D], BF16):
        P.wait("gpsimd", "qp", g["_p1_qt"])
        dma(P, "gpsimd", "qtap", g["tap_xnpark"], g["d_xn_park"])


def _p2(nc, P, g, tap):
    P.wait("scalar", "qp", g["_p1_qt"])
    for ch in range(DCH):
        dma(P, "scalar", "qn", g["XN"][:, ch * XNROWS:(ch + 1) * XNROWS],
            g["d_xn_park"][:, ch * 128:(ch + 1) * 128], transpose=True)
    g["_xn_qt"] = P.val("qn")

    pool = WPool(P, g, ["WB0", "WB1"], ["w0", "w1"])
    g["_wpool"] = pool
    pool2 = WPool(P, g, ["WC0", "WC1"], ["c0", "c1"])
    g["_wpool2"] = pool2
    rb = _bias_ap(g, "a_right_b")
    rbh = _bias_ap(g, "right_b_halo")
    lb = _bias_ap(g, "a_left_b")
    psfree = g.setdefault("_psf", {})

    pool.prefetch(g["a_right_W"][0, :, :], DCH * 128)
    for oh in range(DCH):
        if oh + 1 < DCH:
            pool.prefetch(g["a_right_W"][oh + 1, :, :], DCH * 128)
        bn, lv = pool.take()
        pm = (g["PM0"], g["PM1"])[oh % 2]
        key = ("PM0", "PM1")[oh % 2]
        if psfree.get(key):
            P.wait("tensor", psfree[key][0], psfree[key][1])
        P.wait("tensor", pool.sems[bn], lv)
        if oh == 0:
            P.wait("tensor", "qn", g["_xn_qt"])
        if psfree.get("PH"):
            P.wait("tensor", psfree["PH"][0], psfree["PH"][1])
        for k in range(DCH):
            for (po, mo, wd) in [(0, 0, 512), (512, 512, 512)]:
                P.op("tensor", lambda e, k_=k, po_=po, mo_=mo, wd_=wd, pm_=pm, bn_=bn:
                     e.matmul(pm_[:, po_:po_ + wd_],
                              g[bn_][:, k_ * 128:(k_ + 1) * 128],
                              g["XN"][:, k_ * XNROWS + mo_: k_ * XNROWS + mo_ + wd_],
                              start=(k_ == 0), stop=(k_ == DCH - 1)), inc=None)
            P.op("tensor", lambda e, k_=k, bn_=bn:
                 e.matmul(g["PX"][:, 0:2 * HALO],
                          g[bn_][:, k_ * 128:(k_ + 1) * 128],
                          g["XN"][:, k_ * XNROWS + 1024: k_ * XNROWS + 1024 + 2 * HALO],
                          start=(k_ == 0), stop=(k_ == DCH - 1)),
                 inc="mm" if k == DCH - 1 else None)
        pool.consumed(bn, "mm", P.val("mm"))
        P.wait("vector", "mm")
        for (p0, u0) in [(0, HALO), (512, HS + HALO)]:
            P.op("vector", lambda e, oh_=oh, p0_=p0, u0_=u0, pm_=pm:
                 e.tensor_scalar_add(g["UEXT"][:, oh_ * BB * HS + u0_:
                                               oh_ * BB * HS + u0_ + 512],
                                     pm_[:, p0_:p0_ + 512],
                                     rb[:, oh_:oh_ + 1]), inc="dv")
        for b in range(BB):
            P.op("vector", lambda e, oh_=oh, b_=b:
                 e.tensor_scalar_add(g["UEXT"][:, oh_ * BB * HS + b_ * HS:
                                               oh_ * BB * HS + b_ * HS + HALO],
                                     g["PX"][:, b_ * HALO:(b_ + 1) * HALO],
                                     rbh[:, oh_:oh_ + 1]), inc="dv")
        vac = P.val("dv")
        psfree[key] = ("dv", vac)
        psfree["PH"] = ("dv", vac)
    g["_uext_dv"] = P.val("dv")

    g["_uext_done"] = True
    g["_p2_mm"] = P.val("mm")

    if tap("xn", [128, DCH * XNROWS], BF16):
        P.wait("gpsimd", "qn", g["_xn_qt"])
        dma(P, "gpsimd", "qtap", g["tap_xn"], g["XN"][:])
    if tap("uext", [128, DCH * BB * HS], BF16):
        P.wait("gpsimd", "dv", g["_uext_dv"])
        dma(P, "gpsimd", "qtap", g["tap_uext"], g["UEXT"][:, 0:DCH * BB * HS])



def _p3(nc, P, g, tap):
    pool, pool2 = g["_wpool"], g["_wpool2"]
    psf = g["_psf"]

    P.wait("gpsimd", "q9", 16)
    vcd = dma(P, "gpsimd", "q9", g["CDIAG"][:],
              g["conv_diag"].rearrange("n p f -> p n f"))

    cb = _bias_ap(g, "a_conv_b")
    pmh = [(g["PM0"], 0), (g["PM0"], 512), (g["PM1"], 0), (g["PM1"], 512)]
    P.wait("tensor", "q9", vcd)
    P.wait("tensor", "dv", g["_uext_dv"])
    i = 0
    for ch in range(DCH):
        for b in range(BB):
            pm, po = pmh[i % 4]
            key = "conv" + str(i % 4)
            if psf.get(key):
                P.wait("tensor", psf[key][0], psf[key][1])
            for k in range(4):
                P.op("tensor", lambda e, ch_=ch, b_=b, k_=k, pm_=pm, po_=po:
                     e.matmul(pm_[:, po_:po_ + 512],
                              g["CDIAG"][:, (ch_ * 4 + k_) * 128:(ch_ * 4 + k_ + 1) * 128],
                              g["UEXT"][:, ch_ * BB * HS + b_ * HS + k_:
                                        ch_ * BB * HS + b_ * HS + k_ + 512],
                              start=(k_ == 0), stop=(k_ == 3)),
                     inc="mm" if k == 3 else None)
            P.wait("vector", "mm")
            v = P.op("vector", lambda e, ch_=ch, b_=b, pm_=pm, po_=po:
                     e.tensor_scalar_add(g["YB"][:, ch_ * TLOC + b_ * 512:
                                                 ch_ * TLOC + b_ * 512 + 512],
                                         pm_[:, po_:po_ + 512],
                                         cb[:, ch_:ch_ + 1]), inc="dv")
            psf[key] = ("dv", v)
            i += 1
    g["_y_dv"] = P.val("dv")
    if tap("ym", [128, DCH * TLOC], BF16):
        P.wait("gpsimd", "dv", g["_y_dv"])
        dma(P, "gpsimd", "qtap", g["tap_ym"], g["YB"][:])

    gab = _bias_ap(g, "a_ga_b")
    gxb = _bias_ap(g, "a_gx_b")
    spm8 = _bias_ap(g, "a_a_param")
    pool.prefetch(g["a_ga_W"][0, :, :], DCH * 128)
    pool2.prefetch(g["a_gx_W"][0, :, :], DCH * 128)
    for oh in range(DCH):
        if oh + 1 < DCH:
            pool.prefetch(g["a_ga_W"][oh + 1, :, :], DCH * 128)
            pool2.prefetch(g["a_gx_W"][oh + 1, :, :], DCH * 128)
        else:
            pool.prefetch(g["a_left_W"][0, :, :], DCH * 128)
        bga, vga = pool.take()
        bgx, vgx = pool2.take()
        pma = (g["PM0"], g["PMC"])[oh % 2]
        pmx = (g["PM1"], g["PMD"])[oh % 2]
        keya, keyx = "ga" + str(oh % 2), "gx" + str(oh % 2)
        vmm_grp = []
        for kk, (bname, vv, pm_, pl_) in enumerate(
                [(bga, vga, pma, pool), (bgx, vgx, pmx, pool2)]):
            key = (keya, keyx)[kk]
            if psf.get(key):
                P.wait("tensor", psf[key][0], psf[key][1])
            P.wait("tensor", pl_.sems[bname], vv)
            if oh == 0 and kk == 0:
                P.wait("tensor", "dv", g["_y_dv"])
            for k in range(DCH):
                for hb in range(BB):
                    P.op("tensor", lambda e, k_=k, b_=bname, pm2=pm_, hb_=hb:
                         e.matmul(pm2[:, hb_ * 512:hb_ * 512 + 512],
                                  g[b_][:, k_ * 128:(k_ + 1) * 128],
                                  g["YB"][:, k_ * TLOC + hb_ * 512:
                                          k_ * TLOC + hb_ * 512 + 512],
                                  start=(k_ == 0), stop=(k_ == DCH - 1)),
                         inc="mm" if (k == DCH - 1 and hb == BB - 1) else None)
            pl_.consumed(bname, "mm", P.val("mm"))
            vmm_grp.append(P.val("mm"))
        for ss, skey in ((g["SCR"], "scA"), (g["SCR3"], "scB")):
            if psf.get(skey):
                P.wait("scalar", psf[skey][0], psf[skey][1])
        P.wait("scalar", "mm", vmm_grp[0])
        va1 = P.op("scalar", lambda e, oh_=oh, pm_=pma:
                   e.activation(out=g["SCR"][:], in_=pm_[:, 0:512],
                                func=AF.Sigmoid, bias=gab[:, oh_:oh_ + 1]),
                   inc="ac")
        va1b = P.op("scalar", lambda e, oh_=oh, pm_=pma:
                    e.activation(out=g["SCR3"][:], in_=pm_[:, 512:1024],
                                 func=AF.Sigmoid, bias=gab[:, oh_:oh_ + 1]),
                    inc="ac")
        P.wait("scalar", "mm", vmm_grp[1])
        va2 = P.op("scalar", lambda e, oh_=oh, pm_=pmx:
                   e.activation(out=g["IH"][:, oh_ * 1024:oh_ * 1024 + 512],
                                in_=pm_[:, 0:512],
                                func=AF.Sigmoid, bias=gxb[:, oh_:oh_ + 1]),
                   inc="ac")
        va2b = P.op("scalar", lambda e, oh_=oh, pm_=pmx:
                    e.activation(out=g["IH"][:, oh_ * 1024 + 512:oh_ * 1024 + 1024],
                                 in_=pm_[:, 512:1024],
                                 func=AF.Sigmoid, bias=gxb[:, oh_:oh_ + 1]),
                    inc="ac")
        psf[keya] = ("ac", va1b)
        psf[keyx] = ("ac", va2b)
        P.wait("vector", "ac", va1)
        vla0 = P.op("vector", lambda e, oh_=oh:
                    e.tensor_scalar_mul(
                        g["LAB"][:, oh_ * TLOC: oh_ * TLOC + 512],
                        g["SCR"][:], spm8[:, oh_:oh_ + 1]), inc="dv")
        P.wait("vector", "ac", va1b)
        vla1 = P.op("vector", lambda e, oh_=oh:
                    e.tensor_scalar_mul(
                        g["LAB"][:, oh_ * TLOC + 512: oh_ * TLOC + 1024],
                        g["SCR3"][:], spm8[:, oh_:oh_ + 1]), inc="dv")
        psf["scA"] = ("dv", vla0)
        psf["scB"] = ("dv", vla1)
    for gb in range(2 * DCH):
        oh, hb = gb // 2, gb % 2
        la_ap = g["LAB"][:, oh * TLOC + hb * 512: oh * TLOC + hb * 512 + 512]
        scr_m = g["SCR"] if gb % 2 == 0 else g["SCR3"]
        key = "scr" + str(gb % 2)
        if psf.get(key):
            P.wait("scalar", psf[key][0], psf[key][1])
        if gb == 0:
            P.wait("scalar", "dv")
        P.op("scalar", lambda e, la_=la_ap, s_=scr_m:
             e.activation(out=s_[:], in_=la_, func=AF.Exp, scale=2.0), inc="ac")
        P.wait("scalar", "ac")
        P.op("scalar", lambda e, s_=scr_m:
             e.activation(out=s_[:], in_=s_[:], func=AF.Ln, scale=-1.0,
                          bias=_cst(g, 1.0)), inc="ac")
        P.wait("scalar", "ac")
        vmu = P.op("scalar", lambda e, s_=scr_m:
                   e.activation(out=s_[:], in_=s_[:], func=AF.Exp, scale=0.5),
                   inc="ac")
        P.wait("vector", "ac", vmu)
        P.op("vector", lambda e, oh_=oh, hb_=hb:
             e.tensor_mul(g["SCR2"][:],
                          g["IH"][:, oh_ * 1024 + hb_ * 512:
                                  oh_ * 1024 + hb_ * 512 + 512],
                          g["YB"][:, oh_ * TLOC + hb_ * 512:
                                  oh_ * TLOC + hb_ * 512 + 512]), inc="dv")
        P.wait("vector", "dv")
        vb = P.op("vector", lambda e, oh_=oh, hb_=hb, s_=scr_m:
                  e.tensor_mul(g["BBT"][:, oh_ * TLOC + hb_ * 512:
                                        oh_ * TLOC + hb_ * 512 + 512],
                               g["SCR2"][:], s_[:]), inc="dv")
        psf[key] = ("dv", vb)
    g["_b_dv"] = P.val("dv")
    g["_gates_ac"] = P.val("ac")
    lb = _bias_ap(g, "a_left_b")
    for oh in range(DCH):
        if oh + 1 < DCH:
            pool.prefetch(g["a_left_W"][oh + 1, :, :], DCH * 128)
        bn, lv = pool.take()
        pm = (g["PM0"], g["PM1"])[oh % 2]
        key = ("PM0", "PM1")[oh % 2]
        if oh < 2:
            P.wait("tensor", "ac", g["_gates_ac"])
        if psf.get(key):
            P.wait("tensor", psf[key][0], psf[key][1])
        P.wait("tensor", pool.sems[bn], lv)
        for k in range(DCH):
            for ri, (po, mo, wd) in enumerate([(0, 0, 512), (512, 512, 512)]):
                P.op("tensor", lambda e, k_=k, po_=po, mo_=mo, wd_=wd, pm_=pm, bn_=bn:
                     e.matmul(pm_[:, po_:po_ + wd_],
                              g[bn_][:, k_ * 128:(k_ + 1) * 128],
                              g["XN"][:, k_ * XNROWS + mo_: k_ * XNROWS + mo_ + wd_],
                              start=(k_ == 0), stop=(k_ == DCH - 1)),
                     inc="mm" if (k == DCH - 1 and ri == 1) else None)
        pool.consumed(bn, "mm", P.val("mm"))
        P.wait("scalar", "mm")
        if oh == 0:
            P.wait("scalar", "dv", g["_b_dv"])
            P.wait("scalar", "qtap")
        for p0 in (0, 512):
            P.op("scalar", lambda e, oh_=oh, p0_=p0, pm_=pm:
                 e.activation(out=g["YB"][:, oh_ * TLOC + p0_: oh_ * TLOC + p0_ + 512],
                              in_=pm_[:, p0_:p0_ + 512], func=AF.Gelu,
                              bias=lb[:, oh_:oh_ + 1]), inc="ac")
        psf[key] = ("ac", P.val("ac"))
    g["_left_ac"] = P.val("ac")
    g["_left_mm"] = P.val("mm")
    if tap("la", [128, DCH * TLOC], BF16):
        P.wait("gpsimd", "dv", g["_b_dv"])
        dma(P, "gpsimd", "qtap", g["tap_la"], g["LAB"][:])
    if tap("bb", [128, DCH * TLOC], BF16):
        P.wait("gpsimd", "dv", g["_b_dv"])
        dma(P, "gpsimd", "qtap", g["tap_bb"], g["BBT"][:])

    for gb in range(32):
        oh, b = gb // 2, gb % 2
        P.op("vector", lambda e, oh_=oh, b_=b, gb_=gb:
             e.reduce_sum(g["AB"][:, gb_:gb_ + 1],
                          g["LAB"][:, oh_ * TLOC + b_ * 512:
                                   oh_ * TLOC + b_ * 512 + 512], axis=AXX),
             inc="dv")
    P.wait("scalar", "dv")
    P.op("scalar", lambda e: e.activation(out=g["AB"][:, 0:32], in_=g["AB"][:, 0:32],
                                          func=AF.Exp), inc="ac")
    vab = P.val("ac")
    for gb in range(32):
        oh, b = gb // 2, gb % 2
        scr_a = g["SCR"] if gb % 2 == 0 else g["SCR3"]
        key = "scn" + str(gb % 2)
        if psf.get(key):
            P.wait("scalar", psf[key][0], psf[key][1])
        P.op("scalar", lambda e, oh_=oh, b_=b, s_=scr_a:
             e.activation(out=s_[:], in_=g["LAB"][:, oh_ * TLOC + b_ * 512:
                                                  oh_ * TLOC + b_ * 512 + 512],
                          func=AF.Exp), inc="ac")
        P.wait("vector", "ac")
        P.op("vector", lambda e, oh_=oh, b_=b, s_=scr_a:
             e.tensor_tensor_scan(out=g["SCR2"][:], data0=s_[:],
                                  data1=g["BBT"][:, oh_ * TLOC + b_ * 512:
                                                 oh_ * TLOC + b_ * 512 + 512],
                                  initial=0.0, op0=ALU.mult, op1=ALU.add),
             inc="dv")
        P.wait("vector", "dv")
        vcp = P.op("vector", lambda e, gb_=gb:
                   e.tensor_copy(out=g["AB"][:, 32 + gb_:33 + gb_],
                                 in_=g["SCR2"][:, 511:512]), inc="dv")
        psf[key] = ("dv", vcp)
    P.wait("gpsimd", "ac", vab)
    P.wait("gpsimd", "dv")
    vci = dma(P, "gpsimd", "q2", g["d_cin"], g["AB"][:])
    P.wait("gpsimd", "q2", vci)
    P.op("gpsimd", lambda e: e.collective_compute(
        "AllGather", ALU.bypass, replica_groups=[list(range(NC))],
        ins=[g["d_cin"].tensor.ap().opt()], outs=[g["d_cout"].tensor.ap().opt()]),
        inc="cc")
    P.wait("gpsimd", "cc")
    vcar = dma(P, "gpsimd", "q3", g["CAR"][:], g["d_cout"])
    P.wait("vector", "q3", vcar)
    P.op("vector", lambda e: e.memset(g["HPRE"][:, 0:32], 0.0), inc="dv")
    for r in range(NC):
        P.wait("vector", "dv")
        P.op("vector", lambda e, r_=r:
             e.tensor_mul(g["HPRE"][:, (r_ + 1) * 32:(r_ + 2) * 32],
                          g["CAR"][:, r_ * 64: r_ * 64 + 32],
                          g["HPRE"][:, r_ * 32:(r_ + 1) * 32]), inc="dv")
        P.wait("vector", "dv")
        P.op("vector", lambda e, r_=r:
             e.tensor_add(g["HPRE"][:, (r_ + 1) * 32:(r_ + 2) * 32],
                          g["HPRE"][:, (r_ + 1) * 32:(r_ + 2) * 32],
                          g["CAR"][:, r_ * 64 + 32: r_ * 64 + 64]), inc="dv")
    P.op("vector", lambda e: e.memset(g["HMINE"][:], 0.0), inc="dv")
    for r in range(NC):
        P.wait("vector", "dv")
        P.op("vector", lambda e, r_=r:
             e.scalar_tensor_tensor(out=g["HMINE"][:],
                                    in0=g["HPRE"][:, r_ * 32:(r_ + 1) * 32],
                                    scalar=g["cROH"][:, r_:r_ + 1],
                                    in1=g["HMINE"][:], op0=ALU.mult, op1=ALU.add),
             inc="dv")
    P.wait("vector", "qtap")
    P.wait("vector", "mm", g.get("_left_mm", 0))
    for gb in range(32):
        oh, b = gb // 2, gb % 2
        scr_a = g["SCR"] if gb % 2 == 0 else g["SCR3"]
        key = "scn" + str(gb % 2)
        if psf.get(key):
            P.wait("scalar", psf[key][0], psf[key][1])
        if gb == 0:
            P.wait("scalar", "dv")
        P.op("scalar", lambda e, oh_=oh, b_=b, s_=scr_a:
             e.activation(out=s_[:], in_=g["LAB"][:, oh_ * TLOC + b_ * 512:
                                                  oh_ * TLOC + b_ * 512 + 512],
                          func=AF.Exp), inc="ac")
        P.wait("vector", "ac")
        vsc = P.op("vector", lambda e, oh_=oh, b_=b, gb_=gb, s_=scr_a:
                   e.tensor_tensor_scan(
                       out=g["RGT"][:, oh_ * TLOC + b_ * 512:
                                    oh_ * TLOC + b_ * 512 + 512],
                       data0=s_[:],
                       data1=g["BBT"][:, oh_ * TLOC + b_ * 512:
                                      oh_ * TLOC + b_ * 512 + 512],
                       initial=g["HMINE"][:, gb_:gb_ + 1],
                       op0=ALU.mult, op1=ALU.add), inc="dv")
        psf[key] = ("dv", vsc)
    g["_scan2_dv"] = P.val("dv")
    if tap("right", [128, DCH * TLOC], BF16):
        P.wait("gpsimd", "dv", g["_scan2_dv"])
        dma(P, "gpsimd", "qtap", g["tap_right"], g["RGT"][:])
    P.wait("vector", "ac", g["_left_ac"])
    for ch in range(DCH):
        vmu = P.op("vector", lambda e, ch_=ch:
                   e.tensor_mul(g["YB"][:, ch_ * TLOC:(ch_ + 1) * TLOC],
                                g["YB"][:, ch_ * TLOC:(ch_ + 1) * TLOC],
                                g["RGT"][:, ch_ * TLOC:(ch_ + 1) * TLOC]), inc="dv")
    g["_lr_dv"] = P.val("dv")
    if tap("lr", [128, DCH * TLOC], BF16):
        P.wait("gpsimd", "dv", g["_lr_dv"])
        dma(P, "gpsimd", "qtap", g["tap_lr"], g["YB"][:])


def _x1c(g, ch):
    if ch < 8:
        return g["X1a"][:, ch * 1024:(ch + 1) * 1024]
    return g["X1b"][:, (ch - 8) * 1024:(ch - 7) * 1024]


def _p4(nc, P, g, tap):
    pool = g["_wpool"]
    psf = g["_psf"]
    ob = _bias_ap(g, "a_out_b")
    n2w = _bias_ap(g, "a_norm2_w")

    P.wait("scalar", "qp", g["_p1_qt"])
    P.wait("scalar", "qtap")
    P.wait("vector", "qtap")
    for oh in range(DCH):
        xbuf = ("XLT0", "XLT1")[oh % 2]
        lbuf = ("SQ4a", "SQ4b")[oh % 2]
        qsem = ("q4", "q5")[oh % 2]
        key = "xlt" + str(oh % 2)
        if psf.get(key):
            P.wait("scalar", psf[key][0], psf[key][1])
        vxl = dma(P, "scalar", qsem, g[xbuf][:, 0:1024],
                  g["d_xln_park"][:, oh * 128:(oh + 1) * 128], transpose=True)
        vxl = dma(P, "scalar", qsem, g[lbuf].bitcast(BF16)[:, 0:1024],
                  g["d_xln_lo"][:, oh * 128:(oh + 1) * 128], transpose=True)
        if oh == 0:
            pool.prefetch(g["a_out_W"][0, :, :], DCH * 128)
        if oh + 1 < DCH:
            pool.prefetch(g["a_out_W"][oh + 1, :, :], DCH * 128)
        bn, lv = pool.take()
        pm = (g["PM0"], g["PM1"])[oh % 2]
        pkey = "PM" + str(oh % 2)
        if psf.get(pkey):
            P.wait("tensor", psf[pkey][0], psf[pkey][1])
        P.wait("tensor", pool.sems[bn], lv)
        if oh == 0:
            P.wait("tensor", "dv", g["_lr_dv"])
        for k in range(DCH):
            for ri, (po, mo, wd) in enumerate([(0, 0, 512), (512, 512, 512)]):
                P.op("tensor", lambda e, k_=k, po_=po, mo_=mo, wd_=wd, pm_=pm, bn_=bn:
                     e.matmul(pm_[:, po_:po_ + wd_],
                              g[bn_][:, k_ * 128:(k_ + 1) * 128],
                              g["YB"][:, k_ * TLOC + mo_: k_ * TLOC + mo_ + wd_],
                              start=(k_ == 0), stop=(k_ == DCH - 1)),
                     inc="mm" if (k == DCH - 1 and ri == 1) else None)
        pool.consumed(bn, "mm", P.val("mm"))
        P.wait("vector", "mm")
        P.wait("vector", qsem, vxl)
        P.op("vector", lambda e, oh_=oh, pm_=pm, b_=xbuf:
             e.scalar_tensor_tensor(out=_x1c(g, oh_), in0=pm_[:],
                                    scalar=ob[:, oh_:oh_ + 1],
                                    in1=g[b_][:, 0:1024],
                                    op0=ALU.add, op1=ALU.add), inc="dv")
        vev = P.op("vector", lambda e, oh_=oh, b_=lbuf:
                   e.tensor_add(_x1c(g, oh_), _x1c(g, oh_),
                                g[b_].bitcast(BF16)[:, 0:1024]), inc="dv")
        psf[pkey] = ("dv", vev)
        psf[key] = ("dv", vev)
    g["_x1_dv"] = P.val("dv")
    if tap("x1a", [128, 8192]):
        P.wait("gpsimd", "dv", g["_x1_dv"])
        dma(P, "gpsimd", "qtap", g["tap_x1a"], g["X1a"])
    if tap("x1b", [128, 8192]):
        P.wait("gpsimd", "dv", g["_x1_dv"])
        dma(P, "gpsimd", "qtap", g["tap_x1b"], g["X1b"])

    P.wait("scalar", "dv", g["_x1_dv"])
    for half in range(2):
        for oh in range(DCH):
            sq = (g["SQ4a"], g["SQ4b"])[oh % 2]
            key = "sq" + str(oh % 2)
            if psf.get(key):
                P.wait("scalar", psf[key][0], psf[key][1])
            P.op("scalar", lambda e, oh_=oh, s_=sq, h_=half:
                 e.activation(out=s_[:, 0:512],
                              in_=_x1c(g, oh_)[:, h_ * 512:(h_ + 1) * 512],
                              func=AF.Square), inc="ac")
            P.wait("tensor", "ac")
            if oh == 0:
                P.wait("tensor", "dv")
            P.op("tensor", lambda e, s_=sq, oh_=oh:
                 e.matmul(g["PS1"][:], g["cONEc"][:], s_[:, 0:512],
                          start=(oh_ == 0), stop=(oh_ == DCH - 1)), inc="mm")
            psf[key] = ("mm", P.val("mm"))
        P.wait("vector", "mm")
        vss = P.op("vector", lambda e, h_=half:
                   e.tensor_copy(out=g["SS4"][0:1, h_ * 512:(h_ + 1) * 512],
                                 in_=g["PS1"][:]), inc="dv")
    P.wait("scalar", "dv", vss)
    P.op("scalar", lambda e: e.activation(out=g["SS4"][0:1, :], in_=g["SS4"][0:1, :],
                                          func=AF.Ln, scale=1.0 / D,
                                          bias=_cst(g, EPS_RMS_A, 0, 1)), inc="ac")
    P.wait("scalar", "ac")
    vrs = P.op("scalar", lambda e: e.activation(out=g["SS4"][0:1, :],
                                                in_=g["SS4"][0:1, :],
                                                func=AF.Exp, scale=-0.5), inc="ac")
    P.wait("tensor", "ac", vrs)
    for half in range(2):
        P.op("tensor", lambda e, h_=half:
             e.matmul(g["PM0"][:, h_ * 512:(h_ + 1) * 512], g["cONEr"][:],
                      g["SS4"][0:1, h_ * 512:(h_ + 1) * 512], start=True, stop=True),
             inc="mm")
    P.wait("vector", "mm")
    vrsb = P.op("vector", lambda e: e.tensor_copy(out=g["RSB"][:], in_=g["PM0"][:]),
                inc="dv")
    for oh in range(DCH):
        P.op("vector", lambda e, oh_=oh:
             e.scalar_tensor_tensor(out=g["XN2"][:, oh_ * 1024:(oh_ + 1) * 1024],
                                    in0=_x1c(g, oh_), scalar=n2w[:, oh_:oh_ + 1],
                                    in1=g["RSB"][:], op0=ALU.mult, op1=ALU.mult),
             inc="dv")
    g["_xn2_dv"] = P.val("dv")
    g["_pm0_free"] = ("dv", vrsb)
    if tap("xn2", [128, DCH * TLOC], BF16):
        P.wait("gpsimd", "dv", g["_xn2_dv"])
        dma(P, "gpsimd", "qtap", g["tap_xn2"], g["XN2"][:])


def _p5(nc, P, g, tap):
    pool, pool2 = g["_wpool"], g["_wpool2"]
    psf = g["_psf"]
    u1b = _bias_ap(g, "a_up1_b")
    u2b = _bias_ap(g, "a_up2_b")
    dnb = _bias_ap(g, "a_down_b")

    P.wait("scalar", "qtap")
    glu_w = []
    pool.prefetch(g["a_up1_W"][0, :, :], DCH * 128)
    pool2.prefetch(g["a_up2_W"][0, :, :], DCH * 128)
    for of in range(FCH):
        if of + 1 < FCH:
            pool.prefetch(g["a_up1_W"][of + 1, :, :], DCH * 128)
            pool2.prefetch(g["a_up2_W"][of + 1, :, :], DCH * 128)
        b1, v1 = pool.take()
        b2, v2 = pool2.take()
        pmg = (g["PM0"], g["PM1"])[of % 2]
        pmu = (g["PN0"], g["PN1"])[of % 2]
        keyg, keyu = "pmg" + str(of % 2), "pmu" + str(of % 2)
        if psf.get(keyg):
            P.wait("tensor", psf[keyg][0], psf[keyg][1])
        P.wait("tensor", pool.sems[b1], v1)
        if of == 0:
            P.wait("tensor", "dv", g["_xn2_dv"])
        for k in range(DCH):
            for ri, (po, mo, wd) in enumerate([(0, 0, 512), (512, 512, 512)]):
                P.op("tensor", lambda e, k_=k, po_=po, mo_=mo, wd_=wd, pm_=pmg, b_=b1:
                     e.matmul(pm_[:, po_:po_ + wd_],
                              g[b_][:, k_ * 128:(k_ + 1) * 128],
                              g["XN2"][:, k_ * TLOC + mo_: k_ * TLOC + mo_ + wd_],
                              start=(k_ == 0), stop=(k_ == DCH - 1)),
                     inc="mm" if (k == DCH - 1 and ri == 1) else None)
        pool.consumed(b1, "mm", P.val("mm"))
        P.wait("scalar", "mm")
        scr_g = (g["SCR"], g["SCR3"])[of % 2]
        skey = "scr5" + str(of % 2)
        if psf.get(skey):
            P.wait("scalar", psf[skey][0], psf[skey][1])
        vg0 = P.op("scalar", lambda e, of_=of, pm_=pmg, s_=scr_g:
                   e.activation(out=s_[:], in_=pm_[:, 0:512], func=AF.Gelu,
                                bias=u1b[:, of_:of_ + 1]), inc="ac")
        vg1 = P.op("scalar", lambda e, of_=of, pm_=pmg:
                   e.activation(out=g["SCR2"][:], in_=pm_[:, 512:1024], func=AF.Gelu,
                                bias=u1b[:, of_:of_ + 1]), inc="ac")
        psf[keyg] = ("ac", vg1)
        gst = (g["GSTG0"], g["GSTG1"])[of % 2]
        gkey = "gst" + str(of % 2)
        for hb in range(2):
            if psf.get(keyu):
                P.wait("tensor", psf[keyu][0], psf[keyu][1])
            P.wait("tensor", pool2.sems[b2], v2)
            for k in range(DCH):
                P.op("tensor", lambda e, k_=k, hb_=hb, pm_=pmu, b_=b2:
                     e.matmul(pm_[:], g[b_][:, k_ * 128:(k_ + 1) * 128],
                              g["XN2"][:, k_ * TLOC + hb_ * 512:
                                       k_ * TLOC + hb_ * 512 + 512],
                              start=(k_ == 0), stop=(k_ == DCH - 1)),
                     inc="mm" if k == DCH - 1 else None)
            P.wait("vector", "mm")
            P.wait("vector", "ac", (vg0, vg1)[hb])
            if hb == 0 and psf.get(gkey):
                P.wait("vector", psf[gkey][0], psf[gkey][1])
            vgl = P.op("vector", lambda e, of_=of, hb_=hb, pm_=pmu, gs_=gst,
                       s_=(scr_g, g["SCR2"])[hb]:
                       e.scalar_tensor_tensor(out=gs_[:, hb_ * 512:(hb_ + 1) * 512],
                                              in0=pm_[:], scalar=u2b[:, of_:of_ + 1],
                                              in1=s_[:], op0=ALU.add, op1=ALU.mult),
                       inc="dv")
            psf[keyu] = ("dv", vgl)
            psf["scr5" + str(of % 2) if hb == 0 else skey] = ("dv", vgl)
        pool2.consumed(b2, "mm", P.val("mm"))
        P.wait("scalar", "dv", vgl)
        gq = ("q7", "qo2")[of % 2]
        vgw = dma(P, "scalar", gq, g["d_glu"][of, :, :], gst[:, 0:TLOC])
        psf[gkey] = (gq, vgw)
    g["_glu_q7"] = P.val("q7")
    g["_glu_qo2"] = P.val("qo2")

    slots = [(g["PMA"], 0), (g["PMA"], 512), (g["PMB"], 0), (g["PMB"], 512),
             (g["PMC"], 0), (g["PMC"], 512), (g["PMD"], 0), (g["PMD"], 512)]
    P.wait("sync", "q7", g["_glu_q7"])
    P.wait("sync", "qo2", g["_glu_qo2"])
    P.wait("sync", "dv")
    P.wait("gpsimd", "dv")
    atn_q = 0
    for hb in range(2):
        for grp in range(2):
            if psf.get("down"):
                P.wait("tensor", psf["down"][0], psf["down"][1])
            def _ld_gb(kf_, hb_):
                gb_ = ("GB0", "GB1")[kf_ % 2]
                gq_ = ("q4", "q5")[kf_ % 2]
                gkey_ = "gb" + str(kf_ % 2)
                if psf.get(gkey_):
                    P.wait("sync", psf[gkey_][0], psf[gkey_][1])
                    psf[gkey_] = None
                v_ = dma(P, "sync", gq_, g[gb_][:, 0:512],
                         g["d_glu"][kf_, :, hb_ * 512:(hb_ + 1) * 512])
                wd_ = ("WD0", "WD1")[kf_ % 2]
                wq_ = ("d0", "d1")[kf_ % 2]
                wkey_ = "wd" + str(kf_ % 2)
                if psf.get(wkey_):
                    P.wait("gpsimd", psf[wkey_][0], psf[wkey_][1])
                    psf[wkey_] = None
                v2_ = dma(P, "gpsimd", wq_, g[wd_][:, 0:1024],
                          g["a_down_W_r"][kf_, :, grp * 1024:(grp + 1) * 1024])
                return (gb_, gq_, v_, wd_, wq_, v2_)
            pend_d = [_ld_gb(0, hb)]
            for kf in range(FCH):
                if kf + 1 < FCH:
                    pend_d.append(_ld_gb(kf + 1, hb))
                gb, gq, vgb, wd, wq, vwd = pend_d.pop(0)
                gkey = "gb" + str(kf % 2)
                wkey = "wd" + str(kf % 2)
                P.wait("tensor", gq, vgb)
                P.wait("tensor", wq, vwd)
                for j in range(8):
                    P.op("tensor", lambda e, j_=j, kf_=kf, gb_=gb, wd_=wd:
                         e.matmul(slots[j_][0][:, slots[j_][1]:slots[j_][1] + 512],
                                  g[wd_][:, j_ * 128:(j_ + 1) * 128],
                                  g[gb_][:, 0:512],
                                  start=(kf_ == 0), stop=(kf_ == FCH - 1)),
                         inc="mm" if j == 7 else None)
                mmv = P.val("mm")
                psf[gkey] = ("mm", mmv)
                psf[wkey] = ("mm", mmv)
            P.wait("vector", "mm")
            for j in range(8):
                oh = grp * 8 + j
                dst = (g["DSTG0"], g["DSTG1"])[j % 2]
                dkey = "dstg" + str(j % 2)
                if psf.get(dkey):
                    P.wait("vector", psf[dkey][0], psf[dkey][1])
                P.op("vector", lambda e, j_=j, oh_=oh, d_=dst, hb_=hb:
                     e.scalar_tensor_tensor(
                         out=d_[:, 0:512],
                         in0=slots[j_][0][:, slots[j_][1]:slots[j_][1] + 512],
                         scalar=dnb[:, oh_:oh_ + 1],
                         in1=_x1c(g, oh_)[:, hb_ * 512:(hb_ + 1) * 512],
                         op0=ALU.add, op1=ALU.add), inc="dv")
                vdv = P.val("dv")
                P.wait("scalar", "dv", vdv)
                dq = ("q6", "qo3")[j % 2]
                vq = dma(P, "scalar", dq, g["d_atn_park"][:, oh * TLOC + hb * 512:
                                                          oh * TLOC + hb * 512 + 512],
                         dst[:, 0:512])
                psf[dkey] = (dq, vq)
            psf["down"] = ("dv", P.val("dv"))
    g["_atn_q6"] = P.val("q6")
    g["_atn_qo3"] = P.val("qo3")


def _p6(nc, P, g, tap):
    psf = g["_psf"]
    P.wait("tensor", "dv")
    P.wait("tensor", "qc")
    _bcast_row(nc, P, g, g["f_w_router"], g["WRBC"], D)

    P.wait("sync", "q6", g["_atn_q6"])
    P.wait("sync", "qo3", g["_atn_qo3"])
    def atnt(tt):
        if tt < 4:
            return g["ATNTa"][:, tt * D:(tt + 1) * D]
        return g["ATNTb"][:, (tt - 4) * D:(tt - 3) * D]
    for oh in range(DCH):
        half = oh % 2
        qsem = ("q4", "q5")[half]
        key = "ath" + str(half)
        if psf.get(key):
            P.wait("sync", psf[key][0], psf[key][1])
        abuf = ("ATHF0", "ATHF1")[half]
        vld = dma(P, "sync", qsem, g[abuf][:],
                  g["d_atn_park"][:, oh * TLOC:(oh + 1) * TLOC])
        P.wait("tensor", qsem, vld)
        for tt in range(8):
            if psf.get("pxh"):
                P.wait("tensor", psf["pxh"][0], psf["pxh"][1])
            P.op("tensor", lambda e, tt_=tt, a_=abuf:
                 e.transpose(g["PX"][:], g[a_][:, tt_ * 128:(tt_ + 1) * 128],
                             g["cEYEf"][:]), inc="mm")
            P.wait("vector", "mm")
            vcp = P.op("vector", lambda e, oh_=oh, tt_=tt:
                       e.tensor_copy(out=atnt(tt_)[:, oh_ * 128:(oh_ + 1) * 128],
                                     in_=g["PX"][:]), inc="dv")
            psf["pxh"] = ("dv", vcp)
        psf[key] = ("dv", P.val("dv"))
    g["_atnt_dv"] = P.val("dv")

    x = g["x"]
    P.wait("scalar", "q6", g["_atn_q6"])
    P.wait("scalar", "qo3", g["_atn_qo3"])
    P.op("vector", lambda e: e.memset(g["EXT6"][:], 0.0), inc="dv")
    xa_q = ex_q = ob_q = 0
    for tt in range(8):
        b, q = tt // 4, tt % 4
        if tt:
            P.wait("sync", "dv")
            P.wait("scalar", "qo1", ob_q)
            P.wait("vector", "qo1", ob_q)
        vld = dma(P, "sync", "q8", g["XTOK6"][:],
                  x[b, HALO + q * 128: HALO + (q + 1) * 128, :])
        P.wait("vector", "q8", vld)
        P.wait("scalar", "q8", vld)
        if tt == 0:
            P.wait("vector", "qtap")
        P.op("vector", lambda e, tt_=tt: e.tensor_add(
            g["XTOK6"][:], g["XTOK6"][:], atnt(tt_)), inc="dv")
        if xa_q:
            P.wait("vector", "q6", xa_q)
        P.op("vector", lambda e: e.tensor_copy(out=g["TOKH"][:, 0:D],
                                               in_=g["XTOK6"][:]), inc="dv")
        vpk = P.val("dv")
        P.wait("scalar", "dv", vpk)
        dma(P, "scalar", "q6", g["d_xa_park"][tt * 128:(tt + 1) * 128, :],
            g["TOKH"][:, 0:D])
        xa_q = P.val("q6")
        _ln_tile(nc, P, g, g["XTOK6"][:], 0, 128, EPS_LN, None, sq=g["SQ6"][:],
                 out=g["HID6"][:])
        P.op("vector", lambda e: e.tensor_mul(g["SQ6"][:], g["HID6"][:], g["WRBC"][:]),
             inc="dv")
        P.op("vector", lambda e, tt_=tt: e.reduce_sum(
            g["SCO6"][:, tt_:tt_ + 1], g["SQ6"][:], axis=AXX), inc="dv")
        P.op("vector", lambda e: e.tensor_add(g["SQ6"][:], g["XTOK6"][:], g["HID6"][:]),
             inc="dv")
        vob = P.val("dv")
        P.wait("scalar", "dv", vob)
        dma(P, "scalar", "qo1", g["out_base"][tt * 128:(tt + 1) * 128, :], g["SQ6"][:])
        ob_q = P.val("qo1")
        if ex_q:
            P.wait("vector", "qa", ex_q)
        for col, srcap in [(0, g["ST1"][:, 0:1]), (1, g["ST2"][:, 0:1]),
                           (2, g["cTAU"]), (3, g["SCO6"]), (4, g["cROW"])]:
            if col in (2, 4):
                P.op("vector", lambda e, c_=col, s_=srcap, tt_=tt:
                     e.tensor_copy(out=g["EXT6"][:, c_:c_ + 1], in_=s_[:, tt_:tt_ + 1]),
                     inc="dv")
            elif col == 3:
                P.op("vector", lambda e, c_=col, tt_=tt:
                     e.tensor_copy(out=g["EXT6"][:, c_:c_ + 1],
                                   in_=g["SCO6"][:, tt_:tt_ + 1]), inc="dv")
            else:
                P.op("vector", lambda e, c_=col, s_=srcap:
                     e.tensor_copy(out=g["EXT6"][:, c_:c_ + 1], in_=s_), inc="dv")
        vex = P.val("dv")
        P.wait("sync", "dv", vex)
        dma(P, "sync", "qa", g["d_ex_park"][tt * 128:(tt + 1) * 128, :], g["EXT6"][:])
        ex_q = P.val("qa")
    g["_xa_q6v"] = xa_q
    psf2 = g["_psf"]
    if psf2.get("pxh"):
        P.wait("tensor", psf2["pxh"][0], psf2["pxh"][1])
    P.wait("tensor", "dv")
    P.op("tensor", lambda e: e.transpose(g["PX"][0:8, 0:128], g["SCO6"][:],
                                         g["cEYEf"][:]), inc="mm")
    P.wait("vector", "mm")
    P.op("vector", lambda e: e.tensor_copy(out=g["ST2"][0:8, 0:16] if False else
                                           g["SCO8R"][0:8, :], in_=g["PX"][0:8, 0:128]),
         inc="dv")
    P.wait("sync", "dv")
    vsc = dma(P, "sync", "qb", g["d_scin"].rearrange("(a b) o -> a (b o)", a=8),
              g["SCO8R"][0:8, :])
    P.wait("gpsimd", "qb", P.val("qb"))
    P.wait("gpsimd", "qa", P.val("qa"))
    P.wait("gpsimd", "qo1")
    P.op("gpsimd", lambda e: e.collective_compute(
        "AllGather", ALU.bypass, replica_groups=[list(range(NC))],
        ins=[g["d_scin"].tensor.ap().opt()], outs=[g["d_scag"].tensor.ap().opt()]),
        inc="cc")
    g["_p6_cc"] = P.val("cc")
    if tap("hidden", [128, D]):
        P.wait("gpsimd", "dv")
        dma(P, "gpsimd", "qtap", g["tap_hidden"], g["HID6"][:])
    if tap("xapark", [TLOC, D], BF16):
        P.wait("gpsimd", "q6", xa_q)
        dma(P, "gpsimd", "qtap", g["tap_xapark"], g["d_xa_park"])
    if tap("expark", [TLOC, 8]):
        P.wait("gpsimd", "qa", ex_q)
        dma(P, "gpsimd", "qtap", g["tap_expark"], g["d_ex_park"])
    if tap("scag", [NC * TLOC, 1]):
        P.wait("gpsimd", "cc", g["_p6_cc"])
        dma(P, "gpsimd", "qtap", g["tap_scag"], g["d_scag"])


def _p7(nc, P, g, tap):
    psf = g["_psf"]
    P.wait("sync", "cc", g["_p6_cc"])
    for b in range(BB):
        dma(P, "sync", "qd", g["SC7"][:, b * 32:(b + 1) * 32],
            g["d_scag"].rearrange("(po pi f o) -> (po pi) f o", po=8, pi=16, f=32)[
                :, :, 0] if False else
            _scag_ap(g, b))
    vsc7 = P.val("qd")
    P.op("vector", lambda e: e.memset(g["BIG7"][:], 4000.0), inc="dv")
    P.op("vector", lambda e: e.memset(g["ZERO7"][:], 0.0), inc="dv")
    lo, hi, mid = g["ST1"][0:1, 0:1], g["ST1"][0:1, 1:2], g["ST1"][0:1, 2:3]
    cnt, prd, tt7 = g["ST1"][0:1, 3:4], g["ST1"][0:1, 4:5], g["ST1"][0:1, 5:6]
    cgt, need = g["ST1"][0:1, 6:7], g["ST1"][0:1, 7:8]
    row128 = g["ROW1"][:]
    P.wait("vector", "qd", vsc7)

    def mmt_colrow(col_ap, out_row):
        P.wait("tensor", "dv")
        P.op("tensor", lambda e: e.matmul(g["PS1"][0:1, 0:128], col_ap,
                                          g["cEYEf"][:], start=True, stop=True),
             inc="mm")
        P.wait("vector", "mm")
        P.op("vector", lambda e: e.tensor_copy(out=out_row, in_=g["PS1"][0:1, 0:128]),
             inc="dv")

    def bcast_col(scalar_ap, out_col):
        P.wait("tensor", "dv")
        P.op("tensor", lambda e: e.matmul(g["PX"][:, 0:1], g["cONEr"][:], scalar_ap,
                                          start=True, stop=True), inc="mm")
        P.wait("vector", "mm")
        P.op("vector", lambda e: e.tensor_copy(out=out_col, in_=g["PX"][:, 0:1]),
             inc="dv")

    def reduce_total(col_ap, out_sc):
        P.wait("tensor", "dv")
        P.op("tensor", lambda e: e.matmul(g["PS1"][0:1, 128:129], col_ap,
                                          g["cONEc"][:], start=True, stop=True),
             inc="mm")
        P.wait("vector", "mm")
        P.op("vector", lambda e: e.tensor_copy(out=out_sc,
                                               in_=g["PS1"][0:1, 128:129]), inc="dv")

    for b in range(BB):
        scb = g["SC7"][:, b * 32:(b + 1) * 32]
        if b:
            P.wait("vector", "qd")
        P.op("vector", lambda e, s_=scb: e.reduce_max(g["CNTC7"][:], s_, axis=AXX), inc="dv")
        mmt_colrow(g["CNTC7"][:], row128)
        P.op("vector", lambda e: e.reduce_max(hi, row128, axis=AXX), inc="dv")
        P.op("vector", lambda e, s_=scb: e.tensor_scalar_mul(g["T7a"][:], s_, -1.0), inc="dv")
        P.op("vector", lambda e: e.reduce_max(g["CNTC7"][:], g["T7a"][:], axis=AXX),
             inc="dv")
        mmt_colrow(g["CNTC7"][:], row128)
        P.op("vector", lambda e: e.reduce_max(lo, row128, axis=AXX), inc="dv")
        P.op("vector", lambda e: e.tensor_scalar_mul(lo, lo, -1.0), inc="dv")
        for it in range(BISECT_ITERS):
            P.op("vector", lambda e: e.tensor_sub(tt7, hi, lo), inc="dv")
            P.op("vector", lambda e: e.scalar_tensor_tensor(
                out=mid, in0=tt7, scalar=0.5, in1=lo, op0=ALU.mult, op1=ALU.add),
                inc="dv")
            bcast_col(mid, g["MIDB7"][:])
            P.op("vector", lambda e, s_=scb: e.tensor_scalar(
                out=g["T7a"][:], in0=s_, scalar1=g["MIDB7"][:], scalar2=0.0,
                op0=ALU.is_gt, op1=ALU.add, accum_out=g["CNTC7"][:]), inc="dv")
            reduce_total(g["CNTC7"][:], cnt)
            P.op("vector", lambda e: e.tensor_scalar(
                out=prd, in0=cnt, scalar1=float(KSEL) - 0.5, scalar2=None,
                op0=ALU.is_gt), inc="dv")
            P.op("vector", lambda e: e.tensor_sub(tt7, mid, lo), inc="dv")
            P.op("vector", lambda e: e.scalar_tensor_tensor(
                out=lo, in0=tt7, scalar=prd, in1=lo, op0=ALU.mult, op1=ALU.add),
                inc="dv")
            P.op("vector", lambda e: e.tensor_sub(tt7, hi, mid), inc="dv")
            P.op("vector", lambda e: e.scalar_tensor_tensor(
                out=hi, in0=tt7, scalar=prd, in1=mid, op0=ALU.mult, op1=ALU.add),
                inc="dv")
        bcast_col(hi, g["HIB7"][:])
        P.op("vector", lambda e, s_=scb: e.tensor_scalar(
            out=g["MG7"][:], in0=s_, scalar1=g["HIB7"][:], scalar2=0.0,
            op0=ALU.is_gt, op1=ALU.add, accum_out=g["CNTC7"][:]), inc="dv")
        reduce_total(g["CNTC7"][:], cgt)
        P.op("vector", lambda e, s_=scb: e.tensor_scalar(
            out=g["MEQ7"][:], in0=s_, scalar1=g["HIB7"][:], scalar2=0.0,
            op0=ALU.is_equal, op1=ALU.add, accum_out=g["CNTC7"][:]), inc="dv")
        P.wait("tensor", "dv")
        P.op("tensor", lambda e: e.matmul(g["PX"][:, 0:1], g["cLTRI"][:],
                                          g["CNTC7"][:], start=True, stop=True),
             inc="mm")
        P.wait("vector", "mm")
        P.op("vector", lambda e: e.tensor_copy(out=g["EOFF7"][:], in_=g["PX"][:, 0:1]),
             inc="dv")
        P.op("vector", lambda e: e.tensor_tensor_scan(
            out=g["INCL7"][:], data0=g["MEQ7"][:], data1=g["ZERO7"][:], initial=0.0,
            op0=ALU.add, op1=ALU.add), inc="dv")
        P.op("vector", lambda e: e.tensor_scalar_add(g["INCL7"][:], g["INCL7"][:],
                                                     g["EOFF7"][:]), inc="dv")
        P.op("vector", lambda e: e.tensor_scalar(
            out=need, in0=cgt, scalar1=-1.0, scalar2=float(KSEL), op0=ALU.mult,
            op1=ALU.add), inc="dv")
        bcast_col(need, g["NEEDB7"][:])
        P.op("vector", lambda e: e.tensor_scalar(
            out=g["T7b"][:], in0=g["INCL7"][:], scalar1=g["NEEDB7"][:], scalar2=None,
            op0=ALU.is_le), inc="dv")
        P.op("vector", lambda e: e.tensor_mul(g["T7b"][:], g["T7b"][:], g["MEQ7"][:]),
             inc="dv")
        P.op("vector", lambda e: e.tensor_add(g["MASK7"][:], g["MG7"][:], g["T7b"][:]),
             inc="dv")
        P.op("vector", lambda e: e.reduce_sum(g["CNTC7"][:], g["MASK7"][:], axis=AXX),
             inc="dv")
        P.wait("tensor", "dv")
        P.op("tensor", lambda e: e.matmul(g["PX"][:, 0:1], g["cLTRI"][:],
                                          g["CNTC7"][:], start=True, stop=True),
             inc="mm")
        P.wait("vector", "mm")
        P.op("vector", lambda e: e.tensor_copy(out=g["EOFF7"][:], in_=g["PX"][:, 0:1]),
             inc="dv")
        P.op("vector", lambda e: e.tensor_tensor_scan(
            out=g["INCL7"][:], data0=g["MASK7"][:], data1=g["ZERO7"][:], initial=0.0,
            op0=ALU.add, op1=ALU.add), inc="dv")
        P.op("vector", lambda e: e.tensor_scalar_add(g["INCL7"][:], g["INCL7"][:],
                                                     g["EOFF7"][:]), inc="dv")
        P.op("vector", lambda e: e.tensor_sub(g["CEX7"][:], g["INCL7"][:],
                                              g["MASK7"][:]), inc="dv")
        if b == 1:
            P.op("vector", lambda e: e.tensor_scalar_add(g["CEX7"][:], g["CEX7"][:],
                                                         float(KSEL)), inc="dv")
        P.op("vector", lambda e: e.tensor_mul(g["CEX7"][:], g["CEX7"][:],
                                              g["MASK7"][:]), inc="dv")
        P.op("vector", lambda e: e.tensor_scalar(
            out=g["T7b"][:], in0=g["MASK7"][:], scalar1=-4000.0, scalar2=4000.0,
            op0=ALU.mult, op1=ALU.add), inc="dv")
        P.op("vector", lambda e: e.tensor_add(g["CEX7"][:], g["CEX7"][:],
                                              g["T7b"][:]), inc="dv")
        P.op("vector", lambda e: e.tensor_copy(out=g["IDXI7"][:], in_=g["CEX7"][:]),
             inc="dv")
        P.op("vector", lambda e: e.tensor_copy(
            out=g["STAG7"].rearrange("p (f j) -> p f j", j=4),
            in_=g["IDXI7"][:].unsqueeze(2).to_broadcast([128, 32, 4])), inc="dv")
        vst = P.val("dv")
        P.wait("sync", "dv", vst)
        dma(P, "sync", "qd", _slotrec_ap(g, b), g["STAG7"][:])
        if b == 0 and tap("seldbg", [128, 176]):
            P.wait("gpsimd", "dv", vst)
            dma(P, "gpsimd", "qtap", g["tap_seldbg"][:, 0:64], g["SC7"][:])
            dma(P, "gpsimd", "qtap", g["tap_seldbg"][:, 64:80], g["ST1"][:])
            dma(P, "gpsimd", "qtap", g["tap_seldbg"][:, 80:112], g["MASK7"][:])
            dma(P, "gpsimd", "qtap", g["tap_seldbg"][:, 112:144], g["CEX7"][:])
            dma(P, "gpsimd", "qtap", g["tap_seldbg"][:, 144:176], g["MG7"][:])
            P.wait("vector", "qtap")
    g["_slotrec_qd"] = P.val("qd")

    P.op("vector", lambda e: e.memset(g["P7A"][:], 0.0), inc="dv")
    P.op("vector", lambda e: e.memset(g["P7E"][:], 0.0), inc="dv")
    vz = P.val("dv")
    P.wait("gpsimd", "dv", vz)
    for r in range(NC):
        dma(P, "gpsimd", "qe", g["d_send"][r * 128:(r + 1) * 128, :], g["P7A"][:])
        dma(P, "gpsimd", "qe", g["d_send_ex"][r * 128:(r + 1) * 128, :], g["P7E"][:])
    vzs = P.val("qe")
    P.wait("gpsimd", "qd", g["_slotrec_qd"])
    P.wait("gpsimd", "qe", vzs)
    P.wait("gpsimd", "q6", g["_xa_q6v"])
    P.wait("gpsimd", "qa")
    scat_done = 0
    for t8 in range(8):
        if scat_done:
            P.wait("gpsimd", "qg", scat_done)
        dma(P, "gpsimd", "qe", g["P7A"][:],
            g["d_xa_park"][t8 * 128:(t8 + 1) * 128, :])
        vpe = dma(P, "gpsimd", "qe", g["P7E"][:],
                  g["d_ex_park"][t8 * 128:(t8 + 1) * 128, :])
        vig = P.op("gpsimd", lambda e, t_=t8: e.indirect_dma_start(
            out=g["IDG7"][:], out_offset=None, in_=g["d_slotrec"],
            in_offset=IndirectOffsetOnAxis(ap=g["cGRW"][:, t_:t_ + 1], axis=0)),
            inc="qf", inc_n=16)
        P.wait("gpsimd", "qe", vpe)
        P.wait("gpsimd", "qf", vig)
        P.op("gpsimd", lambda e: e.indirect_dma_start(
            out=g["d_send"], out_offset=IndirectOffsetOnAxis(ap=g["IDG7"][:, 0:1],
                                                             axis=0),
            in_=g["P7A"][:], in_offset=None,
            bounds_check=NSLOT - 1, oob_is_err=False), inc="qg", inc_n=16)
        scat_done = P.op("gpsimd", lambda e: e.indirect_dma_start(
            out=g["d_send_ex"], out_offset=IndirectOffsetOnAxis(ap=g["IDG7"][:, 0:1],
                                                                axis=0),
            in_=g["P7E"][:], in_offset=None,
            bounds_check=NSLOT - 1, oob_is_err=False), inc="qg", inc_n=16)
    P.wait("gpsimd", "qg", scat_done)
    from concourse.aot_env import allow_ici_all_to_all

    def a2a(e):
        with allow_ici_all_to_all():
            return e.collective_compute(
                "AllToAll", ALU.bypass, replica_groups=[list(range(NC))],
                ins=[g["d_send"].tensor.ap().opt()],
                outs=[g["d_recv"].tensor.ap().opt()])
    P.op("gpsimd", a2a, inc="cc")

    def a2a_ex(e):
        with allow_ici_all_to_all():
            return e.collective_compute(
                "AllToAll", ALU.bypass, replica_groups=[list(range(NC))],
                ins=[g["d_send_ex"].tensor.ap().opt()],
                outs=[g["d_recv_ex"].tensor.ap().opt()])
    P.op("gpsimd", a2a_ex, inc="cc")
    g["_p7_cc"] = P.val("cc")
    if tap("slotrec", [BB * SEQ, 4], I32):
        P.wait("gpsimd", "qd", g["_slotrec_qd"])
        dma(P, "gpsimd", "qtap", g["tap_slotrec"], g["d_slotrec"])
    if tap("recv", [NSLOT, PAYW], BF16):
        P.wait("gpsimd", "cc", g["_p7_cc"])
        dma(P, "gpsimd", "qtap", g["tap_recv"], g["d_recv"])


def _mk_ap(base, offset, dims):
    return type(base)(base.tensor, offset, dims)


def _scag_ap(g, b):
    return _mk_ap(g["d_scag"], b * 512, [[1024, 8], [32, 16], [1, 32]])


def _slotrec_ap(g, b):
    return _mk_ap(g["d_slotrec"], b * 16384, [[128, 128], [4, 32], [1, 4]])


def _p8(nc, P, g, tap):
    psf = g["_psf"]
    pool, pool2 = g["_wpool"], g["_wpool2"]
    pxh = g["PMD"][:, 0:64].bitcast(BF16)

    P.wait("gpsimd", "cc", g["_p7_cc"])
    for r in range(NC):
        v = dma(P, "gpsimd", "qe", g["P7A"][:], g["d_recv"][r * 128:(r + 1) * 128, :])
        v2 = dma(P, "gpsimd", "qe", g["P7E"][:],
                 g["d_recv_ex"][r * 128:(r + 1) * 128, :])
        P.wait("vector", "qe", v2)
        if r == 0:
            P.op("vector", lambda e: e.tensor_copy(out=g["XSEL"][:], in_=g["P7A"][:]),
                 inc="dv")
            P.op("vector", lambda e: e.tensor_copy(out=g["EXF"][:], in_=g["P7E"][:]),
                 inc="dv")
        else:
            P.op("vector", lambda e: e.tensor_add(g["XSEL"][:], g["XSEL"][:],
                                                  g["P7A"][:]), inc="dv")
            P.op("vector", lambda e: e.tensor_add(g["EXF"][:], g["EXF"][:],
                                                  g["P7E"][:]), inc="dv")
        P.wait("gpsimd", "dv")
    P.op("vector", lambda e: e.tensor_scalar(
        out=g["HID"][:], in0=g["XSEL"][:, 0:D], scalar1=g["EXF"][:, 0:1],
        scalar2=g["EXF"][:, 1:2], op0=ALU.subtract, op1=ALU.mult), inc="dv")
    P.wait("scalar", "dv")
    P.op("scalar", lambda e: e.activation(out=g["ATTV"][:], in_=g["HID"][:],
                                          func=AF.Square, accum_out=g["ST1"][:, 0:1]),
         inc="ac")
    P.wait("vector", "ac")
    P.op("vector", lambda e: e.tensor_scalar_mul(g["ST1"][:, 0:1], g["ST1"][:, 0:1],
                                                 1.0 / D), inc="dv")
    P.wait("scalar", "dv")
    P.op("scalar", lambda e: e.activation(out=g["ST1"][:, 0:1], in_=g["ST1"][:, 0:1],
                                          func=AF.Ln, bias=_cst(g, EPS_RMS_B)),
         inc="ac")
    P.op("scalar", lambda e: e.activation(out=g["ST1"][:, 0:1], in_=g["ST1"][:, 0:1],
                                          func=AF.Exp, scale=-0.5), inc="ac")
    P.wait("vector", "ac")
    P.op("vector", lambda e: e.tensor_scalar_mul(g["P7A"][:, 0:D], g["HID"][:],
                                                 g["ST1"][:, 0:1]), inc="dv")
    if _P8STOP < 1:
        return
    l1w = _bias_ap(g, "f_ln1_w")
    for ch in range(DCH):
        if psf.get("pxh"):
            P.wait("tensor", psf["pxh"][0], psf["pxh"][1])
        P.wait("tensor", "dv")
        P.op("tensor", lambda e, ch_=ch:
             e.transpose(pxh, g["P7A"][:, ch_ * 128:(ch_ + 1) * 128], g["cEYEh"][:]),
             inc="mm")
        P.wait("vector", "mm")
        vcp = P.op("vector", lambda e, ch_=ch:
                   e.tensor_scalar_mul(g["QKVf"][:, ch_ * 128:(ch_ + 1) * 128],
                                       pxh, l1w[:, ch_:ch_ + 1]), inc="dv")
        psf["pxh"] = ("dv", vcp)

    if _P8STOP < 2:
        return
    bq = _bias_ap(g, "f_bq")
    bk = _bias_ap(g, "f_bk")
    bv = _bias_ap(g, "f_bv")
    P.wait("tensor", "dv")
    qkv_seq = [("f_wq", oh, "QR", bq) for oh in range(DCH)] + \
              [("f_wk", oh, "KR", bk) for oh in range(KVCH)] + \
              [("f_wv", oh, "VTt", bv) for oh in range(KVCH)]
    pool.prefetch(g["f_wq"][0, :, :], DCH * 128)
    for qi, (wname, oh, dst, bias) in enumerate(qkv_seq):
        if True:
            if qi + 1 < len(qkv_seq):
                nw, noh = qkv_seq[qi + 1][0], qkv_seq[qi + 1][1]
                pool.prefetch(g[nw][noh, :, :], DCH * 128)
            bn, lv = pool.take()
            pm = (g["PMA"][:, 0:128], g["PMA"][:, 512:640])[oh % 2]
            key = "p8q" + str(oh % 2)
            if psf.get(key):
                P.wait("tensor", psf[key][0], psf[key][1])
            P.wait("tensor", pool.sems[bn], lv)
            for k in range(DCH):
                P.op("tensor", lambda e, k_=k, pm_=pm, b_=bn:
                     e.matmul(pm_, g[b_][:, k_ * 128:(k_ + 1) * 128],
                              g["QKVf"][:, k_ * 128:(k_ + 1) * 128],
                              start=(k_ == 0), stop=(k_ == DCH - 1)),
                     inc="mm" if k == DCH - 1 else None)
            pool.consumed(bn, "mm", P.val("mm"))
            P.wait("vector", "mm")
            vev = P.op("vector", lambda e, oh_=oh, pm_=pm, d_=dst, bi_=bias:
                       e.tensor_scalar_add(g[d_][:, oh_ * 128:(oh_ + 1) * 128], pm_,
                                           bi_[:, oh_:oh_ + 1]), inc="dv")
            psf[key] = ("dv", vev)

    if _P8STOP < 3:
        return
    P.wait("tensor", "dv")
    P.op("tensor", lambda e: e.matmul(g["PS1"][0:1, 0:128], g["EXF"][:, 2:3],
                                      g["cEYEf"][:], start=True, stop=True), inc="mm")
    P.wait("vector", "mm")
    P.op("vector", lambda e: e.tensor_copy(out=g["ROW1"][:],
                                           in_=g["PS1"][0:1, 0:128]), inc="dv")
    P.wait("tensor", "dv")
    P.op("tensor", lambda e: e.matmul(g["PX"][:], g["cINVF"][:], g["ROW1"][:],
                                      start=True, stop=True), inc="mm")
    P.wait("vector", "mm")
    P.op("vector", lambda e: e.tensor_copy(out=g["ANG"][:], in_=g["PX"][:]), inc="dv")
    ti = g["T8"].bitcast(I32)
    TWO_PI = float(2 * np.pi)
    for (dst, shift) in [("SINt", 0.0), ("COSt", float(np.pi / 2))]:
        P.op("vector", lambda e, s_=shift: e.tensor_scalar(
            out=g["T8b"][:], in0=g["ANG"][:], scalar1=s_,
            scalar2=float(1.0 / TWO_PI), op0=ALU.add, op1=ALU.mult), inc="dv")
        P.op("vector", lambda e: e.tensor_copy(out=ti[:, 0:128], in_=g["T8b"][:]),
             inc="dv")
        P.op("vector", lambda e: e.tensor_copy(out=g["T8b"][:], in_=ti[:, 0:128]),
             inc="dv")
        P.op("vector", lambda e, d_=dst, s_=shift: e.tensor_scalar_add(
            g[d_][:], g["ANG"][:], s_), inc="dv")
        P.op("vector", lambda e, d_=dst: e.scalar_tensor_tensor(
            out=g[d_][:], in0=g["T8b"][:], scalar=-TWO_PI, in1=g[d_][:],
            op0=ALU.mult, op1=ALU.add), inc="dv")
        P.op("vector", lambda e, d_=dst: e.tensor_scalar(
            out=g["T8b"][:], in0=g[d_][:], scalar1=float(np.pi), scalar2=None,
            op0=ALU.is_gt), inc="dv")
        P.op("vector", lambda e, d_=dst: e.tensor_scalar(
            out=ti[:, 0:128].bitcast(F32), in0=g[d_][:], scalar1=float(-np.pi),
            scalar2=None, op0=ALU.is_lt), inc="dv")
        P.op("vector", lambda e: e.tensor_sub(g["T8b"][:], ti[:, 0:128].bitcast(F32),
                                              g["T8b"][:]), inc="dv")
        P.op("vector", lambda e, d_=dst: e.scalar_tensor_tensor(
            out=g[d_][:], in0=g["T8b"][:], scalar=TWO_PI, in1=g[d_][:],
            op0=ALU.mult, op1=ALU.add), inc="dv")
        P.wait("scalar", "dv")
        P.op("scalar", lambda e, d_=dst: e.activation(out=g[d_][:], in_=g[d_][:],
                                                      func=AF.Sin), inc="ac")
    P.wait("vector", "ac")
    for (buf, nh, scale) in [("QR", NH, INV_SQRT_HD), ("KR", NKV, 1.0)]:
        for h in range(nh):
            hap = g[buf][:, h * 128:(h + 1) * 128]
            P.wait("tensor", "dv")
            P.op("tensor", lambda e, a_=hap: e.matmul(
                g["PX"][:], g["cROT"][:], a_, start=True, stop=True), inc="mm")
            P.wait("vector", "mm")
            P.op("vector", lambda e: e.tensor_mul(g["T8"][:], g["PX"][:], g["SINt"][:]),
                 inc="dv")
            P.op("vector", lambda e, a_=hap: e.tensor_mul(g["T8b"][:], a_,
                                                          g["COSt"][:]), inc="dv")
            P.op("vector", lambda e: e.tensor_add(g["T8"][:], g["T8"][:], g["T8b"][:]),
                 inc="dv")
            P.op("vector", lambda e, a_=hap, s_=scale: e.tensor_scalar_mul(
                a_, g["T8"][:], s_), inc="dv")

    if _P8STOP < 4:
        return
    vkw = P.val("dv")
    P.wait("scalar", "dv", vkw)
    dma(P, "scalar", "q7", _kvin_k_ap(g), g["KR"][:])
    P.wait("tensor", "dv")
    for kv in range(NKV):
        if psf.get("pxh"):
            P.wait("tensor", psf["pxh"][0], psf["pxh"][1])
        P.op("tensor", lambda e, k_=kv: e.transpose(
            pxh, g["VTt"][:, k_ * 128:(k_ + 1) * 128], g["cEYEh"][:]), inc="mm")
        P.wait("vector", "mm")
        vcp = P.op("vector", lambda e, k_=kv: e.tensor_copy(
            out=g["P7A"][:, k_ * 128:(k_ + 1) * 128], in_=pxh), inc="dv")
        psf["pxh"] = ("dv", vcp)
    P.wait("scalar", "dv")
    dma(P, "scalar", "q7", _kvin_v_ap(g), g["P7A"][:, 0:NKV * 128])
    P.wait("gpsimd", "q7")
    P.op("gpsimd", lambda e: e.collective_compute(
        "AllGather", ALU.bypass, replica_groups=[list(range(NC))],
        ins=[g["d_kvin"].tensor.ap().opt()], outs=[g["d_kvout"].tensor.ap().opt()]),
        inc="cc")
    vkv_cc = P.val("cc")

    P.op("gpsimd", lambda e: e.iota(g["ATTS"][:], pattern=[[1, 1024]], base=0,
                                    channel_multiplier=0,
                                    allow_small_or_imprecise_dtypes=True), inc="gp")
    P.op("gpsimd", lambda e: e.iota(g["KS8"][:], pattern=[[128, 8]], base=0,
                                    channel_multiplier=1,
                                    allow_small_or_imprecise_dtypes=True), inc="gp")
    P.wait("vector", "gp")
    P.op("vector", lambda e: e.tensor_scalar(
        out=g["MASKM"][:], in0=g["ATTS"][:], scalar1=g["cSLOT"][:], scalar2=None,
        op0=ALU.is_le), inc="dv")
    P.op("vector", lambda e: e.tensor_scalar(
        out=g["NEGT"][:], in0=g["MASKM"][:], scalar1=1.0e30, scalar2=-1.0e30,
        op0=ALU.mult, op1=ALU.add), inc="dv")
    P.wait("tensor", "dv")
    P.op("tensor", lambda e: e.matmul(g["PS1"][0:1, 0:128], g["cSLOT"][:],
                                      g["cEYEf"][:], start=True, stop=True), inc="mm")
    P.wait("vector", "mm")
    P.op("vector", lambda e: e.tensor_copy(out=g["ROW1"][:],
                                           in_=g["PS1"][0:1, 0:128]), inc="dv")
    P.wait("tensor", "dv")
    P.op("tensor", lambda e: e.matmul(g["PX"][:], g["cONEr"][:], g["ROW1"][:],
                                      start=True, stop=True), inc="mm")
    P.wait("vector", "mm")
    P.op("vector", lambda e: e.tensor_copy(out=g["QROW8"][:], in_=g["PX"][:]),
         inc="dv")
    for ck in range(NC):
        P.op("vector", lambda e, c_=ck: e.tensor_scalar(
            out=g["MTCH"][:, c_ * 128:(c_ + 1) * 128], in0=g["QROW8"][:],
            scalar1=g["KS8"][:, c_:c_ + 1], scalar2=None, op0=ALU.is_ge), inc="dv")

    if _P8STOP < 5:
        return
    P.wait("sync", "cc", vkv_cc)
    wo_pool = pool
    for h in range(NH):
        kvh = h // 2
        if psf.get("att"):
            P.wait("sync", psf["att"][0], psf["att"][1])
        vk = dma(P, "sync", ("q4", "q5")[h % 2], g["KBH"][:], _kvout_k_ap(g, kvh))
        vv = dma(P, "sync", ("q4", "q5")[h % 2], g["VBH"][:], _kvout_v_ap(g, kvh))
        qh = g["QR"][:, h * 128:(h + 1) * 128]
        if psf.get("att"):
            P.wait("tensor", psf["att"][0], psf["att"][1])
        P.wait("tensor", ("q4", "q5")[h % 2], vv)
        for hf in range(2):
            P.op("tensor", lambda e, q_=qh, f_=hf: e.matmul(
                g["PMA"][:, f_ * 512:(f_ + 1) * 512], q_,
                g["KBH"][:, f_ * 512:(f_ + 1) * 512], start=True, stop=True),
                inc="mm" if hf == 1 else None)
        P.wait("vector", "mm")
        P.op("vector", lambda e: e.tensor_mul(g["ATTS"][:], g["PMA"][:],
                                              g["MASKM"][:]), inc="dv")
        P.op("vector", lambda e: e.tensor_add(g["ATTS"][:], g["ATTS"][:],
                                              g["NEGT"][:]), inc="dv")
        vats = P.val("dv")
        psf["att"] = ("dv", vats)
        P.op("vector", lambda e: e.reduce_max(g["ST1"][:, 0:1], g["ATTS"][:],
                                              axis=AXX), inc="dv")
        P.wait("tensor", "dv")
        P.op("tensor", lambda e: e.matmul(g["PS1"][0:1, 0:128], g["ST1"][:, 0:1],
                                          g["cEYEf"][:], start=True, stop=True),
             inc="mm")
        P.wait("vector", "mm")
        P.op("vector", lambda e: e.tensor_copy(out=g["ROW1"][:],
                                               in_=g["PS1"][0:1, 0:128]), inc="dv")
        P.op("vector", lambda e: e.reduce_max(g["ST1"][0:1, 1:2], g["ROW1"][:],
                                              axis=AXX), inc="dv")
        P.wait("tensor", "dv")
        P.op("tensor", lambda e: e.matmul(g["PX"][:, 0:1], g["cONEr"][:],
                                          g["ST1"][0:1, 1:2], start=True, stop=True),
             inc="mm")
        P.wait("vector", "mm")
        P.op("vector", lambda e: e.tensor_scalar_mul(g["MHB8"][:], g["PX"][:, 0:1],
                                                     -1.0), inc="dv")
        vmh = P.val("dv")
        P.wait("scalar", "dv", vmh)
        P.op("scalar", lambda e, h_=h: e.activation(
            out=g["ATTS"][:], in_=g["ATTS"][:], func=AF.Exp, bias=g["MHB8"][:],
            accum_out=g["DEN8"][:, h_:h_ + 1]), inc="ac")
        P.wait("vector", "ac")
        P.op("vector", lambda e, h_=h: e.reciprocal(g["REC8"][:, h_:h_ + 1],
                                                    g["DEN8"][:, h_:h_ + 1]), inc="dv")
        for ck in range(NC):
            P.op("tensor", lambda e, c_=ck, q_=qh: e.matmul(
                g["PMB"][:, c_ * 128:(c_ + 1) * 128],
                g["KBH"][:, c_ * 128:(c_ + 1) * 128], q_, start=True, stop=True),
                inc="mm" if ck == NC - 1 else None)
        P.wait("scalar", "mm")
        for ck in range(NC):
            P.op("scalar", lambda e, c_=ck: e.activation(
                out=g["EHS"][:, c_ * 128:(c_ + 1) * 128],
                in_=g["PMB"][:, c_ * 128:(c_ + 1) * 128], func=AF.Exp,
                bias=g["MHB8"][:]), inc="ac")
        P.wait("vector", "ac")
        P.op("vector", lambda e: e.tensor_mul(g["ETS"][:], g["EHS"][:], g["MTCH"][:]),
             inc="dv")
        vets = P.val("dv")
        P.wait("tensor", "dv")
        P.op("tensor", lambda e, h_=h: e.matmul(g["PS1"][0:1, 0:128],
                                                g["REC8"][:, h_:h_ + 1], g["cEYEf"][:],
                                                start=True, stop=True), inc="mm")
        P.wait("vector", "mm")
        P.op("vector", lambda e: e.tensor_copy(out=g["ROW1"][:],
                                               in_=g["PS1"][0:1, 0:128]), inc="dv")
        P.wait("tensor", "dv")
        P.op("tensor", lambda e: e.matmul(g["PX"][:], g["cONEr"][:],
                                          g["ROW1"][:], start=True, stop=True),
             inc="mm")
        P.wait("vector", "mm")
        P.op("vector", lambda e: e.tensor_copy(out=g["RBC8"][:], in_=g["PX"][:]),
             inc="dv")
        P.wait("tensor", "dv", vets)
        for ck in range(NC):
            P.op("tensor", lambda e, c_=ck: e.matmul(
                g["PMC"][:, 0:128], g["VBH"][:, c_ * 128:(c_ + 1) * 128],
                g["ETS"][:, c_ * 128:(c_ + 1) * 128],
                start=(c_ == 0), stop=(c_ == NC - 1)),
                inc="mm" if ck == NC - 1 else None)
        P.wait("vector", "mm")
        vav = P.op("vector", lambda e, h_=h: e.tensor_mul(
            g["AVO"][:, h_ * 128:(h_ + 1) * 128], g["PMC"][:, 0:128], g["RBC8"][:]),
            inc="dv")
        psf["att"] = ("dv", vav)

    if _P8STOP < 6:
        return
    P.wait("tensor", "dv")
    wo_pool.prefetch(g["f_wo"][0, :, :], DCH * 128)
    for oh in range(DCH):
        if oh + 1 < DCH:
            wo_pool.prefetch(g["f_wo"][oh + 1, :, :], DCH * 128)
        bn, lv = wo_pool.take()
        pm = (g["PMA"][:, 0:128], g["PMA"][:, 512:640])[oh % 2]
        key = "p8o" + str(oh % 2)
        if psf.get(key):
            P.wait("tensor", psf[key][0], psf[key][1])
        P.wait("tensor", wo_pool.sems[bn], lv)
        for k in range(DCH):
            P.op("tensor", lambda e, k_=k, pm_=pm, b_=bn:
                 e.matmul(pm_, g[b_][:, k_ * 128:(k_ + 1) * 128],
                          g["AVO"][:, k_ * 128:(k_ + 1) * 128],
                          start=(k_ == 0), stop=(k_ == DCH - 1)),
                 inc="mm" if k == DCH - 1 else None)
        wo_pool.consumed(bn, "mm", P.val("mm"))
        P.wait("vector", "mm")
        vev = P.op("vector", lambda e, oh_=oh, pm_=pm: e.tensor_copy(
            out=g["AOF"][:, oh_ * 128:(oh_ + 1) * 128], in_=pm_), inc="dv")
        psf[key] = ("dv", vev)
    for ch in range(DCH):
        if psf.get("pxh"):
            P.wait("tensor", psf["pxh"][0], psf["pxh"][1])
        P.wait("tensor", "dv")
        P.op("tensor", lambda e, c_=ch: e.transpose(
            pxh, g["AOF"][:, c_ * 128:(c_ + 1) * 128], g["cEYEh"][:]), inc="mm")
        P.wait("vector", "mm")
        vcp = P.op("vector", lambda e, c_=ch: e.tensor_add(
            g["X1B"][:, c_ * 128:(c_ + 1) * 128],
            g["HID"][:, c_ * 128:(c_ + 1) * 128], pxh), inc="dv")
        psf["pxh"] = ("dv", vcp)
    l2w = _bias_ap(g, "f_ln2_w")
    P.wait("scalar", "dv")
    P.op("scalar", lambda e: e.activation(out=g["ATTV"][:], in_=g["X1B"][:],
                                          func=AF.Square, accum_out=g["ST1"][:, 4:5]),
         inc="ac")
    P.wait("vector", "ac")
    P.op("vector", lambda e: e.tensor_scalar_mul(g["ST1"][:, 4:5], g["ST1"][:, 4:5],
                                                 1.0 / D), inc="dv")
    P.wait("scalar", "dv")
    P.op("scalar", lambda e: e.activation(out=g["ST1"][:, 4:5], in_=g["ST1"][:, 4:5],
                                          func=AF.Ln, bias=_cst(g, EPS_RMS_B)),
         inc="ac")
    P.op("scalar", lambda e: e.activation(out=g["ST1"][:, 4:5], in_=g["ST1"][:, 4:5],
                                          func=AF.Exp, scale=-0.5), inc="ac")
    P.wait("vector", "ac")
    P.op("vector", lambda e: e.tensor_scalar_mul(g["P7A"][:, 0:D], g["X1B"][:],
                                                 g["ST1"][:, 4:5]), inc="dv")
    for ch in range(DCH):
        if psf.get("pxh"):
            P.wait("tensor", psf["pxh"][0], psf["pxh"][1])
        P.wait("tensor", "dv")
        P.op("tensor", lambda e, c_=ch: e.transpose(
            pxh, g["P7A"][:, c_ * 128:(c_ + 1) * 128], g["cEYEh"][:]), inc="mm")
        P.wait("vector", "mm")
        vcp = P.op("vector", lambda e, c_=ch: e.tensor_scalar_mul(
            g["H2"][:, c_ * 128:(c_ + 1) * 128], pxh, l2w[:, c_:c_ + 1]), inc="dv")
        psf["pxh"] = ("dv", vcp)

    if _P8STOP < 7:
        return
    P.wait("tensor", "dv")
    pool.prefetch(g["f_w_gate"][0, :, :], DCH * 128)
    pool2.prefetch(g["f_w_up"][0, :, :], DCH * 128)
    for of in range(ICH):
        if of + 1 < ICH:
            pool.prefetch(g["f_w_gate"][of + 1, :, :], DCH * 128)
            pool2.prefetch(g["f_w_up"][of + 1, :, :], DCH * 128)
        b1, v1 = pool.take()
        b2, v2 = pool2.take()
        pmg = (g["PMA"][:, 0:128], g["PMA"][:, 512:640])[of % 2]
        pmu = (g["PMB"][:, 0:128], g["PMB"][:, 512:640])[of % 2]
        keyg, keyu = "p8g" + str(of % 2), "p8u" + str(of % 2)
        for (bn, vv, pm_, key, po) in [(b1, v1, pmg, keyg, pool),
                                       (b2, v2, pmu, keyu, pool2)]:
            if psf.get(key):
                P.wait("tensor", psf[key][0], psf[key][1])
            P.wait("tensor", po.sems[bn], vv)
            for k in range(DCH):
                P.op("tensor", lambda e, k_=k, pm2=pm_, b_=bn:
                     e.matmul(pm2, g[b_][:, k_ * 128:(k_ + 1) * 128],
                              g["H2"][:, k_ * 128:(k_ + 1) * 128],
                              start=(k_ == 0), stop=(k_ == DCH - 1)),
                     inc="mm" if k == DCH - 1 else None)
            po.consumed(bn, "mm", P.val("mm"))
        P.wait("scalar", "mm")
        scr8 = (g["T8"], g["T8b"])[of % 2]
        if psf.get("scr8" + str(of % 2)):
            P.wait("scalar", psf["scr8" + str(of % 2)][0],
                   psf["scr8" + str(of % 2)][1])
        P.op("scalar", lambda e, pm_=pmg, s_=scr8:
             e.activation(out=s_[:], in_=pm_, func=AF.Silu), inc="ac")
        vsg = P.val("ac")
        psf[keyg] = ("ac", vsg)
        P.wait("vector", "ac", vsg)
        vgl = P.op("vector", lambda e, of_=of, pm_=pmu, s_=scr8: e.tensor_mul(
            g["GLB"][:, of_ * 128:(of_ + 1) * 128], s_[:], pm_), inc="dv")
        psf[keyu] = ("dv", vgl)
        psf["scr8" + str(of % 2)] = ("dv", vgl)

    if _P8STOP < 8:
        return
    dpool = g.get("_wpoold") or WPool(P, g, ["WD0", "WD1"], ["d0", "d1"])
    g["_wpoold"] = dpool
    dpool.prefetch(g["f_w_down"][0, :, :], ICH * 128)
    for oh in range(DCH):
        if oh + 1 < DCH:
            dpool.prefetch(g["f_w_down"][oh + 1, :, :], ICH * 128)
        bn, lv = dpool.take()
        pm = (g["PMC"][:, 0:128], g["PMC"][:, 512:640])[oh % 2]
        key = "p8d" + str(oh % 2)
        if psf.get(key):
            P.wait("tensor", psf[key][0], psf[key][1])
        P.wait("tensor", dpool.sems[bn], lv)
        if oh == 0:
            P.wait("tensor", "dv")
        for k in range(ICH):
            P.op("tensor", lambda e, k_=k, pm_=pm, b_=bn:
                 e.matmul(pm_, g[b_][:, k_ * 128:(k_ + 1) * 128],
                          g["GLB"][:, k_ * 128:(k_ + 1) * 128],
                          start=(k_ == 0), stop=(k_ == ICH - 1)),
                 inc="mm" if k == ICH - 1 else None)
        dpool.consumed(bn, "mm", P.val("mm"))
        P.wait("vector", "mm")
        vev = P.op("vector", lambda e, oh_=oh, pm_=pm: e.tensor_copy(
            out=g["MLPf"][:, oh_ * 128:(oh_ + 1) * 128], in_=pm_), inc="dv")
        psf[key] = ("dv", vev)
    for ch in range(DCH):
        if psf.get("pxh"):
            P.wait("tensor", psf["pxh"][0], psf["pxh"][1])
        P.wait("tensor", "dv")
        P.op("tensor", lambda e, c_=ch: e.transpose(
            pxh, g["MLPf"][:, c_ * 128:(c_ + 1) * 128], g["cEYEh"][:]), inc="mm")
        P.wait("vector", "mm")
        vcp = P.op("vector", lambda e, c_=ch: e.tensor_add(
            g["PROC"][:, c_ * 128:(c_ + 1) * 128],
            g["X1B"][:, c_ * 128:(c_ + 1) * 128], pxh), inc="dv")
        psf["pxh"] = ("dv", vcp)
    P.wait("scalar", "dv")
    P.op("scalar", lambda e: e.activation(out=g["ST1"][:, 5:6], in_=g["EXF"][:, 3:4],
                                          func=AF.Sigmoid), inc="ac")
    P.wait("vector", "ac")
    P.op("vector", lambda e: e.tensor_scalar(
        out=g["ST1"][:, 6:7], in0=g["ST1"][:, 5:6], scalar1=-1.0, scalar2=1.0,
        op0=ALU.mult, op1=ALU.add), inc="dv")
    P.op("vector", lambda e: e.tensor_scalar_mul(g["ATTV"][:], g["HID"][:],
                                                 g["ST1"][:, 6:7]), inc="dv")
    P.op("vector", lambda e: e.scalar_tensor_tensor(
        out=g["PROC"][:], in0=g["PROC"][:], scalar=g["ST1"][:, 5:6], in1=g["ATTV"][:],
        op0=ALU.mult, op1=ALU.add), inc="dv")
    P.op("vector", lambda e: e.tensor_add(g["ATTV"][:], g["XSEL"][:, 0:D],
                                          g["PROC"][:]), inc="dv")
    P.op("vector", lambda e: e.tensor_copy(out=g["EXI"][:, 0:1], in_=g["EXF"][:, 4:5]),
         inc="dv")
    vfin = P.val("dv")
    P.wait("scalar", "dv", vfin)
    dma(P, "scalar", "qo1", g["out_sel"], g["ATTV"][:])
    dma(P, "scalar", "qo2", g["out_rowid"], g["EXI"][:, 0:1])


def _kvin_k_ap(g):
    return _mk_ap(g["d_kvin"], 0, [[128, 128], [16384, NKV], [1, 128]])


def _kvin_v_ap(g):
    return _mk_ap(g["d_kvin"], 131072, [[1024, 128], [128, NKV], [1, 128]])


def _kvout_k_ap(g, kvh):
    return _mk_ap(g["d_kvout"], kvh * 16384, [[128, 128], [KVPACK, NC], [1, 128]])


def _kvout_v_ap(g, kvh):
    return _mk_ap(g["d_kvout"], 131072 + kvh * 128,
                  [[1024, 128], [KVPACK, NC], [1, 128]])



def kernel(**inputs):
    global _LAST_RES
    in_maps = _prepare_in_maps(inputs)
    nc, taps = _build_program(debug=_DEBUG_TAPS)
    res = run_bass_kernel_spmd(nc, in_maps, core_ids=list(range(NC)))
    _LAST_RES = res
    return _assemble([r for r in res.results])


# version 05 (final)
# speedup vs baseline: 1.0279x; 1.0279x over previous
import numpy as np
import ml_dtypes
from contextlib import ExitStack

import concourse.bass as bass
import concourse.mybir as mybir
from concourse.bass_utils import run_bass_kernel_spmd
from concourse.bass_core_rust import IndirectOffsetOnAxis

F32 = mybir.dt.float32
BF16 = mybir.dt.bfloat16
I32 = mybir.dt.int32
AF = mybir.ActivationFunctionType
ALU = mybir.AluOpType
AXX = mybir.AxisListType.X
BF16NP = ml_dtypes.bfloat16

NC = 8
BB = 2
SEQ = 4096
D = 2048
DF = 6144
IM = 8192
SLOC = 512
TLOC = BB * SLOC
HALO = 3
KSEL = 512
NSLOT = BB * KSEL
SLOTC = NSLOT // NC
HD = 128
NH = 16
NKV = 8
DCH = D // 128
FCH = DF // 128
ICH = IM // 128
KVCH = (NKV * HD) // 128
XNROWS = 1152
HS = HALO + SLOC
EPS_LN = 1e-5
EPS_RMS_A = 1e-8
EPS_RMS_B = 1e-6
C_RGLRU = 8.0
ROPE_THETA = 10000.0
INV_SQRT_HD = float(1.0 / np.sqrt(HD))
NEG_BIG = -1.0e30
BISECT_ITERS = 40
PAYW = 2048
KVPACK = SLOTC * HD * NKV * 2

_DEBUG_TAPS = ()
_LAST_RES = None
_PHASES = 8
_P8STOP = 99



def _block_major(w, kdim, ndim):
    kch, nblk = kdim // 128, ndim // 128
    return np.ascontiguousarray(
        w.reshape(kch, 128, nblk, 128).transpose(2, 1, 0, 3).reshape(nblk, 128, kch * 128))


_BIGW = [
    ("L0_attn_left_W", "a_left_W", D, D),
    ("L0_attn_right_W", "a_right_W", D, D),
    ("L0_attn_ga_W", "a_ga_W", D, D),
    ("L0_attn_gx_W", "a_gx_W", D, D),
    ("L0_attn_out_W", "a_out_W", D, D),
    ("L0_attn_up1_W", "a_up1_W", D, DF),
    ("L0_attn_up2_W", "a_up2_W", D, DF),
    ("L0_attn_down_W", "a_down_W", DF, D),
    ("L0_attn_down_W", "a_down_W_r", D, DF),
    ("L0_ffn_wq", "f_wq", D, D),
    ("L0_ffn_wk", "f_wk", D, NKV * HD),
    ("L0_ffn_wv", "f_wv", D, NKV * HD),
    ("L0_ffn_wo", "f_wo", D, D),
    ("L0_ffn_w_gate", "f_w_gate", D, IM),
    ("L0_ffn_w_up", "f_w_up", D, IM),
    ("L0_ffn_w_down", "f_w_down", IM, D),
]

_VECW = [
    ("L0_attn_norm1_w", "a_norm1_w"), ("L0_attn_left_b", "a_left_b"),
    ("L0_attn_right_b", "a_right_b"), ("L0_attn_conv_b", "a_conv_b"),
    ("L0_attn_ga_b", "a_ga_b"), ("L0_attn_gx_b", "a_gx_b"),
    ("L0_attn_a_param", "a_a_param"), ("L0_attn_out_b", "a_out_b"),
    ("L0_attn_norm2_w", "a_norm2_w"), ("L0_attn_up1_b", "a_up1_b"),
    ("L0_attn_up2_b", "a_up2_b"), ("L0_attn_down_b", "a_down_b"),
    ("L0_ffn_w_router", "f_w_router"), ("L0_ffn_bq", "f_bq"),
    ("L0_ffn_bk", "f_bk"), ("L0_ffn_bv", "f_bv"),
    ("L0_ffn_ln1_w", "f_ln1_w"), ("L0_ffn_ln2_w", "f_ln2_w"),
]


def _prepare_in_maps(inp):
    x = np.asarray(inp["L0_attn_x"], np.float32)
    shared = {}
    for iname, lname, kdim, ndim in _BIGW:
        if lname == "a_down_W_r":
            shared[lname] = np.ascontiguousarray(
                np.asarray(inp[iname], np.float32).reshape(DF // 128, 128, D))
        else:
            shared[lname] = _block_major(np.asarray(inp[iname], np.float32), kdim, ndim)
    for iname, lname in _VECW:
        shared[lname] = np.ascontiguousarray(np.asarray(inp[iname], np.float32))
    shared["eye_f"] = np.eye(128, dtype=np.float32)
    shared["eye_h"] = np.eye(128, dtype=np.float32).astype(BF16NP)
    shared["ones_col"] = np.ones((128, 1), np.float32)
    shared["ones_row"] = np.ones((1, 128), np.float32)
    shared["ltri"] = (np.arange(128)[:, None] < np.arange(128)[None, :]).astype(np.float32)
    invf = 1.0 / ROPE_THETA ** (np.arange(0, HD, 2, dtype=np.float32) / HD)
    shared["invf_dup"] = np.concatenate([invf, invf])[None, :].astype(np.float32)
    rot = np.zeros((128, 128), np.float32)
    for p in range(64):
        rot[p + 64, p] = -1.0
        rot[p, p + 64] = 1.0
    shared["rot_mat"] = rot
    conv_w = np.asarray(inp["L0_attn_conv_w"], np.float32)
    conv_diag = np.zeros((64, 128, 128), np.float32)
    for ch in range(DCH):
        for k in range(4):
            np.fill_diagonal(conv_diag[ch * 4 + k], conv_w[k, ch * 128:(ch + 1) * 128])
    shared["conv_diag"] = conv_diag
    zeros_rb = np.zeros(D, np.float32)

    in_maps = []
    p128 = np.arange(128)
    for c in range(NC):
        m = dict(shared)
        xs = np.zeros((BB, HS, D), np.float32)
        lo = c * SLOC
        h0 = max(0, lo - HALO)
        xs[:, HALO - (lo - h0):HALO] = x[:, h0:lo]
        xs[:, HALO:] = x[:, lo:lo + SLOC]
        m["x"] = xs
        m["right_b_halo"] = shared["a_right_b"] if c > 0 else zeros_rb
        m["rank_oh_col"] = np.tile((np.arange(NC) == c).astype(np.float32), (128, 1))
        tau = np.zeros((128, 8), np.float32)
        rowid = np.zeros((128, 8), np.float32)
        for tt in range(8):
            b, q = tt // 4, tt % 4
            tv = c * SLOC + q * 128 + p128
            tau[:, tt] = tv
            rowid[:, tt] = b * SEQ + tv
        m["taucol"] = tau
        m["rowidcol"] = rowid
        m["gath_rows"] = rowid.astype(np.int32)
        m["slotcol"] = (c * SLOTC + p128)[:, None].astype(np.float32)
        in_maps.append(m)
    return in_maps


def _assemble(results):
    out = np.zeros((BB, SEQ, D), np.float32)
    for c in range(NC):
        ob = results[c]["out_base"]
        out[0, c * SLOC:(c + 1) * SLOC] = ob[:SLOC]
        out[1, c * SLOC:(c + 1) * SLOC] = ob[SLOC:]
    flat = out.reshape(BB * SEQ, D)
    for c in range(NC):
        rid = results[c]["out_rowid"][:, 0].astype(np.int64)
        flat[rid] = results[c]["out_sel"]
    return out



ENGS = ("sync", "tensor", "vector", "scalar", "gpsimd")


class Plan:
    def __init__(self, nc):
        self.nc = nc
        self.items = []
        self.semh = {}
        self.cnt = {}

    def new_sem(self, es, name):
        self.semh[name] = es.enter_context(self.nc.semaphore(name))
        self.cnt[name] = 0

    SELF_SEM = {"vector": "dv", "scalar": "ac", "gpsimd": "gp"}

    def op(self, eng, fn, inc=None, inc_n=1):
        own = self.SELF_SEM.get(eng)
        if own is not None and inc == own:
            self.wait(eng, own)
        if inc is not None:
            self.cnt[inc] += inc_n
            sem, n = self.semh[inc], inc_n
            self.items.append((eng, lambda e: fn(e).then_inc(sem, n)))
            return self.cnt[inc]
        self.items.append((eng, lambda e: fn(e)))
        return None

    def wait(self, eng, sem, val=None):
        v = self.cnt[sem] if val is None else val
        if v <= 0:
            return
        s = self.semh[sem]
        self.items.append((eng, lambda e: e.wait_ge(s, v)))

    def val(self, sem):
        return self.cnt[sem]

    def replay(self, block):
        def mk(engname):
            def body(engobj):
                for eng, fn in self.items:
                    if eng == engname:
                        fn(engobj)
            return body
        for engname in ENGS:
            getattr(block, engname)(mk(engname))


def dma(P, queue, sem, out, in_, transpose=False, **kw):
    if transpose:
        return P.op(queue, lambda e: e.dma_start(out=out, in_=in_, transpose=True, **kw),
                    inc=sem, inc_n=16)
    return P.op(queue, lambda e: e.dma_start(out=out, in_=in_, **kw), inc=sem, inc_n=16)


class WPool:
    def __init__(self, P, g, names, sems):
        self.P, self.g = P, g
        self.names = list(names)
        self.sems = dict(zip(names, sems))
        self.free_after = {n: None for n in names}
        self.i = 0
        self.pending = []

    def load(self, wap_block, width):
        n = self.names[self.i % len(self.names)]
        self.i += 1
        fa = self.free_after[n]
        if fa is not None:
            self.P.wait("gpsimd", fa[0], fa[1])
        v = dma(self.P, "gpsimd", self.sems[n], self.g[n][:, 0:width], wap_block)
        return n, v

    def prefetch(self, wap_block, width):
        self.pending.append(self.load(wap_block, width))

    def take(self):
        return self.pending.pop(0)

    def consumed(self, bufname, sem, val):
        self.free_after[bufname] = (sem, val)



def _build_program(debug=()):
    nc = bass.Bass("TRN2", target_bir_lowering=False)
    g = {}

    def din(name, shape, dt=F32):
        g[name] = nc.dram_tensor(name, shape, dt, kind="ExternalInput").ap()
    def dout(name, shape, dt=F32):
        g[name] = nc.dram_tensor(name, shape, dt, kind="ExternalOutput").ap()
    def dint(name, shape, dt=F32):
        g[name] = nc.dram_tensor(name, shape, dt).ap()

    din("x", [BB, HS, D])
    for iname, lname, kdim, ndim in _BIGW:
        din(lname, [ndim // 128, 128, (kdim // 128) * 128])
    for iname, lname in _VECW:
        n = {"f_bk": NKV * HD, "f_bv": NKV * HD, "a_up1_b": DF, "a_up2_b": DF}.get(lname, D)
        din(lname, [n])
    din("right_b_halo", [D])
    din("eye_f", [128, 128]); din("eye_h", [128, 128], BF16)
    din("ones_col", [128, 1]); din("ones_row", [1, 128])
    din("ltri", [128, 128]); din("invf_dup", [1, 128]); din("rot_mat", [128, 128])
    din("conv_diag", [64, 128, 128])
    din("rank_oh_col", [128, 8]); din("taucol", [128, 8]); din("rowidcol", [128, 8])
    din("gath_rows", [128, 8], I32); din("slotcol", [128, 1])

    dout("out_base", [TLOC, D])
    dout("out_sel", [SLOTC, D])
    dout("out_rowid", [SLOTC, 1], I32)

    dint("d_xn_park", [XNROWS, D], BF16)
    dint("d_xln_park", [TLOC, D], BF16)
    dint("d_xln_lo", [TLOC, D], BF16)
    dint("d_left_park", [128, DCH * TLOC], BF16)
    dint("d_atn_park", [128, DCH * TLOC])
    dint("d_glu", [FCH, 128, TLOC], BF16)
    dint("d_xa_park", [TLOC, D], BF16)
    dint("d_ex_park", [TLOC, 8])
    dint("d_slotrec", [BB * SEQ, 4], I32)
    dint("d_send", [NSLOT, PAYW], BF16)
    dint("d_recv", [NSLOT, PAYW], BF16)
    dint("d_send_ex", [NSLOT, 8])
    dint("d_recv_ex", [NSLOT, 8])
    dint("d_scin", [TLOC, 1])
    dint("d_scag", [NC * TLOC, 1])
    dint("d_cin", [128, 64])
    dint("d_cout", [NC * 128, 64])
    dint("d_kvin", [1, KVPACK], BF16)
    dint("d_kvout", [NC, KVPACK], BF16)

    taps = []
    def tap(name, shape, dt=F32):
        if name in debug:
            g["tap_" + name] = nc.dram_tensor("tap_" + name, shape, dt,
                                              kind="ExternalOutput").ap()
            taps.append(name)
            return True
        return False

    es = ExitStack()
    with es:
        def sb(name, shape, dt=F32):
            g[name] = es.enter_context(nc.sbuf_tensor(name, shape, dt))

        def psm(name, shape, dt=F32):
            g[name] = es.enter_context(nc.psum_tensor(name, shape, dt))

        sb("cEYEf", [128, 128]); sb("cEYEh", [128, 128], BF16)
        sb("cONEc", [128, 1]); sb("cONEr", [1, 128]); sb("cLTRI", [128, 128])
        sb("cINVF", [1, 128]); sb("cROT", [128, 128], BF16)
        sb("cROH", [128, 8]); sb("cTAU", [128, 8]); sb("cROW", [128, 8])
        sb("cGRW", [128, 8], I32); sb("cSLOT", [128, 1])
        sb("BIAS", [128, 384])
        sb("ST1", [128, 16]); sb("ST2", [128, 16]); sb("ST3", [128, 16])
        sb("ROW1", [1, 128])
        sb("CST", [128, 8])
        sb("SCR", [128, SLOC]); sb("SCR2", [128, SLOC]); sb("SCR3", [128, SLOC])
        sb("WB0", [128, DCH * 128], BF16); sb("WB1", [128, DCH * 128], BF16)
        sb("WC0", [128, DCH * 128], BF16); sb("WC1", [128, DCH * 128], BF16)

        sb("ARA", [128, 9216])
        sb("ARB", [128, 8320])
        sb("ARC", [128, 8192])
        sb("ARD", [128, 8192])
        sb("ARE", [128, 8192])
        sb("ARF", [128, 4096])

        def carve(arena, f32_off, f32_len, dt=F32):
            ap = g[arena][:, f32_off:f32_off + f32_len]
            if dt is F32:
                return ap
            return ap.bitcast(dt)

        g["XN"] = carve("ARA", 0, 9216, BF16)
        g["IH"] = carve("ARB", 0, 8192, BF16)
        g["X1a"] = carve("ARA", 0, 8192)
        g["UEXT"] = carve("ARB", 0, 8240, BF16)
        g["X1b"] = carve("ARB", 0, 8192)
        g["W1BC"] = carve("ARC", 0, 2048)
        g["XTOK"] = carve("ARC", 2048, 2048)
        g["TOK2"] = carve("ARC", 4096, 2048)
        g["XTOKb"] = carve("ARC", 6144, 2048)
        g["TOK2b"] = carve("ARB", 0, 2048)
        g["TOKHb1"] = carve("ARB", 2048, 1024, BF16)
        g["YB"] = carve("ARC", 0, 8192, BF16)
        g["RGT"] = carve("ARA", 0, 8192, BF16)
        g["WD0"] = carve("ARC", 0, 4096, BF16)
        g["WD1"] = carve("ARC", 4096, 4096, BF16)
        g["XTOK6"] = carve("ARC", 0, 2048)
        g["TOK26"] = carve("ARC", 2048, 2048)
        g["WRBC"] = carve("ARC", 4096, 2048)
        g["LAB"] = carve("ARD", 0, 8192, BF16)
        g["XLT0"] = carve("ARD", 0, 512, BF16)
        g["XLT1"] = carve("ARD", 512, 512, BF16)
        g["SQ4a"] = carve("ARD", 1024, 1024)
        g["SQ4b"] = carve("ARD", 2048, 1024)
        g["SS4"] = carve("ARD", 3072, 1024)
        g["RSB"] = carve("ARD", 4096, 1024)
        g["GB0"] = carve("ARD", 0, 256, BF16)
        g["GB1"] = carve("ARD", 256, 256, BF16)
        g["GSTG0"] = carve("ARD", 512, 512, BF16)
        g["GSTG1"] = carve("ARD", 1024, 512, BF16)
        g["DSTG0"] = carve("ARD", 6144, 512)
        g["DSTG1"] = carve("ARD", 6656, 512)
        g["ATNTa"] = carve("ARA", 0, 8192)
        g["ATNTb"] = carve("ARB", 0, 8192)
        g["ATHF0"] = carve("ARF", 1024, 1024)
        g["ATHF1"] = carve("ARF", 2048, 1024)
        g["BBT"] = carve("ARE", 0, 8192, BF16)
        g["XN2"] = carve("ARE", 0, 8192, BF16)
        g["XTOK6b"] = carve("ARD", 0, 2048)
        g["SQ6b"] = carve("ARD", 2048, 2048)
        g["HID6b"] = carve("ARD", 4096, 2048)
        g["TOKH6b"] = carve("ARD", 6144, 1024, BF16)
        g["EXT6b"] = carve("ARD", 7168, 8)
        g["SQ6"] = carve("ARE", 0, 2048)
        g["HID6"] = carve("ARE", 2048, 2048)
        g["SCO6"] = carve("ARE", 4096, 8)
        g["EXT6"] = carve("ARE", 4104, 8)
        g["CDIAG"] = carve("ARF", 0, 4096, BF16)
        g["AB"] = carve("ARF", 0, 64)
        g["CAR"] = carve("ARF", 64, 512)
        g["HPRE"] = carve("ARF", 576, 288)
        g["HMINE"] = carve("ARF", 864, 32)
        g["TOKH"] = carve("ARF", 1024, 1024, BF16)
        g["STAGE"] = carve("ARF", 2048, 128)
        g["HROW"] = carve("ARF", 2048, 2048)[0:1, :]
        g["SC7"] = carve("ARE", 4104, 64)
        g["MG7"] = carve("ARE", 4168, 32)
        g["MEQ7"] = carve("ARE", 4200, 32)
        g["MASK7"] = carve("ARE", 4232, 32)
        g["CEX7"] = carve("ARE", 4264, 32)
        g["INCL7"] = carve("ARE", 4296, 32)
        g["T7a"] = carve("ARE", 4328, 32)
        g["T7b"] = carve("ARE", 4360, 32)
        g["BIG7"] = carve("ARE", 4392, 32)
        g["ZERO7"] = carve("ARE", 4424, 32)
        g["STAG7"] = carve("ARE", 4456, 128).bitcast(I32)
        g["IDXI7"] = carve("ARE", 4584, 32).bitcast(I32)
        g["IDG7"] = carve("ARE", 4616, 4).bitcast(I32)
        g["EOFF7"] = carve("ARE", 4620, 1)
        g["MIDB7"] = carve("ARE", 4621, 1)
        g["HIB7"] = carve("ARE", 4622, 1)
        g["NEEDB7"] = carve("ARE", 4623, 1)
        g["CNTC7"] = carve("ARE", 4624, 1)
        g["P7A"] = carve("ARE", 3072, 1024, BF16)
        g["P7E"] = carve("ARE", 4626, 8)
        g["SCO8R"] = carve("ARE", 4640, 128)
        g["HID"] = carve("ARA", 0, 2048)
        g["XSEL"] = carve("ARA", 2048, 1024, BF16)
        g["EXF"] = carve("ARA", 3080, 8)
        g["EXI"] = carve("ARA", 3088, 8).bitcast(I32)
        g["QR"] = carve("ARA", 3096, 1024, BF16)
        g["KR"] = carve("ARA", 4120, 512, BF16)
        g["VTt"] = carve("ARA", 4632, 512, BF16)
        g["ANG"] = carve("ARA", 5144, 128)
        g["SINt"] = carve("ARA", 5272, 128)
        g["COSt"] = carve("ARA", 5400, 128)
        g["X1B"] = carve("ARA", 5528, 2048)
        g["QKVf"] = carve("ARA", 7576, 1024, BF16)
        g["T8"] = carve("ARA", 8600, 128)
        g["T8b"] = carve("ARA", 8728, 128)
        g["MHB8"] = carve("ARA", 8856, 1)
        g["KBH"] = carve("ARB", 0, 512, BF16)
        g["VBH"] = carve("ARB", 512, 512, BF16)
        g["ATTS"] = carve("ARB", 1024, 1024)
        g["ETS"] = carve("ARB", 2048, 512, BF16)
        g["EHS"] = carve("ARB", 2560, 512, BF16)
        g["AVO"] = carve("ARB", 3072, 1024, BF16)
        g["MASKM"] = carve("ARB", 4096, 1024)
        g["NEGT"] = carve("ARB", 5120, 1024)
        g["MTCH"] = carve("ARB", 6144, 512, BF16)
        g["QROW8"] = carve("ARB", 6656, 128)
        g["RBC8"] = carve("ARB", 6784, 128)
        g["KS8"] = carve("ARB", 6912, 8)
        g["DEN8"] = carve("ARB", 6920, 16)
        g["REC8"] = carve("ARB", 6936, 16)
        g["OSEL"] = carve("ARB", 6952, 1).bitcast(I32)
        g["H2"] = carve("ARD", 0, 1024, BF16)
        g["GLB"] = carve("ARD", 1024, 4096, BF16)
        g["PROC"] = carve("ARD", 5120, 2048)
        g["AOF"] = carve("ARD", 7168, 1024, BF16)
        g["MLPf"] = carve("ARE", 0, 1024, BF16)
        g["ATTV"] = carve("ARE", 1024, 2048)

        psm("PMA", [128, 1024]); psm("PMB", [128, 1024])
        psm("PMC", [128, 1024]); psm("PMD", [128, 1024])
        g["PM0"] = g["PMA"][:, :]
        g["PM1"] = g["PMB"][:, :]
        g["PN0"] = g["PMC"][:, 0:512]
        g["PN1"] = g["PMC"][:, 512:1024]
        g["PX"] = g["PMD"][:, 0:128]
        g["PS1"] = g["PMD"][0:1, 512:1024]

        P = Plan(nc)
        for s in ("mm", "dv", "ac", "gp", "cc", "qc", "qx", "qp", "qn",
                  "w0", "w1", "c0", "c1", "d0", "d1",
                  "q1", "q2", "q3", "q4", "q5", "q6", "q7", "q8", "q9",
                  "qa", "qb", "qd", "qe", "qf", "qg",
                  "qo1", "qo2", "qo3", "qtap"):
            P.new_sem(es, s)

        block = es.enter_context(nc.Block())
        _emit_all(nc, P, g, tap)
        P.replay(block)
    return nc, taps


def _bias_ap(g, name):
    off = g["_bias_off"][name]
    return g["BIAS"][:, off:off + g["_bias_n"][name]]



def _emit_all(nc, P, g, tap):
    phases = [_p1, _p2, _p3, _p4, _p5, _p6, _p7, _p8]
    _pre(nc, P, g)
    for i in range(min(_PHASES, 8)):
        phases[i](nc, P, g, tap)
    for sem in ("mm", "dv", "ac", "gp", "cc", "qc", "qx", "qp", "qn",
                "w0", "w1", "c0", "c1", "d0", "d1",
                "q1", "q2", "q3", "q4", "q5", "q6", "q7", "q8", "q9",
                "qa", "qb", "qd", "qe", "qf", "qg",
                "qo1", "qo2", "qo3", "qtap"):
        P.wait("gpsimd", sem)


CST_COLS = {}

def _cst(g, val, base=0, n=128):
    return g["CST"][base:base + n, CST_COLS[val]:CST_COLS[val] + 1]


def _pre(nc, P, g):
    for i, v in enumerate([EPS_LN, EPS_RMS_A, EPS_RMS_B, 1.0, 0.0]):
        CST_COLS[v] = i
        P.op("vector", lambda e, i_=i, v_=v: e.memset(g["CST"][:, i_:i_ + 1], v_),
             inc="dv")
    for n, dst in [("eye_f", "cEYEf"), ("eye_h", "cEYEh"), ("ones_col", "cONEc"),
                   ("ones_row", "cONEr"), ("ltri", "cLTRI"), ("invf_dup", "cINVF"),
                   ("rank_oh_col", "cROH"), ("taucol", "cTAU"), ("rowidcol", "cROW"),
                   ("gath_rows", "cGRW"), ("slotcol", "cSLOT")]:
        dma(P, "sync", "qc", g[dst][:], g[n])
    dma(P, "gpsimd", "q9", g["cROT"][:], g["rot_mat"])

    bias_vecs = [("a_norm1_w", DCH), ("a_left_b", DCH), ("a_right_b", DCH),
                 ("right_b_halo", DCH), ("a_conv_b", DCH), ("a_ga_b", DCH),
                 ("a_gx_b", DCH), ("a_a_param", DCH), ("a_out_b", DCH),
                 ("a_norm2_w", DCH), ("a_up1_b", FCH), ("a_up2_b", FCH),
                 ("a_down_b", DCH), ("f_bq", DCH), ("f_bk", NKV), ("f_bv", NKV),
                 ("f_ln1_w", DCH), ("f_ln2_w", DCH)]
    g["_bias_off"], g["_bias_n"] = {}, {}
    off = 0
    vz = P.op("vector", lambda e: e.memset(g["STAGE"][:], 0.0), inc="dv")
    P.wait("sync", "dv", vz)
    for name, nch in bias_vecs:
        g["_bias_off"][name], g["_bias_n"][name] = off, nch
        v = dma(P, "sync", "qc", g["STAGE"][0:nch, 0:128],
                g[name].rearrange("(a b) -> a b", b=128))
        P.wait("tensor", "qc", v)
        P.op("tensor", lambda e: e.transpose(g["PX"][:], g["STAGE"][:], g["cEYEf"][:]),
             inc="mm")
        P.wait("vector", "mm")
        v2 = P.op("vector", lambda e, o_=off, n_=nch:
                  e.tensor_copy(out=g["BIAS"][:, o_:o_ + n_], in_=g["PX"][:, 0:n_]),
                  inc="dv")
        P.wait("tensor", "dv", v2)
        P.wait("sync", "dv", v2)
        off += nch
    assert off <= 384

    ap = _bias_ap(g, "a_a_param")
    s1, s2 = g["ST1"][:, 0:DCH], g["ST2"][:, 0:DCH]
    P.wait("scalar", "dv")
    P.op("scalar", lambda e: e.activation(out=s1, in_=ap, func=AF.Abs), inc="ac")
    P.wait("scalar", "ac")
    P.op("scalar", lambda e: e.activation(out=s1, in_=s1, func=AF.Exp, scale=-1.0),
         inc="ac")
    P.wait("scalar", "ac")
    P.op("scalar", lambda e: e.activation(out=s1, in_=s1, func=AF.Ln, bias=_cst(g, 1.0)),
         inc="ac")
    P.wait("vector", "ac")
    P.op("vector", lambda e: e.tensor_scalar_max(s2, ap, 0.0), inc="dv")
    P.wait("vector", "dv")
    P.op("vector", lambda e: e.tensor_add(s2, s2, s1), inc="dv")
    P.wait("vector", "dv")
    P.op("vector", lambda e: e.tensor_scalar_mul(ap, s2, -C_RGLRU), inc="dv")

    _bcast_row(nc, P, g, g["a_norm1_w"], g["W1BC"], D)


def _bcast_row(nc, P, g, dram_vec, dst, n):
    v = dma(P, "sync", "qc", g["HROW"][0:1, 0:n], dram_vec.rearrange("(a b) -> a b", a=1))
    P.wait("tensor", "qc", v)
    for h in range(0, n, 1024):
        hw = min(1024, n - h)
        for j in range(0, hw, 512):
            jw = min(512, hw - j)
            P.op("tensor", lambda e, h_=h, j_=j, jw_=jw:
                 e.matmul(g["PM1"][:, j_:j_ + jw_], g["cONEr"][:],
                          g["HROW"][0:1, h_ + j_:h_ + j_ + jw_],
                          start=True, stop=True), inc="mm")
        P.wait("vector", "mm")
        v2 = P.op("vector", lambda e, h_=h, hw_=hw:
                  e.tensor_copy(out=dst[:, h_:h_ + hw_], in_=g["PM1"][:, 0:hw_]),
                  inc="dv")
        P.wait("tensor", "dv", v2)
    P.wait("sync", "dv", P.val("dv"))


def _ln_tile(nc, P, g, xap, base, n, eps, rms_eps=None, sq=None, out=None, col=0):
    st_m = g["ST1"][base:base + n, col:col + 1]
    st_r = g["ST2"][base:base + n, col:col + 1]
    st_3 = g["ST3"][base:base + n, col:col + 1]
    sq = g["TOK2"][base:base + n, :] if sq is None else sq
    P.op("vector", lambda e: e.reduce_sum(st_m, xap, axis=AXX), inc="dv")
    P.wait("vector", "dv")
    P.op("vector", lambda e: e.tensor_scalar_mul(st_m, st_m, 1.0 / D), inc="dv")
    P.wait("scalar", "dv")
    P.op("scalar", lambda e: e.activation(out=sq, in_=xap, func=AF.Square,
                                          accum_out=st_r), inc="ac")
    P.wait("vector", "ac")
    P.op("vector", lambda e: e.tensor_scalar_mul(st_r, st_r, 1.0 / D), inc="dv")
    P.op("vector", lambda e: e.tensor_mul(st_3, st_m, st_m), inc="dv")
    P.wait("vector", "dv")
    P.op("vector", lambda e: e.tensor_sub(st_r, st_r, st_3), inc="dv")
    P.wait("scalar", "dv")
    P.op("scalar", lambda e, e_=eps: e.activation(out=st_r, in_=st_r, func=AF.Ln,
                                                  bias=_cst(g, e_, base, n)), inc="ac")
    P.wait("scalar", "ac")
    P.op("scalar", lambda e: e.activation(out=st_r, in_=st_r, func=AF.Exp, scale=-0.5),
         inc="ac")
    P.wait("vector", "ac")
    out = xap if out is None else out
    P.op("vector", lambda e, o_=out: e.tensor_scalar(out=o_, in0=xap, scalar1=st_m,
                                                     scalar2=st_r, op0=ALU.subtract,
                                                     op1=ALU.mult), inc="dv")
    if rms_eps is not None:
        P.wait("scalar", "dv")
        P.op("scalar", lambda e, o_=out: e.activation(out=sq, in_=o_, func=AF.Square,
                                                      accum_out=st_3), inc="ac")
        P.wait("vector", "ac")
        P.op("vector", lambda e: e.tensor_scalar_mul(st_3, st_3, 1.0 / D), inc="dv")
        P.wait("scalar", "dv")
        P.op("scalar", lambda e, re_=rms_eps: e.activation(out=st_3, in_=st_3,
                                                           func=AF.Ln,
                                                           bias=_cst(g, re_, base, n)),
             inc="ac")
        P.wait("scalar", "ac")
        P.op("scalar", lambda e: e.activation(out=st_3, in_=st_3, func=AF.Exp,
                                              scale=-0.5), inc="ac")
        P.wait("vector", "ac")


def _p1(nc, P, g, tap):
    x = g["x"]
    tap
    tokh_free = [0, 0]
    xtok_free = [0, 0]
    for tt in range(8):
        b, q = tt // 4, tt % 4
        par = tt % 2
        XT = (g["XTOK"], g["XTOKb"])[par][:]
        T2 = (g["TOK2"], g["TOK2b"])[par]
        TH = (g["TOKH"], g["TOKHb1"])[par]
        qls = ("qx", "q8")[par]
        if xtok_free[par]:
            P.wait("sync", "dv", xtok_free[par])
        vld = dma(P, "sync", qls, XT, x[b, HALO + q * 128: HALO + (q + 1) * 128, :])
        P.wait("vector", qls, vld)
        P.wait("scalar", qls, vld)
        _ln_tile(nc, P, g, XT, 0, 128, EPS_LN, EPS_RMS_A, sq=T2[:], col=par)
        qps = ("qp", "qc")[par]
        if tokh_free[par]:
            P.wait("vector", qps, tokh_free[par])
        P.op("vector", lambda e, x_=XT, t_=TH: e.tensor_copy(out=t_[:], in_=x_),
             inc="dv")
        vxl = P.val("dv")
        P.wait("scalar", "dv", vxl)
        dma(P, "scalar", qps, g["d_xln_park"][tt * 128:(tt + 1) * 128, :], TH[:])
        tokh_free[par] = P.val(qps)
        P.op("vector", lambda e, x_=XT, t_=TH, s_=T2: e.tensor_sub(
            s_.bitcast(BF16)[:, 0:D], x_, t_[:]), inc="dv")
        vlo = P.val("dv")
        P.wait("scalar", "dv", vlo)
        dma(P, "scalar", qps, g["d_xln_lo"][tt * 128:(tt + 1) * 128, :],
            T2.bitcast(BF16)[:, 0:D])
        tokh_free[par] = P.val(qps)
        P.wait("vector", qps, tokh_free[par])
        P.op("vector", lambda e, x_=XT, t_=TH, p_=par: e.scalar_tensor_tensor(
            out=t_[:], in0=x_, scalar=g["ST3"][:, p_:p_ + 1],
            in1=g["W1BC"][:], op0=ALU.mult, op1=ALU.mult), inc="dv")
        xtok_free[par] = P.val("dv")
        P.wait("scalar", "dv", xtok_free[par])
        dma(P, "scalar", qps, g["d_xn_park"][tt * 128:(tt + 1) * 128, :], TH[:])
        tokh_free[par] = P.val(qps)
    P.wait("sync", "dv", xtok_free[0])
    P.wait("sync", "dv", xtok_free[1])
    dma(P, "sync", "qx", g["XTOK"][0:HALO, :], x[0, 0:HALO, :])
    vld = dma(P, "sync", "qx", g["XTOK"][32:32 + HALO, :], x[1, 0:HALO, :])
    P.wait("vector", "qx", vld)
    P.wait("scalar", "qx", vld)
    P.wait("vector", "qp", tokh_free[0])
    for base in (0, 32):
        _ln_tile(nc, P, g, g["XTOK"][base:base + HALO, :], base, HALO, EPS_LN,
                 EPS_RMS_A, sq=g["TOK2"][base:base + HALO, :])
        P.op("vector", lambda e, b_=base: e.scalar_tensor_tensor(
            out=g["TOKH"][b_:b_ + HALO, :], in0=g["XTOK"][b_:b_ + HALO, :],
            scalar=g["ST3"][b_:b_ + HALO, 0:1], in1=g["W1BC"][b_:b_ + HALO, :],
            op0=ALU.mult, op1=ALU.mult), inc="dv")
    vh = P.val("dv")
    P.wait("scalar", "dv", vh)
    dma(P, "scalar", "qp", g["d_xn_park"][1024:1024 + HALO, :], g["TOKH"][0:HALO, :])
    vhp = dma(P, "scalar", "qp", g["d_xn_park"][1027:1027 + HALO, :], g["TOKH"][32:32 + HALO, :])
    P.wait("vector", "qp", vhp)
    vz = P.op("vector", lambda e: e.memset(g["TOKH"][:], 0.0), inc="dv")
    P.wait("scalar", "dv", vz)
    dma(P, "scalar", "qp", g["d_xn_park"][1030:XNROWS, :], g["TOKH"][0:XNROWS - 1030, :])
    g["_p1_qt"] = P.val("qp")
    g["_p1_qt2"] = P.val("qc")
    g["_p1_dv"] = P.val("dv")
    if tap("xlnpark", [TLOC, D], BF16):
        P.wait("gpsimd", "qp", g["_p1_qt"])
        dma(P, "gpsimd", "qtap", g["tap_xlnpark"], g["d_xln_park"])
    if tap("xnpark", [XNROWS, D], BF16):
        P.wait("gpsimd", "qp", g["_p1_qt"])
        dma(P, "gpsimd", "qtap", g["tap_xnpark"], g["d_xn_park"])


def _p2(nc, P, g, tap):
    P.wait("scalar", "qp", g["_p1_qt"])
    P.wait("scalar", "qc", g["_p1_qt2"])
    for ch in range(DCH):
        dma(P, "scalar", "qn", g["XN"][:, ch * XNROWS:(ch + 1) * XNROWS],
            g["d_xn_park"][:, ch * 128:(ch + 1) * 128], transpose=True)
    g["_xn_qt"] = P.val("qn")

    pool = WPool(P, g, ["WB0", "WB1"], ["w0", "w1"])
    g["_wpool"] = pool
    pool2 = WPool(P, g, ["WC0", "WC1"], ["c0", "c1"])
    g["_wpool2"] = pool2
    rb = _bias_ap(g, "a_right_b")
    rbh = _bias_ap(g, "right_b_halo")
    lb = _bias_ap(g, "a_left_b")
    psfree = g.setdefault("_psf", {})

    pool.prefetch(g["a_right_W"][0, :, :], DCH * 128)
    for oh in range(DCH):
        if oh + 1 < DCH:
            pool.prefetch(g["a_right_W"][oh + 1, :, :], DCH * 128)
        bn, lv = pool.take()
        pm = (g["PM0"], g["PM1"])[oh % 2]
        key = ("PM0", "PM1")[oh % 2]
        if psfree.get(key):
            P.wait("tensor", psfree[key][0], psfree[key][1])
        P.wait("tensor", pool.sems[bn], lv)
        if oh == 0:
            P.wait("tensor", "qn", g["_xn_qt"])
        if psfree.get("PH"):
            P.wait("tensor", psfree["PH"][0], psfree["PH"][1])
        for k in range(DCH):
            for (po, mo, wd) in [(0, 0, 512), (512, 512, 512)]:
                P.op("tensor", lambda e, k_=k, po_=po, mo_=mo, wd_=wd, pm_=pm, bn_=bn:
                     e.matmul(pm_[:, po_:po_ + wd_],
                              g[bn_][:, k_ * 128:(k_ + 1) * 128],
                              g["XN"][:, k_ * XNROWS + mo_: k_ * XNROWS + mo_ + wd_],
                              start=(k_ == 0), stop=(k_ == DCH - 1)), inc=None)
            P.op("tensor", lambda e, k_=k, bn_=bn:
                 e.matmul(g["PX"][:, 0:2 * HALO],
                          g[bn_][:, k_ * 128:(k_ + 1) * 128],
                          g["XN"][:, k_ * XNROWS + 1024: k_ * XNROWS + 1024 + 2 * HALO],
                          start=(k_ == 0), stop=(k_ == DCH - 1)),
                 inc="mm" if k == DCH - 1 else None)
        pool.consumed(bn, "mm", P.val("mm"))
        P.wait("vector", "mm")
        for (p0, u0) in [(0, HALO), (512, HS + HALO)]:
            P.op("vector", lambda e, oh_=oh, p0_=p0, u0_=u0, pm_=pm:
                 e.tensor_scalar_add(g["UEXT"][:, oh_ * BB * HS + u0_:
                                               oh_ * BB * HS + u0_ + 512],
                                     pm_[:, p0_:p0_ + 512],
                                     rb[:, oh_:oh_ + 1]), inc="dv")
        for b in range(BB):
            P.op("vector", lambda e, oh_=oh, b_=b:
                 e.tensor_scalar_add(g["UEXT"][:, oh_ * BB * HS + b_ * HS:
                                               oh_ * BB * HS + b_ * HS + HALO],
                                     g["PX"][:, b_ * HALO:(b_ + 1) * HALO],
                                     rbh[:, oh_:oh_ + 1]), inc="dv")
        vac = P.val("dv")
        psfree[key] = ("dv", vac)
        psfree["PH"] = ("dv", vac)
    g["_uext_dv"] = P.val("dv")

    g["_uext_done"] = True
    g["_p2_mm"] = P.val("mm")

    if tap("xn", [128, DCH * XNROWS], BF16):
        P.wait("gpsimd", "qn", g["_xn_qt"])
        dma(P, "gpsimd", "qtap", g["tap_xn"], g["XN"][:])
    if tap("uext", [128, DCH * BB * HS], BF16):
        P.wait("gpsimd", "dv", g["_uext_dv"])
        dma(P, "gpsimd", "qtap", g["tap_uext"], g["UEXT"][:, 0:DCH * BB * HS])



def _p3(nc, P, g, tap):
    pool, pool2 = g["_wpool"], g["_wpool2"]
    psf = g["_psf"]

    P.wait("gpsimd", "q9", 16)
    vcd = dma(P, "gpsimd", "q9", g["CDIAG"][:],
              g["conv_diag"].rearrange("n p f -> p n f"))

    cb = _bias_ap(g, "a_conv_b")
    pmh = [(g["PM0"], 0), (g["PM0"], 512), (g["PM1"], 0), (g["PM1"], 512)]
    P.wait("tensor", "q9", vcd)
    P.wait("tensor", "dv", g["_uext_dv"])
    i = 0
    for ch in range(DCH):
        for b in range(BB):
            pm, po = pmh[i % 4]
            key = "conv" + str(i % 4)
            if psf.get(key):
                P.wait("tensor", psf[key][0], psf[key][1])
            for k in range(4):
                P.op("tensor", lambda e, ch_=ch, b_=b, k_=k, pm_=pm, po_=po:
                     e.matmul(pm_[:, po_:po_ + 512],
                              g["CDIAG"][:, (ch_ * 4 + k_) * 128:(ch_ * 4 + k_ + 1) * 128],
                              g["UEXT"][:, ch_ * BB * HS + b_ * HS + k_:
                                        ch_ * BB * HS + b_ * HS + k_ + 512],
                              start=(k_ == 0), stop=(k_ == 3)),
                     inc="mm" if k == 3 else None)
            P.wait("vector", "mm")
            v = P.op("vector", lambda e, ch_=ch, b_=b, pm_=pm, po_=po:
                     e.tensor_scalar_add(g["YB"][:, ch_ * TLOC + b_ * 512:
                                                 ch_ * TLOC + b_ * 512 + 512],
                                         pm_[:, po_:po_ + 512],
                                         cb[:, ch_:ch_ + 1]), inc="dv")
            psf[key] = ("dv", v)
            i += 1
    g["_y_dv"] = P.val("dv")
    if tap("ym", [128, DCH * TLOC], BF16):
        P.wait("gpsimd", "dv", g["_y_dv"])
        dma(P, "gpsimd", "qtap", g["tap_ym"], g["YB"][:])

    gab = _bias_ap(g, "a_ga_b")
    gxb = _bias_ap(g, "a_gx_b")
    spm8 = _bias_ap(g, "a_a_param")
    pool.prefetch(g["a_ga_W"][0, :, :], DCH * 128)
    pool2.prefetch(g["a_gx_W"][0, :, :], DCH * 128)
    for oh in range(DCH):
        if oh + 1 < DCH:
            pool.prefetch(g["a_ga_W"][oh + 1, :, :], DCH * 128)
            pool2.prefetch(g["a_gx_W"][oh + 1, :, :], DCH * 128)
        else:
            pool.prefetch(g["a_left_W"][0, :, :], DCH * 128)
        bga, vga = pool.take()
        bgx, vgx = pool2.take()
        pma = (g["PM0"], g["PMC"])[oh % 2]
        pmx = (g["PM1"], g["PMD"])[oh % 2]
        keya, keyx = "ga" + str(oh % 2), "gx" + str(oh % 2)
        vmm_grp = []
        for kk, (bname, vv, pm_, pl_) in enumerate(
                [(bga, vga, pma, pool), (bgx, vgx, pmx, pool2)]):
            key = (keya, keyx)[kk]
            if psf.get(key):
                P.wait("tensor", psf[key][0], psf[key][1])
            P.wait("tensor", pl_.sems[bname], vv)
            if oh == 0 and kk == 0:
                P.wait("tensor", "dv", g["_y_dv"])
            for k in range(DCH):
                for hb in range(BB):
                    P.op("tensor", lambda e, k_=k, b_=bname, pm2=pm_, hb_=hb:
                         e.matmul(pm2[:, hb_ * 512:hb_ * 512 + 512],
                                  g[b_][:, k_ * 128:(k_ + 1) * 128],
                                  g["YB"][:, k_ * TLOC + hb_ * 512:
                                          k_ * TLOC + hb_ * 512 + 512],
                                  start=(k_ == 0), stop=(k_ == DCH - 1)),
                         inc="mm" if (k == DCH - 1 and hb == BB - 1) else None)
            pl_.consumed(bname, "mm", P.val("mm"))
            vmm_grp.append(P.val("mm"))
        for ss, skey in ((g["SCR"], "scA"), (g["SCR3"], "scB")):
            if psf.get(skey):
                P.wait("scalar", psf[skey][0], psf[skey][1])
        P.wait("scalar", "mm", vmm_grp[0])
        va1 = P.op("scalar", lambda e, oh_=oh, pm_=pma:
                   e.activation(out=g["SCR"][:], in_=pm_[:, 0:512],
                                func=AF.Sigmoid, bias=gab[:, oh_:oh_ + 1]),
                   inc="ac")
        va1b = P.op("scalar", lambda e, oh_=oh, pm_=pma:
                    e.activation(out=g["SCR3"][:], in_=pm_[:, 512:1024],
                                 func=AF.Sigmoid, bias=gab[:, oh_:oh_ + 1]),
                    inc="ac")
        P.wait("scalar", "mm", vmm_grp[1])
        va2 = P.op("scalar", lambda e, oh_=oh, pm_=pmx:
                   e.activation(out=g["IH"][:, oh_ * 1024:oh_ * 1024 + 512],
                                in_=pm_[:, 0:512],
                                func=AF.Sigmoid, bias=gxb[:, oh_:oh_ + 1]),
                   inc="ac")
        va2b = P.op("scalar", lambda e, oh_=oh, pm_=pmx:
                    e.activation(out=g["IH"][:, oh_ * 1024 + 512:oh_ * 1024 + 1024],
                                 in_=pm_[:, 512:1024],
                                 func=AF.Sigmoid, bias=gxb[:, oh_:oh_ + 1]),
                    inc="ac")
        psf[keya] = ("ac", va1b)
        psf[keyx] = ("ac", va2b)
        P.wait("vector", "ac", va1)
        vla0 = P.op("vector", lambda e, oh_=oh:
                    e.tensor_scalar_mul(
                        g["LAB"][:, oh_ * TLOC: oh_ * TLOC + 512],
                        g["SCR"][:], spm8[:, oh_:oh_ + 1]), inc="dv")
        P.wait("vector", "ac", va1b)
        vla1 = P.op("vector", lambda e, oh_=oh:
                    e.tensor_scalar_mul(
                        g["LAB"][:, oh_ * TLOC + 512: oh_ * TLOC + 1024],
                        g["SCR3"][:], spm8[:, oh_:oh_ + 1]), inc="dv")
        psf["scA"] = ("dv", vla0)
        psf["scB"] = ("dv", vla1)
    for gb in range(2 * DCH):
        oh, hb = gb // 2, gb % 2
        la_ap = g["LAB"][:, oh * TLOC + hb * 512: oh * TLOC + hb * 512 + 512]
        scr_m = g["SCR"] if gb % 2 == 0 else g["SCR3"]
        key = "scr" + str(gb % 2)
        if psf.get(key):
            P.wait("scalar", psf[key][0], psf[key][1])
        if gb == 0:
            P.wait("scalar", "dv")
        P.op("scalar", lambda e, la_=la_ap, s_=scr_m:
             e.activation(out=s_[:], in_=la_, func=AF.Exp, scale=2.0), inc="ac")
        P.wait("scalar", "ac")
        P.op("scalar", lambda e, s_=scr_m:
             e.activation(out=s_[:], in_=s_[:], func=AF.Ln, scale=-1.0,
                          bias=_cst(g, 1.0)), inc="ac")
        P.wait("scalar", "ac")
        vmu = P.op("scalar", lambda e, s_=scr_m:
                   e.activation(out=s_[:], in_=s_[:], func=AF.Exp, scale=0.5),
                   inc="ac")
        P.wait("vector", "ac", vmu)
        P.op("vector", lambda e, oh_=oh, hb_=hb:
             e.tensor_mul(g["SCR2"][:],
                          g["IH"][:, oh_ * 1024 + hb_ * 512:
                                  oh_ * 1024 + hb_ * 512 + 512],
                          g["YB"][:, oh_ * TLOC + hb_ * 512:
                                  oh_ * TLOC + hb_ * 512 + 512]), inc="dv")
        P.wait("vector", "dv")
        vb = P.op("vector", lambda e, oh_=oh, hb_=hb, s_=scr_m:
                  e.tensor_mul(g["BBT"][:, oh_ * TLOC + hb_ * 512:
                                        oh_ * TLOC + hb_ * 512 + 512],
                               g["SCR2"][:], s_[:]), inc="dv")
        psf[key] = ("dv", vb)
    g["_b_dv"] = P.val("dv")
    g["_gates_ac"] = P.val("ac")
    lb = _bias_ap(g, "a_left_b")
    for oh in range(DCH):
        if oh + 1 < DCH:
            pool.prefetch(g["a_left_W"][oh + 1, :, :], DCH * 128)
        bn, lv = pool.take()
        pm = (g["PM0"], g["PM1"])[oh % 2]
        key = ("PM0", "PM1")[oh % 2]
        if oh < 2:
            P.wait("tensor", "ac", g["_gates_ac"])
        if psf.get(key):
            P.wait("tensor", psf[key][0], psf[key][1])
        P.wait("tensor", pool.sems[bn], lv)
        for k in range(DCH):
            for ri, (po, mo, wd) in enumerate([(0, 0, 512), (512, 512, 512)]):
                P.op("tensor", lambda e, k_=k, po_=po, mo_=mo, wd_=wd, pm_=pm, bn_=bn:
                     e.matmul(pm_[:, po_:po_ + wd_],
                              g[bn_][:, k_ * 128:(k_ + 1) * 128],
                              g["XN"][:, k_ * XNROWS + mo_: k_ * XNROWS + mo_ + wd_],
                              start=(k_ == 0), stop=(k_ == DCH - 1)),
                     inc="mm" if (k == DCH - 1 and ri == 1) else None)
        pool.consumed(bn, "mm", P.val("mm"))
        P.wait("scalar", "mm")
        if oh == 0:
            P.wait("scalar", "dv", g["_b_dv"])
            P.wait("scalar", "qtap")
        for p0 in (0, 512):
            P.op("scalar", lambda e, oh_=oh, p0_=p0, pm_=pm:
                 e.activation(out=g["YB"][:, oh_ * TLOC + p0_: oh_ * TLOC + p0_ + 512],
                              in_=pm_[:, p0_:p0_ + 512], func=AF.Gelu,
                              bias=lb[:, oh_:oh_ + 1]), inc="ac")
        psf[key] = ("ac", P.val("ac"))
    g["_left_ac"] = P.val("ac")
    g["_left_mm"] = P.val("mm")
    if tap("la", [128, DCH * TLOC], BF16):
        P.wait("gpsimd", "dv", g["_b_dv"])
        dma(P, "gpsimd", "qtap", g["tap_la"], g["LAB"][:])
    if tap("bb", [128, DCH * TLOC], BF16):
        P.wait("gpsimd", "dv", g["_b_dv"])
        dma(P, "gpsimd", "qtap", g["tap_bb"], g["BBT"][:])

    for gb in range(32):
        oh, b = gb // 2, gb % 2
        P.op("vector", lambda e, oh_=oh, b_=b, gb_=gb:
             e.reduce_sum(g["AB"][:, gb_:gb_ + 1],
                          g["LAB"][:, oh_ * TLOC + b_ * 512:
                                   oh_ * TLOC + b_ * 512 + 512], axis=AXX),
             inc="dv")
    P.wait("scalar", "dv")
    P.op("scalar", lambda e: e.activation(out=g["AB"][:, 0:32], in_=g["AB"][:, 0:32],
                                          func=AF.Exp), inc="ac")
    vab = P.val("ac")
    for gb in range(32):
        oh, b = gb // 2, gb % 2
        scr_a = g["SCR"] if gb % 2 == 0 else g["SCR3"]
        key = "scn" + str(gb % 2)
        if psf.get(key):
            P.wait("scalar", psf[key][0], psf[key][1])
        P.op("scalar", lambda e, oh_=oh, b_=b, s_=scr_a:
             e.activation(out=s_[:], in_=g["LAB"][:, oh_ * TLOC + b_ * 512:
                                                  oh_ * TLOC + b_ * 512 + 512],
                          func=AF.Exp), inc="ac")
        P.wait("vector", "ac")
        P.op("vector", lambda e, oh_=oh, b_=b, s_=scr_a:
             e.tensor_tensor_scan(out=g["SCR2"][:], data0=s_[:],
                                  data1=g["BBT"][:, oh_ * TLOC + b_ * 512:
                                                 oh_ * TLOC + b_ * 512 + 512],
                                  initial=0.0, op0=ALU.mult, op1=ALU.add),
             inc="dv")
        P.wait("vector", "dv")
        vcp = P.op("vector", lambda e, gb_=gb:
                   e.tensor_copy(out=g["AB"][:, 32 + gb_:33 + gb_],
                                 in_=g["SCR2"][:, 511:512]), inc="dv")
        psf[key] = ("dv", vcp)
    P.wait("gpsimd", "ac", vab)
    P.wait("gpsimd", "dv")
    vci = dma(P, "gpsimd", "q2", g["d_cin"], g["AB"][:])
    P.wait("gpsimd", "q2", vci)
    P.op("gpsimd", lambda e: e.collective_compute(
        "AllGather", ALU.bypass, replica_groups=[list(range(NC))],
        ins=[g["d_cin"].tensor.ap().opt()], outs=[g["d_cout"].tensor.ap().opt()]),
        inc="cc")
    P.wait("gpsimd", "cc")
    vcar = dma(P, "gpsimd", "q3", g["CAR"][:], g["d_cout"])
    P.wait("vector", "q3", vcar)
    P.op("vector", lambda e: e.memset(g["HPRE"][:, 0:32], 0.0), inc="dv")
    for r in range(NC):
        P.wait("vector", "dv")
        P.op("vector", lambda e, r_=r:
             e.tensor_mul(g["HPRE"][:, (r_ + 1) * 32:(r_ + 2) * 32],
                          g["CAR"][:, r_ * 64: r_ * 64 + 32],
                          g["HPRE"][:, r_ * 32:(r_ + 1) * 32]), inc="dv")
        P.wait("vector", "dv")
        P.op("vector", lambda e, r_=r:
             e.tensor_add(g["HPRE"][:, (r_ + 1) * 32:(r_ + 2) * 32],
                          g["HPRE"][:, (r_ + 1) * 32:(r_ + 2) * 32],
                          g["CAR"][:, r_ * 64 + 32: r_ * 64 + 64]), inc="dv")
    P.op("vector", lambda e: e.memset(g["HMINE"][:], 0.0), inc="dv")
    for r in range(NC):
        P.wait("vector", "dv")
        P.op("vector", lambda e, r_=r:
             e.scalar_tensor_tensor(out=g["HMINE"][:],
                                    in0=g["HPRE"][:, r_ * 32:(r_ + 1) * 32],
                                    scalar=g["cROH"][:, r_:r_ + 1],
                                    in1=g["HMINE"][:], op0=ALU.mult, op1=ALU.add),
             inc="dv")
    P.wait("vector", "qtap")
    P.wait("vector", "mm", g.get("_left_mm", 0))
    for gb in range(32):
        oh, b = gb // 2, gb % 2
        scr_a = g["SCR"] if gb % 2 == 0 else g["SCR3"]
        key = "scn" + str(gb % 2)
        if psf.get(key):
            P.wait("scalar", psf[key][0], psf[key][1])
        if gb == 0:
            P.wait("scalar", "dv")
        P.op("scalar", lambda e, oh_=oh, b_=b, s_=scr_a:
             e.activation(out=s_[:], in_=g["LAB"][:, oh_ * TLOC + b_ * 512:
                                                  oh_ * TLOC + b_ * 512 + 512],
                          func=AF.Exp), inc="ac")
        P.wait("vector", "ac")
        vsc = P.op("vector", lambda e, oh_=oh, b_=b, gb_=gb, s_=scr_a:
                   e.tensor_tensor_scan(
                       out=g["RGT"][:, oh_ * TLOC + b_ * 512:
                                    oh_ * TLOC + b_ * 512 + 512],
                       data0=s_[:],
                       data1=g["BBT"][:, oh_ * TLOC + b_ * 512:
                                      oh_ * TLOC + b_ * 512 + 512],
                       initial=g["HMINE"][:, gb_:gb_ + 1],
                       op0=ALU.mult, op1=ALU.add), inc="dv")
        psf[key] = ("dv", vsc)
    g["_scan2_dv"] = P.val("dv")
    if tap("right", [128, DCH * TLOC], BF16):
        P.wait("gpsimd", "dv", g["_scan2_dv"])
        dma(P, "gpsimd", "qtap", g["tap_right"], g["RGT"][:])
    P.wait("vector", "ac", g["_left_ac"])
    for ch in range(DCH):
        vmu = P.op("vector", lambda e, ch_=ch:
                   e.tensor_mul(g["YB"][:, ch_ * TLOC:(ch_ + 1) * TLOC],
                                g["YB"][:, ch_ * TLOC:(ch_ + 1) * TLOC],
                                g["RGT"][:, ch_ * TLOC:(ch_ + 1) * TLOC]), inc="dv")
    g["_lr_dv"] = P.val("dv")
    if tap("lr", [128, DCH * TLOC], BF16):
        P.wait("gpsimd", "dv", g["_lr_dv"])
        dma(P, "gpsimd", "qtap", g["tap_lr"], g["YB"][:])


def _x1c(g, ch):
    if ch < 8:
        return g["X1a"][:, ch * 1024:(ch + 1) * 1024]
    return g["X1b"][:, (ch - 8) * 1024:(ch - 7) * 1024]


def _p4(nc, P, g, tap):
    pool = g["_wpool"]
    psf = g["_psf"]
    ob = _bias_ap(g, "a_out_b")
    n2w = _bias_ap(g, "a_norm2_w")

    P.wait("scalar", "qp", g["_p1_qt"])
    P.wait("scalar", "qc", g["_p1_qt2"])
    P.wait("scalar", "qtap")
    P.wait("vector", "qtap")
    for oh in range(DCH):
        xbuf = ("XLT0", "XLT1")[oh % 2]
        lbuf = ("SQ4a", "SQ4b")[oh % 2]
        qsem = ("q4", "q5")[oh % 2]
        key = "xlt" + str(oh % 2)
        if psf.get(key):
            P.wait("scalar", psf[key][0], psf[key][1])
        vxl = dma(P, "scalar", qsem, g[xbuf][:, 0:1024],
                  g["d_xln_park"][:, oh * 128:(oh + 1) * 128], transpose=True)
        vxl = dma(P, "scalar", qsem, g[lbuf].bitcast(BF16)[:, 0:1024],
                  g["d_xln_lo"][:, oh * 128:(oh + 1) * 128], transpose=True)
        if oh == 0:
            pool.prefetch(g["a_out_W"][0, :, :], DCH * 128)
        if oh + 1 < DCH:
            pool.prefetch(g["a_out_W"][oh + 1, :, :], DCH * 128)
        bn, lv = pool.take()
        pm = (g["PM0"], g["PM1"])[oh % 2]
        pkey = "PM" + str(oh % 2)
        if psf.get(pkey):
            P.wait("tensor", psf[pkey][0], psf[pkey][1])
        P.wait("tensor", pool.sems[bn], lv)
        if oh == 0:
            P.wait("tensor", "dv", g["_lr_dv"])
        for k in range(DCH):
            for ri, (po, mo, wd) in enumerate([(0, 0, 512), (512, 512, 512)]):
                P.op("tensor", lambda e, k_=k, po_=po, mo_=mo, wd_=wd, pm_=pm, bn_=bn:
                     e.matmul(pm_[:, po_:po_ + wd_],
                              g[bn_][:, k_ * 128:(k_ + 1) * 128],
                              g["YB"][:, k_ * TLOC + mo_: k_ * TLOC + mo_ + wd_],
                              start=(k_ == 0), stop=(k_ == DCH - 1)),
                     inc="mm" if (k == DCH - 1 and ri == 1) else None)
        pool.consumed(bn, "mm", P.val("mm"))
        P.wait("vector", "mm")
        P.wait("vector", qsem, vxl)
        P.op("vector", lambda e, oh_=oh, pm_=pm, b_=xbuf:
             e.scalar_tensor_tensor(out=_x1c(g, oh_), in0=pm_[:],
                                    scalar=ob[:, oh_:oh_ + 1],
                                    in1=g[b_][:, 0:1024],
                                    op0=ALU.add, op1=ALU.add), inc="dv")
        vev = P.op("vector", lambda e, oh_=oh, b_=lbuf:
                   e.tensor_add(_x1c(g, oh_), _x1c(g, oh_),
                                g[b_].bitcast(BF16)[:, 0:1024]), inc="dv")
        psf[pkey] = ("dv", vev)
        psf[key] = ("dv", vev)
    g["_x1_dv"] = P.val("dv")
    if tap("x1a", [128, 8192]):
        P.wait("gpsimd", "dv", g["_x1_dv"])
        dma(P, "gpsimd", "qtap", g["tap_x1a"], g["X1a"])
    if tap("x1b", [128, 8192]):
        P.wait("gpsimd", "dv", g["_x1_dv"])
        dma(P, "gpsimd", "qtap", g["tap_x1b"], g["X1b"])

    P.wait("scalar", "dv", g["_x1_dv"])
    for half in range(2):
        for oh in range(DCH):
            sq = (g["SQ4a"], g["SQ4b"])[oh % 2]
            key = "sq" + str(oh % 2)
            if psf.get(key):
                P.wait("scalar", psf[key][0], psf[key][1])
            P.op("scalar", lambda e, oh_=oh, s_=sq, h_=half:
                 e.activation(out=s_[:, 0:512],
                              in_=_x1c(g, oh_)[:, h_ * 512:(h_ + 1) * 512],
                              func=AF.Square), inc="ac")
            P.wait("tensor", "ac")
            if oh == 0:
                P.wait("tensor", "dv")
            P.op("tensor", lambda e, s_=sq, oh_=oh:
                 e.matmul(g["PS1"][:], g["cONEc"][:], s_[:, 0:512],
                          start=(oh_ == 0), stop=(oh_ == DCH - 1)), inc="mm")
            psf[key] = ("mm", P.val("mm"))
        P.wait("vector", "mm")
        vss = P.op("vector", lambda e, h_=half:
                   e.tensor_copy(out=g["SS4"][0:1, h_ * 512:(h_ + 1) * 512],
                                 in_=g["PS1"][:]), inc="dv")
    P.wait("scalar", "dv", vss)
    P.op("scalar", lambda e: e.activation(out=g["SS4"][0:1, :], in_=g["SS4"][0:1, :],
                                          func=AF.Ln, scale=1.0 / D,
                                          bias=_cst(g, EPS_RMS_A, 0, 1)), inc="ac")
    P.wait("scalar", "ac")
    vrs = P.op("scalar", lambda e: e.activation(out=g["SS4"][0:1, :],
                                                in_=g["SS4"][0:1, :],
                                                func=AF.Exp, scale=-0.5), inc="ac")
    P.wait("tensor", "ac", vrs)
    for half in range(2):
        P.op("tensor", lambda e, h_=half:
             e.matmul(g["PM0"][:, h_ * 512:(h_ + 1) * 512], g["cONEr"][:],
                      g["SS4"][0:1, h_ * 512:(h_ + 1) * 512], start=True, stop=True),
             inc="mm")
    P.wait("vector", "mm")
    vrsb = P.op("vector", lambda e: e.tensor_copy(out=g["RSB"][:], in_=g["PM0"][:]),
                inc="dv")
    for oh in range(DCH):
        P.op("vector", lambda e, oh_=oh:
             e.scalar_tensor_tensor(out=g["XN2"][:, oh_ * 1024:(oh_ + 1) * 1024],
                                    in0=_x1c(g, oh_), scalar=n2w[:, oh_:oh_ + 1],
                                    in1=g["RSB"][:], op0=ALU.mult, op1=ALU.mult),
             inc="dv")
    g["_xn2_dv"] = P.val("dv")
    g["_pm0_free"] = ("dv", vrsb)
    if tap("xn2", [128, DCH * TLOC], BF16):
        P.wait("gpsimd", "dv", g["_xn2_dv"])
        dma(P, "gpsimd", "qtap", g["tap_xn2"], g["XN2"][:])


def _p5(nc, P, g, tap):
    pool, pool2 = g["_wpool"], g["_wpool2"]
    psf = g["_psf"]
    u1b = _bias_ap(g, "a_up1_b")
    u2b = _bias_ap(g, "a_up2_b")
    dnb = _bias_ap(g, "a_down_b")

    P.wait("scalar", "qtap")
    glu_w = []
    pool.prefetch(g["a_up1_W"][0, :, :], DCH * 128)
    pool2.prefetch(g["a_up2_W"][0, :, :], DCH * 128)
    for of in range(FCH):
        if of + 1 < FCH:
            pool.prefetch(g["a_up1_W"][of + 1, :, :], DCH * 128)
            pool2.prefetch(g["a_up2_W"][of + 1, :, :], DCH * 128)
        b1, v1 = pool.take()
        b2, v2 = pool2.take()
        pmg = (g["PM0"], g["PM1"])[of % 2]
        pmu = (g["PN0"], g["PN1"])[of % 2]
        keyg, keyu = "pmg" + str(of % 2), "pmu" + str(of % 2)
        if psf.get(keyg):
            P.wait("tensor", psf[keyg][0], psf[keyg][1])
        P.wait("tensor", pool.sems[b1], v1)
        if of == 0:
            P.wait("tensor", "dv", g["_xn2_dv"])
        for k in range(DCH):
            for ri, (po, mo, wd) in enumerate([(0, 0, 512), (512, 512, 512)]):
                P.op("tensor", lambda e, k_=k, po_=po, mo_=mo, wd_=wd, pm_=pmg, b_=b1:
                     e.matmul(pm_[:, po_:po_ + wd_],
                              g[b_][:, k_ * 128:(k_ + 1) * 128],
                              g["XN2"][:, k_ * TLOC + mo_: k_ * TLOC + mo_ + wd_],
                              start=(k_ == 0), stop=(k_ == DCH - 1)),
                     inc="mm" if (k == DCH - 1 and ri == 1) else None)
        pool.consumed(b1, "mm", P.val("mm"))
        P.wait("scalar", "mm")
        scr_g = (g["SCR"], g["SCR3"])[of % 2]
        skey = "scr5" + str(of % 2)
        if psf.get(skey):
            P.wait("scalar", psf[skey][0], psf[skey][1])
        vg0 = P.op("scalar", lambda e, of_=of, pm_=pmg, s_=scr_g:
                   e.activation(out=s_[:], in_=pm_[:, 0:512], func=AF.Gelu,
                                bias=u1b[:, of_:of_ + 1]), inc="ac")
        vg1 = P.op("scalar", lambda e, of_=of, pm_=pmg:
                   e.activation(out=g["SCR2"][:], in_=pm_[:, 512:1024], func=AF.Gelu,
                                bias=u1b[:, of_:of_ + 1]), inc="ac")
        psf[keyg] = ("ac", vg1)
        gst = (g["GSTG0"], g["GSTG1"])[of % 2]
        gkey = "gst" + str(of % 2)
        for hb in range(2):
            if psf.get(keyu):
                P.wait("tensor", psf[keyu][0], psf[keyu][1])
            P.wait("tensor", pool2.sems[b2], v2)
            for k in range(DCH):
                P.op("tensor", lambda e, k_=k, hb_=hb, pm_=pmu, b_=b2:
                     e.matmul(pm_[:], g[b_][:, k_ * 128:(k_ + 1) * 128],
                              g["XN2"][:, k_ * TLOC + hb_ * 512:
                                       k_ * TLOC + hb_ * 512 + 512],
                              start=(k_ == 0), stop=(k_ == DCH - 1)),
                     inc="mm" if k == DCH - 1 else None)
            P.wait("vector", "mm")
            P.wait("vector", "ac", (vg0, vg1)[hb])
            if hb == 0 and psf.get(gkey):
                P.wait("vector", psf[gkey][0], psf[gkey][1])
            vgl = P.op("vector", lambda e, of_=of, hb_=hb, pm_=pmu, gs_=gst,
                       s_=(scr_g, g["SCR2"])[hb]:
                       e.scalar_tensor_tensor(out=gs_[:, hb_ * 512:(hb_ + 1) * 512],
                                              in0=pm_[:], scalar=u2b[:, of_:of_ + 1],
                                              in1=s_[:], op0=ALU.add, op1=ALU.mult),
                       inc="dv")
            psf[keyu] = ("dv", vgl)
            psf["scr5" + str(of % 2) if hb == 0 else skey] = ("dv", vgl)
        pool2.consumed(b2, "mm", P.val("mm"))
        P.wait("scalar", "dv", vgl)
        gq = ("q7", "qo2")[of % 2]
        vgw = dma(P, "scalar", gq, g["d_glu"][of, :, :], gst[:, 0:TLOC])
        psf[gkey] = (gq, vgw)
    g["_glu_q7"] = P.val("q7")
    g["_glu_qo2"] = P.val("qo2")

    slots = [(g["PMA"], 0), (g["PMA"], 512), (g["PMB"], 0), (g["PMB"], 512),
             (g["PMC"], 0), (g["PMC"], 512), (g["PMD"], 0), (g["PMD"], 512)]
    P.wait("sync", "q7", g["_glu_q7"])
    P.wait("sync", "qo2", g["_glu_qo2"])
    P.wait("sync", "dv")
    P.wait("gpsimd", "dv")
    atn_q = 0
    for hb in range(2):
        for grp in range(2):
            if psf.get("down"):
                P.wait("tensor", psf["down"][0], psf["down"][1])
            def _ld_gb(kf_, hb_):
                gb_ = ("GB0", "GB1")[kf_ % 2]
                gq_ = ("q4", "q5")[kf_ % 2]
                gkey_ = "gb" + str(kf_ % 2)
                if psf.get(gkey_):
                    P.wait("sync", psf[gkey_][0], psf[gkey_][1])
                    psf[gkey_] = None
                v_ = dma(P, "sync", gq_, g[gb_][:, 0:512],
                         g["d_glu"][kf_, :, hb_ * 512:(hb_ + 1) * 512])
                wd_ = ("WD0", "WD1")[kf_ % 2]
                wq_ = ("d0", "d1")[kf_ % 2]
                wkey_ = "wd" + str(kf_ % 2)
                if psf.get(wkey_):
                    P.wait("gpsimd", psf[wkey_][0], psf[wkey_][1])
                    psf[wkey_] = None
                v2_ = dma(P, "gpsimd", wq_, g[wd_][:, 0:1024],
                          g["a_down_W_r"][kf_, :, grp * 1024:(grp + 1) * 1024])
                return (gb_, gq_, v_, wd_, wq_, v2_)
            pend_d = [_ld_gb(0, hb)]
            for kf in range(FCH):
                if kf + 1 < FCH:
                    pend_d.append(_ld_gb(kf + 1, hb))
                gb, gq, vgb, wd, wq, vwd = pend_d.pop(0)
                gkey = "gb" + str(kf % 2)
                wkey = "wd" + str(kf % 2)
                P.wait("tensor", gq, vgb)
                P.wait("tensor", wq, vwd)
                for j in range(8):
                    P.op("tensor", lambda e, j_=j, kf_=kf, gb_=gb, wd_=wd:
                         e.matmul(slots[j_][0][:, slots[j_][1]:slots[j_][1] + 512],
                                  g[wd_][:, j_ * 128:(j_ + 1) * 128],
                                  g[gb_][:, 0:512],
                                  start=(kf_ == 0), stop=(kf_ == FCH - 1)),
                         inc="mm" if j == 7 else None)
                mmv = P.val("mm")
                psf[gkey] = ("mm", mmv)
                psf[wkey] = ("mm", mmv)
            P.wait("vector", "mm")
            for j in range(8):
                oh = grp * 8 + j
                dst = (g["DSTG0"], g["DSTG1"])[j % 2]
                dkey = "dstg" + str(j % 2)
                if psf.get(dkey):
                    P.wait("vector", psf[dkey][0], psf[dkey][1])
                P.op("vector", lambda e, j_=j, oh_=oh, d_=dst, hb_=hb:
                     e.scalar_tensor_tensor(
                         out=d_[:, 0:512],
                         in0=slots[j_][0][:, slots[j_][1]:slots[j_][1] + 512],
                         scalar=dnb[:, oh_:oh_ + 1],
                         in1=_x1c(g, oh_)[:, hb_ * 512:(hb_ + 1) * 512],
                         op0=ALU.add, op1=ALU.add), inc="dv")
                vdv = P.val("dv")
                P.wait("scalar", "dv", vdv)
                dq = ("q6", "qo3")[j % 2]
                vq = dma(P, "scalar", dq, g["d_atn_park"][:, oh * TLOC + hb * 512:
                                                          oh * TLOC + hb * 512 + 512],
                         dst[:, 0:512])
                psf[dkey] = (dq, vq)
            psf["down"] = ("dv", P.val("dv"))
    g["_atn_q6"] = P.val("q6")
    g["_atn_qo3"] = P.val("qo3")


def _p6(nc, P, g, tap):
    psf = g["_psf"]
    P.wait("tensor", "dv")
    P.wait("tensor", "qc")
    _bcast_row(nc, P, g, g["f_w_router"], g["WRBC"], D)

    P.wait("sync", "q6", g["_atn_q6"])
    P.wait("sync", "qo3", g["_atn_qo3"])
    def atnt(tt):
        if tt < 4:
            return g["ATNTa"][:, tt * D:(tt + 1) * D]
        return g["ATNTb"][:, (tt - 4) * D:(tt - 3) * D]
    for oh in range(DCH):
        half = oh % 2
        qsem = ("q4", "q5")[half]
        key = "ath" + str(half)
        if psf.get(key):
            P.wait("sync", psf[key][0], psf[key][1])
        abuf = ("ATHF0", "ATHF1")[half]
        vld = dma(P, "sync", qsem, g[abuf][:],
                  g["d_atn_park"][:, oh * TLOC:(oh + 1) * TLOC])
        P.wait("tensor", qsem, vld)
        for tt in range(8):
            if psf.get("pxh"):
                P.wait("tensor", psf["pxh"][0], psf["pxh"][1])
            P.op("tensor", lambda e, tt_=tt, a_=abuf:
                 e.transpose(g["PX"][:], g[a_][:, tt_ * 128:(tt_ + 1) * 128],
                             g["cEYEf"][:]), inc="mm")
            P.wait("vector", "mm")
            vcp = P.op("vector", lambda e, oh_=oh, tt_=tt:
                       e.tensor_copy(out=atnt(tt_)[:, oh_ * 128:(oh_ + 1) * 128],
                                     in_=g["PX"][:]), inc="dv")
            psf["pxh"] = ("dv", vcp)
        psf[key] = ("dv", P.val("dv"))
    g["_atnt_dv"] = P.val("dv")

    x = g["x"]
    P.wait("scalar", "q6", g["_atn_q6"])
    P.wait("scalar", "qo3", g["_atn_qo3"])
    P.wait("vector", "q6", g["_atn_q6"])
    P.wait("vector", "qo3", g["_atn_qo3"])
    P.op("vector", lambda e: e.memset(g["EXT6"][:], 0.0), inc="dv")
    P.op("vector", lambda e: e.memset(g["EXT6b"][:], 0.0), inc="dv")
    xa_q, ex_q, ob_q, free_v = [0, 0], [0, 0], [0, 0], [0, 0]
    for tt in range(8):
        b, q = tt // 4, tt % 4
        par = tt % 2
        XT = (g["XTOK6"], g["XTOK6b"])[par][:]
        SQ = (g["SQ6"], g["SQ6b"])[par][:]
        HD = (g["HID6"], g["HID6b"])[par][:]
        TH = (g["TOKH"], g["TOKH6b"])[par]
        EX = (g["EXT6"], g["EXT6b"])[par][:]
        q8s = ("q8", "qx")[par]
        q6s = ("q6", "q7")[par]
        qo1s = ("qo1", "qo2")[par]
        qas = ("qa", "qb")[par]
        if free_v[par]:
            P.wait("sync", "dv", free_v[par])
        vld = dma(P, "sync", q8s, XT,
                  x[b, HALO + q * 128: HALO + (q + 1) * 128, :])
        P.wait("vector", q8s, vld)
        P.wait("scalar", q8s, vld)
        if tt == 0:
            P.wait("vector", "qtap")
        P.op("vector", lambda e, tt_=tt, x_=XT: e.tensor_add(
            x_, x_, atnt(tt_)), inc="dv")
        if xa_q[par]:
            P.wait("vector", q6s, xa_q[par])
        P.op("vector", lambda e, x_=XT, t_=TH: e.tensor_copy(out=t_[:, 0:D],
                                                             in_=x_), inc="dv")
        vpk = P.val("dv")
        P.wait("scalar", "dv", vpk)
        dma(P, "scalar", q6s, g["d_xa_park"][tt * 128:(tt + 1) * 128, :],
            TH[:, 0:D])
        xa_q[par] = P.val(q6s)
        if ob_q[par]:
            P.wait("scalar", qo1s, ob_q[par])
            P.wait("vector", qo1s, ob_q[par])
        _ln_tile(nc, P, g, XT, 0, 128, EPS_LN, None, sq=SQ, out=HD, col=par)
        P.op("vector", lambda e, s_=SQ, h_=HD: e.tensor_mul(s_, h_, g["WRBC"][:]),
             inc="dv")
        P.op("vector", lambda e, tt_=tt, s_=SQ: e.reduce_sum(
            g["SCO6"][:, tt_:tt_ + 1], s_, axis=AXX), inc="dv")
        P.op("vector", lambda e, s_=SQ, x_=XT, h_=HD: e.tensor_add(s_, x_, h_),
             inc="dv")
        vob = P.val("dv")
        free_v[par] = vob
        P.wait("scalar", "dv", vob)
        dma(P, "scalar", qo1s, g["out_base"][tt * 128:(tt + 1) * 128, :], SQ)
        ob_q[par] = P.val(qo1s)
        if ex_q[par]:
            P.wait("vector", qas, ex_q[par])
        for col, srcap in [(0, g["ST1"][:, par:par + 1]),
                           (1, g["ST2"][:, par:par + 1]),
                           (2, g["cTAU"]), (3, g["SCO6"]), (4, g["cROW"])]:
            if col in (2, 4):
                P.op("vector", lambda e, c_=col, s_=srcap, tt_=tt, e_=EX:
                     e.tensor_copy(out=e_[:, c_:c_ + 1], in_=s_[:, tt_:tt_ + 1]),
                     inc="dv")
            elif col == 3:
                P.op("vector", lambda e, c_=col, tt_=tt, e_=EX:
                     e.tensor_copy(out=e_[:, c_:c_ + 1],
                                   in_=g["SCO6"][:, tt_:tt_ + 1]), inc="dv")
            else:
                P.op("vector", lambda e, c_=col, s_=srcap, e_=EX:
                     e.tensor_copy(out=e_[:, c_:c_ + 1], in_=s_), inc="dv")
        vex = P.val("dv")
        P.wait("sync", "dv", vex)
        dma(P, "sync", qas, g["d_ex_park"][tt * 128:(tt + 1) * 128, :], EX)
        ex_q[par] = P.val(qas)
    g["_xa_q6v"] = P.val("q6")
    g["_xa_q7v"] = P.val("q7")
    psf2 = g["_psf"]
    if psf2.get("pxh"):
        P.wait("tensor", psf2["pxh"][0], psf2["pxh"][1])
    P.wait("tensor", "dv")
    P.op("tensor", lambda e: e.transpose(g["PX"][0:8, 0:128], g["SCO6"][:],
                                         g["cEYEf"][:]), inc="mm")
    P.wait("vector", "mm")
    P.op("vector", lambda e: e.tensor_copy(out=g["ST2"][0:8, 0:16] if False else
                                           g["SCO8R"][0:8, :], in_=g["PX"][0:8, 0:128]),
         inc="dv")
    P.wait("sync", "dv")
    vsc = dma(P, "sync", "qb", g["d_scin"].rearrange("(a b) o -> a (b o)", a=8),
              g["SCO8R"][0:8, :])
    P.wait("gpsimd", "qb", P.val("qb"))
    P.wait("gpsimd", "qa", P.val("qa"))
    P.wait("gpsimd", "qo1")
    P.wait("gpsimd", "qo2")
    P.op("gpsimd", lambda e: e.collective_compute(
        "AllGather", ALU.bypass, replica_groups=[list(range(NC))],
        ins=[g["d_scin"].tensor.ap().opt()], outs=[g["d_scag"].tensor.ap().opt()]),
        inc="cc")
    g["_p6_cc"] = P.val("cc")
    if tap("hidden", [128, D]):
        P.wait("gpsimd", "dv")
        dma(P, "gpsimd", "qtap", g["tap_hidden"], g["HID6"][:])
    if tap("xapark", [TLOC, D], BF16):
        P.wait("gpsimd", "q6", g["_xa_q6v"])
        P.wait("gpsimd", "q7", g["_xa_q7v"])
        dma(P, "gpsimd", "qtap", g["tap_xapark"], g["d_xa_park"])
    if tap("expark", [TLOC, 8]):
        P.wait("gpsimd", "qa")
        P.wait("gpsimd", "qb")
        dma(P, "gpsimd", "qtap", g["tap_expark"], g["d_ex_park"])
    if tap("scag", [NC * TLOC, 1]):
        P.wait("gpsimd", "cc", g["_p6_cc"])
        dma(P, "gpsimd", "qtap", g["tap_scag"], g["d_scag"])


def _p7(nc, P, g, tap):
    psf = g["_psf"]
    P.wait("sync", "cc", g["_p6_cc"])
    for b in range(BB):
        dma(P, "sync", "qd", g["SC7"][:, b * 32:(b + 1) * 32],
            g["d_scag"].rearrange("(po pi f o) -> (po pi) f o", po=8, pi=16, f=32)[
                :, :, 0] if False else
            _scag_ap(g, b))
    vsc7 = P.val("qd")
    P.op("vector", lambda e: e.memset(g["BIG7"][:], 4000.0), inc="dv")
    P.op("vector", lambda e: e.memset(g["ZERO7"][:], 0.0), inc="dv")
    lo, hi, mid = g["ST1"][0:1, 0:1], g["ST1"][0:1, 1:2], g["ST1"][0:1, 2:3]
    cnt, prd, tt7 = g["ST1"][0:1, 3:4], g["ST1"][0:1, 4:5], g["ST1"][0:1, 5:6]
    cgt, need = g["ST1"][0:1, 6:7], g["ST1"][0:1, 7:8]
    row128 = g["ROW1"][:]
    P.wait("vector", "qd", vsc7)

    def mmt_colrow(col_ap, out_row):
        P.wait("tensor", "dv")
        P.op("tensor", lambda e: e.matmul(g["PS1"][0:1, 0:128], col_ap,
                                          g["cEYEf"][:], start=True, stop=True),
             inc="mm")
        P.wait("vector", "mm")
        P.op("vector", lambda e: e.tensor_copy(out=out_row, in_=g["PS1"][0:1, 0:128]),
             inc="dv")

    def bcast_col(scalar_ap, out_col):
        P.wait("tensor", "dv")
        P.op("tensor", lambda e: e.matmul(g["PX"][:, 0:1], g["cONEr"][:], scalar_ap,
                                          start=True, stop=True), inc="mm")
        P.wait("vector", "mm")
        P.op("vector", lambda e: e.tensor_copy(out=out_col, in_=g["PX"][:, 0:1]),
             inc="dv")

    def reduce_total(col_ap, out_sc):
        P.wait("tensor", "dv")
        P.op("tensor", lambda e: e.matmul(g["PS1"][0:1, 128:129], col_ap,
                                          g["cONEc"][:], start=True, stop=True),
             inc="mm")
        P.wait("vector", "mm")
        P.op("vector", lambda e: e.tensor_copy(out=out_sc,
                                               in_=g["PS1"][0:1, 128:129]), inc="dv")

    def st(b, c):
        return (g["ST1"], g["ST2"])[b][0:1, c:c + 1]
    scb_l = [g["SC7"][:, 0:32], g["SC7"][:, 32:64]]
    cnt_l = [g["CNTC7"][:], g["MG7"][:, 0:1]]
    midb_l = [g["MIDB7"][:], g["MEQ7"][:, 0:1]]
    dummy_l = [g["T7a"][:], g["MASK7"][:]]
    px_l = [g["PX"][:, 0:1], g["PX"][:, 1:2]]
    ps_l = [g["PS1"][0:1, 128:129], g["PS1"][0:1, 129:130]]
    for b in range(BB):
        scb = scb_l[b]
        lo_b, hi_b = st(b, 0), st(b, 1)
        P.op("vector", lambda e, s_=scb: e.reduce_max(g["CNTC7"][:], s_, axis=AXX), inc="dv")
        mmt_colrow(g["CNTC7"][:], row128)
        P.op("vector", lambda e, h_=hi_b: e.reduce_max(h_, row128, axis=AXX), inc="dv")
        P.op("vector", lambda e, s_=scb: e.tensor_scalar_mul(g["T7a"][:], s_, -1.0), inc="dv")
        P.op("vector", lambda e: e.reduce_max(g["CNTC7"][:], g["T7a"][:], axis=AXX),
             inc="dv")
        mmt_colrow(g["CNTC7"][:], row128)
        P.op("vector", lambda e, l_=lo_b: e.reduce_max(l_, row128, axis=AXX), inc="dv")
        P.op("vector", lambda e, l_=lo_b: e.tensor_scalar_mul(l_, l_, -1.0), inc="dv")
    for it in range(BISECT_ITERS):
        mids, gts = [0, 0], [0, 0]
        for b in range(BB):
            P.op("vector", lambda e, h_=st(b, 1), l_=st(b, 0), t_=st(b, 5):
                 e.tensor_sub(t_, h_, l_), inc="dv")
            mids[b] = P.op("vector", lambda e, m_=st(b, 2), t_=st(b, 5), l_=st(b, 0):
                           e.scalar_tensor_tensor(out=m_, in0=t_, scalar=0.5, in1=l_,
                                                  op0=ALU.mult, op1=ALU.add), inc="dv")
        for b in range(BB):
            P.wait("tensor", "dv", mids[b])
            P.op("tensor", lambda e, m_=st(b, 2), p_=px_l[b]:
                 e.matmul(p_, g["cONEr"][:], m_, start=True, stop=True), inc="mm")
            P.wait("vector", "mm")
            P.op("vector", lambda e, p_=px_l[b], o_=midb_l[b]:
                 e.tensor_copy(out=o_, in_=p_), inc="dv")
            gts[b] = P.op("vector", lambda e, s_=scb_l[b], d_=dummy_l[b],
                          m_=midb_l[b], c_=cnt_l[b]:
                          e.tensor_scalar(out=d_, in0=s_, scalar1=m_, scalar2=0.0,
                                          op0=ALU.is_gt, op1=ALU.add, accum_out=c_),
                          inc="dv")
        for b in range(BB):
            P.wait("tensor", "dv", gts[b])
            P.op("tensor", lambda e, c_=cnt_l[b], p_=ps_l[b]:
                 e.matmul(p_, c_, g["cONEc"][:], start=True, stop=True), inc="mm")
            P.wait("vector", "mm")
            P.op("vector", lambda e, p_=ps_l[b], o_=st(b, 3):
                 e.tensor_copy(out=o_, in_=p_), inc="dv")
            P.op("vector", lambda e, c_=st(b, 3), o_=st(b, 4):
                 e.tensor_scalar(out=o_, in0=c_, scalar1=float(KSEL) - 0.5,
                                 scalar2=None, op0=ALU.is_gt), inc="dv")
            P.op("vector", lambda e, m_=st(b, 2), l_=st(b, 0), t_=st(b, 5):
                 e.tensor_sub(t_, m_, l_), inc="dv")
            P.op("vector", lambda e, t_=st(b, 5), p_=st(b, 4), l_=st(b, 0):
                 e.scalar_tensor_tensor(out=l_, in0=t_, scalar=p_, in1=l_,
                                        op0=ALU.mult, op1=ALU.add), inc="dv")
            P.op("vector", lambda e, h_=st(b, 1), m_=st(b, 2), t_=st(b, 5):
                 e.tensor_sub(t_, h_, m_), inc="dv")
            P.op("vector", lambda e, t_=st(b, 5), p_=st(b, 4), m_=st(b, 2),
                 h_=st(b, 1):
                 e.scalar_tensor_tensor(out=h_, in0=t_, scalar=p_, in1=m_,
                                        op0=ALU.mult, op1=ALU.add), inc="dv")
    for b in range(BB):
        scb = scb_l[b]
        hi, cgt, need = st(b, 1), st(b, 6), st(b, 7)
        if b:
            P.wait("vector", "qd")
        bcast_col(hi, g["HIB7"][:])
        P.op("vector", lambda e, s_=scb: e.tensor_scalar(
            out=g["MG7"][:], in0=s_, scalar1=g["HIB7"][:], scalar2=0.0,
            op0=ALU.is_gt, op1=ALU.add, accum_out=g["CNTC7"][:]), inc="dv")
        reduce_total(g["CNTC7"][:], cgt)
        P.op("vector", lambda e, s_=scb: e.tensor_scalar(
            out=g["MEQ7"][:], in0=s_, scalar1=g["HIB7"][:], scalar2=0.0,
            op0=ALU.is_equal, op1=ALU.add, accum_out=g["CNTC7"][:]), inc="dv")
        P.wait("tensor", "dv")
        P.op("tensor", lambda e: e.matmul(g["PX"][:, 0:1], g["cLTRI"][:],
                                          g["CNTC7"][:], start=True, stop=True),
             inc="mm")
        P.wait("vector", "mm")
        P.op("vector", lambda e: e.tensor_copy(out=g["EOFF7"][:], in_=g["PX"][:, 0:1]),
             inc="dv")
        P.op("vector", lambda e: e.tensor_tensor_scan(
            out=g["INCL7"][:], data0=g["MEQ7"][:], data1=g["ZERO7"][:], initial=0.0,
            op0=ALU.add, op1=ALU.add), inc="dv")
        P.op("vector", lambda e: e.tensor_scalar_add(g["INCL7"][:], g["INCL7"][:],
                                                     g["EOFF7"][:]), inc="dv")
        P.op("vector", lambda e, n_=need, c_=cgt: e.tensor_scalar(
            out=n_, in0=c_, scalar1=-1.0, scalar2=float(KSEL), op0=ALU.mult,
            op1=ALU.add), inc="dv")
        bcast_col(need, g["NEEDB7"][:])
        P.op("vector", lambda e: e.tensor_scalar(
            out=g["T7b"][:], in0=g["INCL7"][:], scalar1=g["NEEDB7"][:], scalar2=None,
            op0=ALU.is_le), inc="dv")
        P.op("vector", lambda e: e.tensor_mul(g["T7b"][:], g["T7b"][:], g["MEQ7"][:]),
             inc="dv")
        P.op("vector", lambda e: e.tensor_add(g["MASK7"][:], g["MG7"][:], g["T7b"][:]),
             inc="dv")
        P.op("vector", lambda e: e.reduce_sum(g["CNTC7"][:], g["MASK7"][:], axis=AXX),
             inc="dv")
        P.wait("tensor", "dv")
        P.op("tensor", lambda e: e.matmul(g["PX"][:, 0:1], g["cLTRI"][:],
                                          g["CNTC7"][:], start=True, stop=True),
             inc="mm")
        P.wait("vector", "mm")
        P.op("vector", lambda e: e.tensor_copy(out=g["EOFF7"][:], in_=g["PX"][:, 0:1]),
             inc="dv")
        P.op("vector", lambda e: e.tensor_tensor_scan(
            out=g["INCL7"][:], data0=g["MASK7"][:], data1=g["ZERO7"][:], initial=0.0,
            op0=ALU.add, op1=ALU.add), inc="dv")
        P.op("vector", lambda e: e.tensor_scalar_add(g["INCL7"][:], g["INCL7"][:],
                                                     g["EOFF7"][:]), inc="dv")
        P.op("vector", lambda e: e.tensor_sub(g["CEX7"][:], g["INCL7"][:],
                                              g["MASK7"][:]), inc="dv")
        if b == 1:
            P.op("vector", lambda e: e.tensor_scalar_add(g["CEX7"][:], g["CEX7"][:],
                                                         float(KSEL)), inc="dv")
        P.op("vector", lambda e: e.tensor_mul(g["CEX7"][:], g["CEX7"][:],
                                              g["MASK7"][:]), inc="dv")
        P.op("vector", lambda e: e.tensor_scalar(
            out=g["T7b"][:], in0=g["MASK7"][:], scalar1=-4000.0, scalar2=4000.0,
            op0=ALU.mult, op1=ALU.add), inc="dv")
        P.op("vector", lambda e: e.tensor_add(g["CEX7"][:], g["CEX7"][:],
                                              g["T7b"][:]), inc="dv")
        P.op("vector", lambda e: e.tensor_copy(out=g["IDXI7"][:], in_=g["CEX7"][:]),
             inc="dv")
        P.op("vector", lambda e: e.tensor_copy(
            out=g["STAG7"].rearrange("p (f j) -> p f j", j=4),
            in_=g["IDXI7"][:].unsqueeze(2).to_broadcast([128, 32, 4])), inc="dv")
        vst = P.val("dv")
        P.wait("sync", "dv", vst)
        dma(P, "sync", "qd", _slotrec_ap(g, b), g["STAG7"][:])
        if b == 0 and tap("seldbg", [128, 176]):
            P.wait("gpsimd", "dv", vst)
            dma(P, "gpsimd", "qtap", g["tap_seldbg"][:, 0:64], g["SC7"][:])
            dma(P, "gpsimd", "qtap", g["tap_seldbg"][:, 64:80], g["ST1"][:])
            dma(P, "gpsimd", "qtap", g["tap_seldbg"][:, 80:112], g["MASK7"][:])
            dma(P, "gpsimd", "qtap", g["tap_seldbg"][:, 112:144], g["CEX7"][:])
            dma(P, "gpsimd", "qtap", g["tap_seldbg"][:, 144:176], g["MG7"][:])
            P.wait("vector", "qtap")
    g["_slotrec_qd"] = P.val("qd")

    P.op("vector", lambda e: e.memset(g["P7A"][:], 0.0), inc="dv")
    P.op("vector", lambda e: e.memset(g["P7E"][:], 0.0), inc="dv")
    vz = P.val("dv")
    P.wait("gpsimd", "dv", vz)
    for r in range(NC):
        dma(P, "gpsimd", "qe", g["d_send"][r * 128:(r + 1) * 128, :], g["P7A"][:])
        dma(P, "gpsimd", "qe", g["d_send_ex"][r * 128:(r + 1) * 128, :], g["P7E"][:])
    vzs = P.val("qe")
    P.wait("gpsimd", "qd", g["_slotrec_qd"])
    P.wait("gpsimd", "qe", vzs)
    P.wait("gpsimd", "q6", g["_xa_q6v"])
    P.wait("gpsimd", "q7", g["_xa_q7v"])
    P.wait("gpsimd", "qa")
    P.wait("gpsimd", "qb")
    scat_done = 0
    for t8 in range(8):
        if scat_done:
            P.wait("gpsimd", "qg", scat_done)
        dma(P, "gpsimd", "qe", g["P7A"][:],
            g["d_xa_park"][t8 * 128:(t8 + 1) * 128, :])
        vpe = dma(P, "gpsimd", "qe", g["P7E"][:],
                  g["d_ex_park"][t8 * 128:(t8 + 1) * 128, :])
        vig = P.op("gpsimd", lambda e, t_=t8: e.indirect_dma_start(
            out=g["IDG7"][:], out_offset=None, in_=g["d_slotrec"],
            in_offset=IndirectOffsetOnAxis(ap=g["cGRW"][:, t_:t_ + 1], axis=0)),
            inc="qf", inc_n=16)
        P.wait("gpsimd", "qe", vpe)
        P.wait("gpsimd", "qf", vig)
        P.op("gpsimd", lambda e: e.indirect_dma_start(
            out=g["d_send"], out_offset=IndirectOffsetOnAxis(ap=g["IDG7"][:, 0:1],
                                                             axis=0),
            in_=g["P7A"][:], in_offset=None,
            bounds_check=NSLOT - 1, oob_is_err=False), inc="qg", inc_n=16)
        scat_done = P.op("gpsimd", lambda e: e.indirect_dma_start(
            out=g["d_send_ex"], out_offset=IndirectOffsetOnAxis(ap=g["IDG7"][:, 0:1],
                                                                axis=0),
            in_=g["P7E"][:], in_offset=None,
            bounds_check=NSLOT - 1, oob_is_err=False), inc="qg", inc_n=16)
    P.wait("gpsimd", "qg", scat_done)
    from concourse.aot_env import allow_ici_all_to_all

    def a2a(e):
        with allow_ici_all_to_all():
            return e.collective_compute(
                "AllToAll", ALU.bypass, replica_groups=[list(range(NC))],
                ins=[g["d_send"].tensor.ap().opt()],
                outs=[g["d_recv"].tensor.ap().opt()])
    P.op("gpsimd", a2a, inc="cc")

    def a2a_ex(e):
        with allow_ici_all_to_all():
            return e.collective_compute(
                "AllToAll", ALU.bypass, replica_groups=[list(range(NC))],
                ins=[g["d_send_ex"].tensor.ap().opt()],
                outs=[g["d_recv_ex"].tensor.ap().opt()])
    P.op("gpsimd", a2a_ex, inc="cc")
    g["_p7_cc"] = P.val("cc")
    if tap("slotrec", [BB * SEQ, 4], I32):
        P.wait("gpsimd", "qd", g["_slotrec_qd"])
        dma(P, "gpsimd", "qtap", g["tap_slotrec"], g["d_slotrec"])
    if tap("recv", [NSLOT, PAYW], BF16):
        P.wait("gpsimd", "cc", g["_p7_cc"])
        dma(P, "gpsimd", "qtap", g["tap_recv"], g["d_recv"])


def _mk_ap(base, offset, dims):
    return type(base)(base.tensor, offset, dims)


def _scag_ap(g, b):
    return _mk_ap(g["d_scag"], b * 512, [[1024, 8], [32, 16], [1, 32]])


def _slotrec_ap(g, b):
    return _mk_ap(g["d_slotrec"], b * 16384, [[128, 128], [4, 32], [1, 4]])


def _p8(nc, P, g, tap):
    psf = g["_psf"]
    pool, pool2 = g["_wpool"], g["_wpool2"]
    pxh = g["PMD"][:, 0:64].bitcast(BF16)

    P.wait("gpsimd", "cc", g["_p7_cc"])
    for r in range(NC):
        v = dma(P, "gpsimd", "qe", g["P7A"][:], g["d_recv"][r * 128:(r + 1) * 128, :])
        v2 = dma(P, "gpsimd", "qe", g["P7E"][:],
                 g["d_recv_ex"][r * 128:(r + 1) * 128, :])
        P.wait("vector", "qe", v2)
        if r == 0:
            P.op("vector", lambda e: e.tensor_copy(out=g["XSEL"][:], in_=g["P7A"][:]),
                 inc="dv")
            P.op("vector", lambda e: e.tensor_copy(out=g["EXF"][:], in_=g["P7E"][:]),
                 inc="dv")
        else:
            P.op("vector", lambda e: e.tensor_add(g["XSEL"][:], g["XSEL"][:],
                                                  g["P7A"][:]), inc="dv")
            P.op("vector", lambda e: e.tensor_add(g["EXF"][:], g["EXF"][:],
                                                  g["P7E"][:]), inc="dv")
        P.wait("gpsimd", "dv")
    P.op("vector", lambda e: e.tensor_scalar(
        out=g["HID"][:], in0=g["XSEL"][:, 0:D], scalar1=g["EXF"][:, 0:1],
        scalar2=g["EXF"][:, 1:2], op0=ALU.subtract, op1=ALU.mult), inc="dv")
    P.wait("scalar", "dv")
    P.op("scalar", lambda e: e.activation(out=g["ATTV"][:], in_=g["HID"][:],
                                          func=AF.Square, accum_out=g["ST1"][:, 0:1]),
         inc="ac")
    P.wait("vector", "ac")
    P.op("vector", lambda e: e.tensor_scalar_mul(g["ST1"][:, 0:1], g["ST1"][:, 0:1],
                                                 1.0 / D), inc="dv")
    P.wait("scalar", "dv")
    P.op("scalar", lambda e: e.activation(out=g["ST1"][:, 0:1], in_=g["ST1"][:, 0:1],
                                          func=AF.Ln, bias=_cst(g, EPS_RMS_B)),
         inc="ac")
    P.op("scalar", lambda e: e.activation(out=g["ST1"][:, 0:1], in_=g["ST1"][:, 0:1],
                                          func=AF.Exp, scale=-0.5), inc="ac")
    P.wait("vector", "ac")
    P.op("vector", lambda e: e.tensor_scalar_mul(g["P7A"][:, 0:D], g["HID"][:],
                                                 g["ST1"][:, 0:1]), inc="dv")
    if _P8STOP < 1:
        return
    l1w = _bias_ap(g, "f_ln1_w")
    for ch in range(DCH):
        if psf.get("pxh"):
            P.wait("tensor", psf["pxh"][0], psf["pxh"][1])
        P.wait("tensor", "dv")
        P.op("tensor", lambda e, ch_=ch:
             e.transpose(pxh, g["P7A"][:, ch_ * 128:(ch_ + 1) * 128], g["cEYEh"][:]),
             inc="mm")
        P.wait("vector", "mm")
        vcp = P.op("vector", lambda e, ch_=ch:
                   e.tensor_scalar_mul(g["QKVf"][:, ch_ * 128:(ch_ + 1) * 128],
                                       pxh, l1w[:, ch_:ch_ + 1]), inc="dv")
        psf["pxh"] = ("dv", vcp)

    if _P8STOP < 2:
        return
    bq = _bias_ap(g, "f_bq")
    bk = _bias_ap(g, "f_bk")
    bv = _bias_ap(g, "f_bv")
    P.wait("tensor", "dv")
    qkv_seq = [("f_wq", oh, "QR", bq) for oh in range(DCH)] + \
              [("f_wk", oh, "KR", bk) for oh in range(KVCH)] + \
              [("f_wv", oh, "VTt", bv) for oh in range(KVCH)]
    pool.prefetch(g["f_wq"][0, :, :], DCH * 128)
    for qi, (wname, oh, dst, bias) in enumerate(qkv_seq):
        if True:
            if qi + 1 < len(qkv_seq):
                nw, noh = qkv_seq[qi + 1][0], qkv_seq[qi + 1][1]
                pool.prefetch(g[nw][noh, :, :], DCH * 128)
            bn, lv = pool.take()
            pm = (g["PMA"][:, 0:128], g["PMA"][:, 512:640])[oh % 2]
            key = "p8q" + str(oh % 2)
            if psf.get(key):
                P.wait("tensor", psf[key][0], psf[key][1])
            P.wait("tensor", pool.sems[bn], lv)
            for k in range(DCH):
                P.op("tensor", lambda e, k_=k, pm_=pm, b_=bn:
                     e.matmul(pm_, g[b_][:, k_ * 128:(k_ + 1) * 128],
                              g["QKVf"][:, k_ * 128:(k_ + 1) * 128],
                              start=(k_ == 0), stop=(k_ == DCH - 1)),
                     inc="mm" if k == DCH - 1 else None)
            pool.consumed(bn, "mm", P.val("mm"))
            P.wait("vector", "mm")
            vev = P.op("vector", lambda e, oh_=oh, pm_=pm, d_=dst, bi_=bias:
                       e.tensor_scalar_add(g[d_][:, oh_ * 128:(oh_ + 1) * 128], pm_,
                                           bi_[:, oh_:oh_ + 1]), inc="dv")
            psf[key] = ("dv", vev)

    if _P8STOP < 3:
        return
    P.wait("tensor", "dv")
    P.op("tensor", lambda e: e.matmul(g["PS1"][0:1, 0:128], g["EXF"][:, 2:3],
                                      g["cEYEf"][:], start=True, stop=True), inc="mm")
    P.wait("vector", "mm")
    P.op("vector", lambda e: e.tensor_copy(out=g["ROW1"][:],
                                           in_=g["PS1"][0:1, 0:128]), inc="dv")
    P.wait("tensor", "dv")
    P.op("tensor", lambda e: e.matmul(g["PX"][:], g["cINVF"][:], g["ROW1"][:],
                                      start=True, stop=True), inc="mm")
    P.wait("vector", "mm")
    P.op("vector", lambda e: e.tensor_copy(out=g["ANG"][:], in_=g["PX"][:]), inc="dv")
    ti = g["T8"].bitcast(I32)
    TWO_PI = float(2 * np.pi)
    for (dst, shift) in [("SINt", 0.0), ("COSt", float(np.pi / 2))]:
        P.op("vector", lambda e, s_=shift: e.tensor_scalar(
            out=g["T8b"][:], in0=g["ANG"][:], scalar1=s_,
            scalar2=float(1.0 / TWO_PI), op0=ALU.add, op1=ALU.mult), inc="dv")
        P.op("vector", lambda e: e.tensor_copy(out=ti[:, 0:128], in_=g["T8b"][:]),
             inc="dv")
        P.op("vector", lambda e: e.tensor_copy(out=g["T8b"][:], in_=ti[:, 0:128]),
             inc="dv")
        P.op("vector", lambda e, d_=dst, s_=shift: e.tensor_scalar_add(
            g[d_][:], g["ANG"][:], s_), inc="dv")
        P.op("vector", lambda e, d_=dst: e.scalar_tensor_tensor(
            out=g[d_][:], in0=g["T8b"][:], scalar=-TWO_PI, in1=g[d_][:],
            op0=ALU.mult, op1=ALU.add), inc="dv")
        P.op("vector", lambda e, d_=dst: e.tensor_scalar(
            out=g["T8b"][:], in0=g[d_][:], scalar1=float(np.pi), scalar2=None,
            op0=ALU.is_gt), inc="dv")
        P.op("vector", lambda e, d_=dst: e.tensor_scalar(
            out=ti[:, 0:128].bitcast(F32), in0=g[d_][:], scalar1=float(-np.pi),
            scalar2=None, op0=ALU.is_lt), inc="dv")
        P.op("vector", lambda e: e.tensor_sub(g["T8b"][:], ti[:, 0:128].bitcast(F32),
                                              g["T8b"][:]), inc="dv")
        P.op("vector", lambda e, d_=dst: e.scalar_tensor_tensor(
            out=g[d_][:], in0=g["T8b"][:], scalar=TWO_PI, in1=g[d_][:],
            op0=ALU.mult, op1=ALU.add), inc="dv")
        P.wait("scalar", "dv")
        P.op("scalar", lambda e, d_=dst: e.activation(out=g[d_][:], in_=g[d_][:],
                                                      func=AF.Sin), inc="ac")
    P.wait("vector", "ac")
    for (buf, nh, scale) in [("QR", NH, INV_SQRT_HD), ("KR", NKV, 1.0)]:
        for h in range(nh):
            hap = g[buf][:, h * 128:(h + 1) * 128]
            P.wait("tensor", "dv")
            P.op("tensor", lambda e, a_=hap: e.matmul(
                g["PX"][:], g["cROT"][:], a_, start=True, stop=True), inc="mm")
            P.wait("vector", "mm")
            P.op("vector", lambda e: e.tensor_mul(g["T8"][:], g["PX"][:], g["SINt"][:]),
                 inc="dv")
            P.op("vector", lambda e, a_=hap: e.tensor_mul(g["T8b"][:], a_,
                                                          g["COSt"][:]), inc="dv")
            P.op("vector", lambda e: e.tensor_add(g["T8"][:], g["T8"][:], g["T8b"][:]),
                 inc="dv")
            P.op("vector", lambda e, a_=hap, s_=scale: e.tensor_scalar_mul(
                a_, g["T8"][:], s_), inc="dv")

    if _P8STOP < 4:
        return
    vkw = P.val("dv")
    P.wait("scalar", "dv", vkw)
    dma(P, "scalar", "q7", _kvin_k_ap(g), g["KR"][:])
    P.wait("tensor", "dv")
    for kv in range(NKV):
        if psf.get("pxh"):
            P.wait("tensor", psf["pxh"][0], psf["pxh"][1])
        P.op("tensor", lambda e, k_=kv: e.transpose(
            pxh, g["VTt"][:, k_ * 128:(k_ + 1) * 128], g["cEYEh"][:]), inc="mm")
        P.wait("vector", "mm")
        vcp = P.op("vector", lambda e, k_=kv: e.tensor_copy(
            out=g["P7A"][:, k_ * 128:(k_ + 1) * 128], in_=pxh), inc="dv")
        psf["pxh"] = ("dv", vcp)
    P.wait("scalar", "dv")
    dma(P, "scalar", "q7", _kvin_v_ap(g), g["P7A"][:, 0:NKV * 128])
    P.wait("gpsimd", "q7")
    P.op("gpsimd", lambda e: e.collective_compute(
        "AllGather", ALU.bypass, replica_groups=[list(range(NC))],
        ins=[g["d_kvin"].tensor.ap().opt()], outs=[g["d_kvout"].tensor.ap().opt()]),
        inc="cc")
    vkv_cc = P.val("cc")

    P.op("gpsimd", lambda e: e.iota(g["ATTS"][:], pattern=[[1, 1024]], base=0,
                                    channel_multiplier=0,
                                    allow_small_or_imprecise_dtypes=True), inc="gp")
    P.op("gpsimd", lambda e: e.iota(g["KS8"][:], pattern=[[128, 8]], base=0,
                                    channel_multiplier=1,
                                    allow_small_or_imprecise_dtypes=True), inc="gp")
    P.wait("vector", "gp")
    P.op("vector", lambda e: e.tensor_scalar(
        out=g["MASKM"][:], in0=g["ATTS"][:], scalar1=g["cSLOT"][:], scalar2=None,
        op0=ALU.is_le), inc="dv")
    P.op("vector", lambda e: e.tensor_scalar(
        out=g["NEGT"][:], in0=g["MASKM"][:], scalar1=1.0e30, scalar2=-1.0e30,
        op0=ALU.mult, op1=ALU.add), inc="dv")
    P.wait("tensor", "dv")
    P.op("tensor", lambda e: e.matmul(g["PS1"][0:1, 0:128], g["cSLOT"][:],
                                      g["cEYEf"][:], start=True, stop=True), inc="mm")
    P.wait("vector", "mm")
    P.op("vector", lambda e: e.tensor_copy(out=g["ROW1"][:],
                                           in_=g["PS1"][0:1, 0:128]), inc="dv")
    P.wait("tensor", "dv")
    P.op("tensor", lambda e: e.matmul(g["PX"][:], g["cONEr"][:], g["ROW1"][:],
                                      start=True, stop=True), inc="mm")
    P.wait("vector", "mm")
    P.op("vector", lambda e: e.tensor_copy(out=g["QROW8"][:], in_=g["PX"][:]),
         inc="dv")
    for ck in range(NC):
        P.op("vector", lambda e, c_=ck: e.tensor_scalar(
            out=g["MTCH"][:, c_ * 128:(c_ + 1) * 128], in0=g["QROW8"][:],
            scalar1=g["KS8"][:, c_:c_ + 1], scalar2=None, op0=ALU.is_ge), inc="dv")

    if _P8STOP < 5:
        return
    P.wait("sync", "cc", vkv_cc)
    wo_pool = pool
    for h in range(NH):
        kvh = h // 2
        if psf.get("att"):
            P.wait("sync", psf["att"][0], psf["att"][1])
        vk = dma(P, "sync", ("q4", "q5")[h % 2], g["KBH"][:], _kvout_k_ap(g, kvh))
        vv = dma(P, "sync", ("q4", "q5")[h % 2], g["VBH"][:], _kvout_v_ap(g, kvh))
        qh = g["QR"][:, h * 128:(h + 1) * 128]
        if psf.get("att"):
            P.wait("tensor", psf["att"][0], psf["att"][1])
        P.wait("tensor", ("q4", "q5")[h % 2], vv)
        for hf in range(2):
            P.op("tensor", lambda e, q_=qh, f_=hf: e.matmul(
                g["PMA"][:, f_ * 512:(f_ + 1) * 512], q_,
                g["KBH"][:, f_ * 512:(f_ + 1) * 512], start=True, stop=True),
                inc="mm" if hf == 1 else None)
        P.wait("vector", "mm")
        P.op("vector", lambda e: e.tensor_mul(g["ATTS"][:], g["PMA"][:],
                                              g["MASKM"][:]), inc="dv")
        P.op("vector", lambda e: e.tensor_add(g["ATTS"][:], g["ATTS"][:],
                                              g["NEGT"][:]), inc="dv")
        vats = P.val("dv")
        psf["att"] = ("dv", vats)
        P.op("vector", lambda e: e.reduce_max(g["ST1"][:, 0:1], g["ATTS"][:],
                                              axis=AXX), inc="dv")
        P.wait("tensor", "dv")
        P.op("tensor", lambda e: e.matmul(g["PS1"][0:1, 0:128], g["ST1"][:, 0:1],
                                          g["cEYEf"][:], start=True, stop=True),
             inc="mm")
        P.wait("vector", "mm")
        P.op("vector", lambda e: e.tensor_copy(out=g["ROW1"][:],
                                               in_=g["PS1"][0:1, 0:128]), inc="dv")
        P.op("vector", lambda e: e.reduce_max(g["ST1"][0:1, 1:2], g["ROW1"][:],
                                              axis=AXX), inc="dv")
        P.wait("tensor", "dv")
        P.op("tensor", lambda e: e.matmul(g["PX"][:, 0:1], g["cONEr"][:],
                                          g["ST1"][0:1, 1:2], start=True, stop=True),
             inc="mm")
        P.wait("vector", "mm")
        P.op("vector", lambda e: e.tensor_scalar_mul(g["MHB8"][:], g["PX"][:, 0:1],
                                                     -1.0), inc="dv")
        vmh = P.val("dv")
        P.wait("scalar", "dv", vmh)
        P.op("scalar", lambda e, h_=h: e.activation(
            out=g["ATTS"][:], in_=g["ATTS"][:], func=AF.Exp, bias=g["MHB8"][:],
            accum_out=g["DEN8"][:, h_:h_ + 1]), inc="ac")
        P.wait("vector", "ac")
        P.op("vector", lambda e, h_=h: e.reciprocal(g["REC8"][:, h_:h_ + 1],
                                                    g["DEN8"][:, h_:h_ + 1]), inc="dv")
        for ck in range(NC):
            P.op("tensor", lambda e, c_=ck, q_=qh: e.matmul(
                g["PMB"][:, c_ * 128:(c_ + 1) * 128],
                g["KBH"][:, c_ * 128:(c_ + 1) * 128], q_, start=True, stop=True),
                inc="mm" if ck == NC - 1 else None)
        P.wait("scalar", "mm")
        for ck in range(NC):
            P.op("scalar", lambda e, c_=ck: e.activation(
                out=g["EHS"][:, c_ * 128:(c_ + 1) * 128],
                in_=g["PMB"][:, c_ * 128:(c_ + 1) * 128], func=AF.Exp,
                bias=g["MHB8"][:]), inc="ac")
        P.wait("vector", "ac")
        P.op("vector", lambda e: e.tensor_mul(g["ETS"][:], g["EHS"][:], g["MTCH"][:]),
             inc="dv")
        vets = P.val("dv")
        P.wait("tensor", "dv")
        P.op("tensor", lambda e, h_=h: e.matmul(g["PS1"][0:1, 0:128],
                                                g["REC8"][:, h_:h_ + 1], g["cEYEf"][:],
                                                start=True, stop=True), inc="mm")
        P.wait("vector", "mm")
        P.op("vector", lambda e: e.tensor_copy(out=g["ROW1"][:],
                                               in_=g["PS1"][0:1, 0:128]), inc="dv")
        P.wait("tensor", "dv")
        P.op("tensor", lambda e: e.matmul(g["PX"][:], g["cONEr"][:],
                                          g["ROW1"][:], start=True, stop=True),
             inc="mm")
        P.wait("vector", "mm")
        P.op("vector", lambda e: e.tensor_copy(out=g["RBC8"][:], in_=g["PX"][:]),
             inc="dv")
        P.wait("tensor", "dv", vets)
        for ck in range(NC):
            P.op("tensor", lambda e, c_=ck: e.matmul(
                g["PMC"][:, 0:128], g["VBH"][:, c_ * 128:(c_ + 1) * 128],
                g["ETS"][:, c_ * 128:(c_ + 1) * 128],
                start=(c_ == 0), stop=(c_ == NC - 1)),
                inc="mm" if ck == NC - 1 else None)
        P.wait("vector", "mm")
        vav = P.op("vector", lambda e, h_=h: e.tensor_mul(
            g["AVO"][:, h_ * 128:(h_ + 1) * 128], g["PMC"][:, 0:128], g["RBC8"][:]),
            inc="dv")
        psf["att"] = ("dv", vav)

    if _P8STOP < 6:
        return
    P.wait("tensor", "dv")
    wo_pool.prefetch(g["f_wo"][0, :, :], DCH * 128)
    for oh in range(DCH):
        if oh + 1 < DCH:
            wo_pool.prefetch(g["f_wo"][oh + 1, :, :], DCH * 128)
        bn, lv = wo_pool.take()
        pm = (g["PMA"][:, 0:128], g["PMA"][:, 512:640])[oh % 2]
        key = "p8o" + str(oh % 2)
        if psf.get(key):
            P.wait("tensor", psf[key][0], psf[key][1])
        P.wait("tensor", wo_pool.sems[bn], lv)
        for k in range(DCH):
            P.op("tensor", lambda e, k_=k, pm_=pm, b_=bn:
                 e.matmul(pm_, g[b_][:, k_ * 128:(k_ + 1) * 128],
                          g["AVO"][:, k_ * 128:(k_ + 1) * 128],
                          start=(k_ == 0), stop=(k_ == DCH - 1)),
                 inc="mm" if k == DCH - 1 else None)
        wo_pool.consumed(bn, "mm", P.val("mm"))
        P.wait("vector", "mm")
        vev = P.op("vector", lambda e, oh_=oh, pm_=pm: e.tensor_copy(
            out=g["AOF"][:, oh_ * 128:(oh_ + 1) * 128], in_=pm_), inc="dv")
        psf[key] = ("dv", vev)
    for ch in range(DCH):
        if psf.get("pxh"):
            P.wait("tensor", psf["pxh"][0], psf["pxh"][1])
        P.wait("tensor", "dv")
        P.op("tensor", lambda e, c_=ch: e.transpose(
            pxh, g["AOF"][:, c_ * 128:(c_ + 1) * 128], g["cEYEh"][:]), inc="mm")
        P.wait("vector", "mm")
        vcp = P.op("vector", lambda e, c_=ch: e.tensor_add(
            g["X1B"][:, c_ * 128:(c_ + 1) * 128],
            g["HID"][:, c_ * 128:(c_ + 1) * 128], pxh), inc="dv")
        psf["pxh"] = ("dv", vcp)
    l2w = _bias_ap(g, "f_ln2_w")
    P.wait("scalar", "dv")
    P.op("scalar", lambda e: e.activation(out=g["ATTV"][:], in_=g["X1B"][:],
                                          func=AF.Square, accum_out=g["ST1"][:, 4:5]),
         inc="ac")
    P.wait("vector", "ac")
    P.op("vector", lambda e: e.tensor_scalar_mul(g["ST1"][:, 4:5], g["ST1"][:, 4:5],
                                                 1.0 / D), inc="dv")
    P.wait("scalar", "dv")
    P.op("scalar", lambda e: e.activation(out=g["ST1"][:, 4:5], in_=g["ST1"][:, 4:5],
                                          func=AF.Ln, bias=_cst(g, EPS_RMS_B)),
         inc="ac")
    P.op("scalar", lambda e: e.activation(out=g["ST1"][:, 4:5], in_=g["ST1"][:, 4:5],
                                          func=AF.Exp, scale=-0.5), inc="ac")
    P.wait("vector", "ac")
    P.op("vector", lambda e: e.tensor_scalar_mul(g["P7A"][:, 0:D], g["X1B"][:],
                                                 g["ST1"][:, 4:5]), inc="dv")
    for ch in range(DCH):
        if psf.get("pxh"):
            P.wait("tensor", psf["pxh"][0], psf["pxh"][1])
        P.wait("tensor", "dv")
        P.op("tensor", lambda e, c_=ch: e.transpose(
            pxh, g["P7A"][:, c_ * 128:(c_ + 1) * 128], g["cEYEh"][:]), inc="mm")
        P.wait("vector", "mm")
        vcp = P.op("vector", lambda e, c_=ch: e.tensor_scalar_mul(
            g["H2"][:, c_ * 128:(c_ + 1) * 128], pxh, l2w[:, c_:c_ + 1]), inc="dv")
        psf["pxh"] = ("dv", vcp)

    if _P8STOP < 7:
        return
    P.wait("tensor", "dv")
    pool.prefetch(g["f_w_gate"][0, :, :], DCH * 128)
    pool2.prefetch(g["f_w_up"][0, :, :], DCH * 128)
    for of in range(ICH):
        if of + 1 < ICH:
            pool.prefetch(g["f_w_gate"][of + 1, :, :], DCH * 128)
            pool2.prefetch(g["f_w_up"][of + 1, :, :], DCH * 128)
        b1, v1 = pool.take()
        b2, v2 = pool2.take()
        pmg = (g["PMA"][:, 0:128], g["PMA"][:, 512:640])[of % 2]
        pmu = (g["PMB"][:, 0:128], g["PMB"][:, 512:640])[of % 2]
        keyg, keyu = "p8g" + str(of % 2), "p8u" + str(of % 2)
        for (bn, vv, pm_, key, po) in [(b1, v1, pmg, keyg, pool),
                                       (b2, v2, pmu, keyu, pool2)]:
            if psf.get(key):
                P.wait("tensor", psf[key][0], psf[key][1])
            P.wait("tensor", po.sems[bn], vv)
            for k in range(DCH):
                P.op("tensor", lambda e, k_=k, pm2=pm_, b_=bn:
                     e.matmul(pm2, g[b_][:, k_ * 128:(k_ + 1) * 128],
                              g["H2"][:, k_ * 128:(k_ + 1) * 128],
                              start=(k_ == 0), stop=(k_ == DCH - 1)),
                     inc="mm" if k == DCH - 1 else None)
            po.consumed(bn, "mm", P.val("mm"))
        P.wait("scalar", "mm")
        scr8 = (g["T8"], g["T8b"])[of % 2]
        if psf.get("scr8" + str(of % 2)):
            P.wait("scalar", psf["scr8" + str(of % 2)][0],
                   psf["scr8" + str(of % 2)][1])
        P.op("scalar", lambda e, pm_=pmg, s_=scr8:
             e.activation(out=s_[:], in_=pm_, func=AF.Silu), inc="ac")
        vsg = P.val("ac")
        psf[keyg] = ("ac", vsg)
        P.wait("vector", "ac", vsg)
        vgl = P.op("vector", lambda e, of_=of, pm_=pmu, s_=scr8: e.tensor_mul(
            g["GLB"][:, of_ * 128:(of_ + 1) * 128], s_[:], pm_), inc="dv")
        psf[keyu] = ("dv", vgl)
        psf["scr8" + str(of % 2)] = ("dv", vgl)

    if _P8STOP < 8:
        return
    dpool = g.get("_wpoold") or WPool(P, g, ["WD0", "WD1"], ["d0", "d1"])
    g["_wpoold"] = dpool
    dpool.prefetch(g["f_w_down"][0, :, :], ICH * 128)
    for oh in range(DCH):
        if oh + 1 < DCH:
            dpool.prefetch(g["f_w_down"][oh + 1, :, :], ICH * 128)
        bn, lv = dpool.take()
        pm = (g["PMC"][:, 0:128], g["PMC"][:, 512:640])[oh % 2]
        key = "p8d" + str(oh % 2)
        if psf.get(key):
            P.wait("tensor", psf[key][0], psf[key][1])
        P.wait("tensor", dpool.sems[bn], lv)
        if oh == 0:
            P.wait("tensor", "dv")
        for k in range(ICH):
            P.op("tensor", lambda e, k_=k, pm_=pm, b_=bn:
                 e.matmul(pm_, g[b_][:, k_ * 128:(k_ + 1) * 128],
                          g["GLB"][:, k_ * 128:(k_ + 1) * 128],
                          start=(k_ == 0), stop=(k_ == ICH - 1)),
                 inc="mm" if k == ICH - 1 else None)
        dpool.consumed(bn, "mm", P.val("mm"))
        P.wait("vector", "mm")
        vev = P.op("vector", lambda e, oh_=oh, pm_=pm: e.tensor_copy(
            out=g["MLPf"][:, oh_ * 128:(oh_ + 1) * 128], in_=pm_), inc="dv")
        psf[key] = ("dv", vev)
    for ch in range(DCH):
        if psf.get("pxh"):
            P.wait("tensor", psf["pxh"][0], psf["pxh"][1])
        P.wait("tensor", "dv")
        P.op("tensor", lambda e, c_=ch: e.transpose(
            pxh, g["MLPf"][:, c_ * 128:(c_ + 1) * 128], g["cEYEh"][:]), inc="mm")
        P.wait("vector", "mm")
        vcp = P.op("vector", lambda e, c_=ch: e.tensor_add(
            g["PROC"][:, c_ * 128:(c_ + 1) * 128],
            g["X1B"][:, c_ * 128:(c_ + 1) * 128], pxh), inc="dv")
        psf["pxh"] = ("dv", vcp)
    P.wait("scalar", "dv")
    P.op("scalar", lambda e: e.activation(out=g["ST1"][:, 5:6], in_=g["EXF"][:, 3:4],
                                          func=AF.Sigmoid), inc="ac")
    P.wait("vector", "ac")
    P.op("vector", lambda e: e.tensor_scalar(
        out=g["ST1"][:, 6:7], in0=g["ST1"][:, 5:6], scalar1=-1.0, scalar2=1.0,
        op0=ALU.mult, op1=ALU.add), inc="dv")
    P.op("vector", lambda e: e.tensor_scalar_mul(g["ATTV"][:], g["HID"][:],
                                                 g["ST1"][:, 6:7]), inc="dv")
    P.op("vector", lambda e: e.scalar_tensor_tensor(
        out=g["PROC"][:], in0=g["PROC"][:], scalar=g["ST1"][:, 5:6], in1=g["ATTV"][:],
        op0=ALU.mult, op1=ALU.add), inc="dv")
    P.op("vector", lambda e: e.tensor_add(g["ATTV"][:], g["XSEL"][:, 0:D],
                                          g["PROC"][:]), inc="dv")
    P.op("vector", lambda e: e.tensor_copy(out=g["EXI"][:, 0:1], in_=g["EXF"][:, 4:5]),
         inc="dv")
    vfin = P.val("dv")
    P.wait("scalar", "dv", vfin)
    dma(P, "scalar", "qo1", g["out_sel"], g["ATTV"][:])
    dma(P, "scalar", "qo2", g["out_rowid"], g["EXI"][:, 0:1])


def _kvin_k_ap(g):
    return _mk_ap(g["d_kvin"], 0, [[128, 128], [16384, NKV], [1, 128]])


def _kvin_v_ap(g):
    return _mk_ap(g["d_kvin"], 131072, [[1024, 128], [128, NKV], [1, 128]])


def _kvout_k_ap(g, kvh):
    return _mk_ap(g["d_kvout"], kvh * 16384, [[128, 128], [KVPACK, NC], [1, 128]])


def _kvout_v_ap(g, kvh):
    return _mk_ap(g["d_kvout"], 131072 + kvh * 128,
                  [[1024, 128], [KVPACK, NC], [1, 128]])



def kernel(**inputs):
    global _LAST_RES
    in_maps = _prepare_in_maps(inputs)
    nc, taps = _build_program(debug=_DEBUG_TAPS)
    res = run_bass_kernel_spmd(nc, in_maps, core_ids=list(range(NC)))
    _LAST_RES = res
    return _assemble([r for r in res.results])


# version 06
# speedup vs baseline: 1.0321x; 1.0040x over previous
import numpy as np
import ml_dtypes
from contextlib import ExitStack

import concourse.bass as bass
import concourse.mybir as mybir
from concourse.bass_utils import run_bass_kernel_spmd
from concourse.bass_core_rust import IndirectOffsetOnAxis

F32 = mybir.dt.float32
BF16 = mybir.dt.bfloat16
I32 = mybir.dt.int32
AF = mybir.ActivationFunctionType
ALU = mybir.AluOpType
AXX = mybir.AxisListType.X
BF16NP = ml_dtypes.bfloat16

NC = 8
BB = 2
SEQ = 4096
D = 2048
DF = 6144
IM = 8192
SLOC = 512
TLOC = BB * SLOC
HALO = 3
KSEL = 512
NSLOT = BB * KSEL
SLOTC = NSLOT // NC
HD = 128
NH = 16
NKV = 8
DCH = D // 128
FCH = DF // 128
ICH = IM // 128
KVCH = (NKV * HD) // 128
XNROWS = 1152
HS = HALO + SLOC
EPS_LN = 1e-5
EPS_RMS_A = 1e-8
EPS_RMS_B = 1e-6
C_RGLRU = 8.0
ROPE_THETA = 10000.0
INV_SQRT_HD = float(1.0 / np.sqrt(HD))
NEG_BIG = -1.0e30
BISECT_ITERS = 40
PAYW = 2048
KVPACK = SLOTC * HD * NKV * 2

_DEBUG_TAPS = ()
_LAST_RES = None
_PHASES = 8
_P8STOP = 99



def _block_major(w, kdim, ndim):
    kch, nblk = kdim // 128, ndim // 128
    return np.ascontiguousarray(
        w.reshape(kch, 128, nblk, 128).transpose(2, 1, 0, 3).reshape(nblk, 128, kch * 128))


_BIGW = [
    ("L0_attn_left_W", "a_left_W", D, D),
    ("L0_attn_right_W", "a_right_W", D, D),
    ("L0_attn_ga_W", "a_ga_W", D, D),
    ("L0_attn_gx_W", "a_gx_W", D, D),
    ("L0_attn_out_W", "a_out_W", D, D),
    ("L0_attn_up1_W", "a_up1_W", D, DF),
    ("L0_attn_up2_W", "a_up2_W", D, DF),
    ("L0_attn_down_W", "a_down_W", DF, D),
    ("L0_attn_down_W", "a_down_W_r", D, DF),
    ("L0_ffn_wq", "f_wq", D, D),
    ("L0_ffn_wk", "f_wk", D, NKV * HD),
    ("L0_ffn_wv", "f_wv", D, NKV * HD),
    ("L0_ffn_wo", "f_wo", D, D),
    ("L0_ffn_w_gate", "f_w_gate", D, IM),
    ("L0_ffn_w_up", "f_w_up", D, IM),
    ("L0_ffn_w_down", "f_w_down", IM, D),
]

_VECW = [
    ("L0_attn_norm1_w", "a_norm1_w"), ("L0_attn_left_b", "a_left_b"),
    ("L0_attn_right_b", "a_right_b"), ("L0_attn_conv_b", "a_conv_b"),
    ("L0_attn_ga_b", "a_ga_b"), ("L0_attn_gx_b", "a_gx_b"),
    ("L0_attn_a_param", "a_a_param"), ("L0_attn_out_b", "a_out_b"),
    ("L0_attn_norm2_w", "a_norm2_w"), ("L0_attn_up1_b", "a_up1_b"),
    ("L0_attn_up2_b", "a_up2_b"), ("L0_attn_down_b", "a_down_b"),
    ("L0_ffn_w_router", "f_w_router"), ("L0_ffn_bq", "f_bq"),
    ("L0_ffn_bk", "f_bk"), ("L0_ffn_bv", "f_bv"),
    ("L0_ffn_ln1_w", "f_ln1_w"), ("L0_ffn_ln2_w", "f_ln2_w"),
]


def _prepare_in_maps(inp):
    x = np.asarray(inp["L0_attn_x"], np.float32)
    shared = {}
    for iname, lname, kdim, ndim in _BIGW:
        if lname == "a_down_W_r":
            shared[lname] = np.ascontiguousarray(
                np.asarray(inp[iname], np.float32).reshape(DF // 128, 128, D))
        else:
            shared[lname] = _block_major(np.asarray(inp[iname], np.float32), kdim, ndim)
    for iname, lname in _VECW:
        shared[lname] = np.ascontiguousarray(np.asarray(inp[iname], np.float32))
    shared["eye_f"] = np.eye(128, dtype=np.float32)
    shared["eye_h"] = np.eye(128, dtype=np.float32).astype(BF16NP)
    shared["ones_col"] = np.ones((128, 1), np.float32)
    shared["ones_row"] = np.ones((1, 128), np.float32)
    shared["ltri"] = (np.arange(128)[:, None] < np.arange(128)[None, :]).astype(np.float32)
    invf = 1.0 / ROPE_THETA ** (np.arange(0, HD, 2, dtype=np.float32) / HD)
    shared["invf_dup"] = np.concatenate([invf, invf])[None, :].astype(np.float32)
    rot = np.zeros((128, 128), np.float32)
    for p in range(64):
        rot[p + 64, p] = -1.0
        rot[p, p + 64] = 1.0
    shared["rot_mat"] = rot
    conv_w = np.asarray(inp["L0_attn_conv_w"], np.float32)
    conv_diag = np.zeros((64, 128, 128), np.float32)
    for ch in range(DCH):
        for k in range(4):
            np.fill_diagonal(conv_diag[ch * 4 + k], conv_w[k, ch * 128:(ch + 1) * 128])
    shared["conv_diag"] = conv_diag
    zeros_rb = np.zeros(D, np.float32)

    in_maps = []
    p128 = np.arange(128)
    for c in range(NC):
        m = dict(shared)
        xs = np.zeros((BB, HS, D), np.float32)
        lo = c * SLOC
        h0 = max(0, lo - HALO)
        xs[:, HALO - (lo - h0):HALO] = x[:, h0:lo]
        xs[:, HALO:] = x[:, lo:lo + SLOC]
        m["x"] = xs
        m["right_b_halo"] = shared["a_right_b"] if c > 0 else zeros_rb
        m["rank_oh_col"] = np.tile((np.arange(NC) == c).astype(np.float32), (128, 1))
        tau = np.zeros((128, 8), np.float32)
        rowid = np.zeros((128, 8), np.float32)
        for tt in range(8):
            b, q = tt // 4, tt % 4
            tv = c * SLOC + q * 128 + p128
            tau[:, tt] = tv
            rowid[:, tt] = b * SEQ + tv
        m["taucol"] = tau
        m["rowidcol"] = rowid
        m["gath_rows"] = rowid.astype(np.int32)
        m["slotcol"] = (c * SLOTC + p128)[:, None].astype(np.float32)
        in_maps.append(m)
    return in_maps


def _assemble(results):
    out = np.zeros((BB, SEQ, D), np.float32)
    for c in range(NC):
        ob = results[c]["out_base"]
        out[0, c * SLOC:(c + 1) * SLOC] = ob[:SLOC]
        out[1, c * SLOC:(c + 1) * SLOC] = ob[SLOC:]
    flat = out.reshape(BB * SEQ, D)
    for c in range(NC):
        rid = results[c]["out_rowid"][:, 0].astype(np.int64)
        flat[rid] = results[c]["out_sel"]
    return out



ENGS = ("sync", "tensor", "vector", "scalar", "gpsimd")


class Plan:
    def __init__(self, nc):
        self.nc = nc
        self.items = []
        self.semh = {}
        self.cnt = {}

    def new_sem(self, es, name):
        self.semh[name] = es.enter_context(self.nc.semaphore(name))
        self.cnt[name] = 0

    SELF_SEM = {"vector": "dv", "scalar": "ac", "gpsimd": "gp"}

    def op(self, eng, fn, inc=None, inc_n=1):
        own = self.SELF_SEM.get(eng)
        if own is not None and inc == own:
            self.wait(eng, own)
        if inc is not None:
            self.cnt[inc] += inc_n
            sem, n = self.semh[inc], inc_n
            self.items.append((eng, lambda e: fn(e).then_inc(sem, n)))
            return self.cnt[inc]
        self.items.append((eng, lambda e: fn(e)))
        return None

    def wait(self, eng, sem, val=None):
        v = self.cnt[sem] if val is None else val
        if v <= 0:
            return
        s = self.semh[sem]
        self.items.append((eng, lambda e: e.wait_ge(s, v)))

    def val(self, sem):
        return self.cnt[sem]

    def replay(self, block):
        def mk(engname):
            def body(engobj):
                for eng, fn in self.items:
                    if eng == engname:
                        fn(engobj)
            return body
        for engname in ENGS:
            getattr(block, engname)(mk(engname))


def dma(P, queue, sem, out, in_, transpose=False, **kw):
    if transpose:
        return P.op(queue, lambda e: e.dma_start(out=out, in_=in_, transpose=True, **kw),
                    inc=sem, inc_n=16)
    return P.op(queue, lambda e: e.dma_start(out=out, in_=in_, **kw), inc=sem, inc_n=16)


class WPool:
    def __init__(self, P, g, names, sems):
        self.P, self.g = P, g
        self.names = list(names)
        self.sems = dict(zip(names, sems))
        self.free_after = {n: None for n in names}
        self.i = 0
        self.pending = []

    def load(self, wap_block, width):
        n = self.names[self.i % len(self.names)]
        self.i += 1
        fa = self.free_after[n]
        if fa is not None:
            self.P.wait("gpsimd", fa[0], fa[1])
        v = dma(self.P, "gpsimd", self.sems[n], self.g[n][:, 0:width], wap_block)
        return n, v

    def prefetch(self, wap_block, width):
        self.pending.append(self.load(wap_block, width))

    def take(self):
        return self.pending.pop(0)

    def consumed(self, bufname, sem, val):
        self.free_after[bufname] = (sem, val)



def _build_program(debug=()):
    nc = bass.Bass("TRN2", target_bir_lowering=False)
    g = {}

    def din(name, shape, dt=F32):
        g[name] = nc.dram_tensor(name, shape, dt, kind="ExternalInput").ap()
    def dout(name, shape, dt=F32):
        g[name] = nc.dram_tensor(name, shape, dt, kind="ExternalOutput").ap()
    def dint(name, shape, dt=F32):
        g[name] = nc.dram_tensor(name, shape, dt).ap()

    din("x", [BB, HS, D])
    for iname, lname, kdim, ndim in _BIGW:
        din(lname, [ndim // 128, 128, (kdim // 128) * 128])
    for iname, lname in _VECW:
        n = {"f_bk": NKV * HD, "f_bv": NKV * HD, "a_up1_b": DF, "a_up2_b": DF}.get(lname, D)
        din(lname, [n])
    din("right_b_halo", [D])
    din("eye_f", [128, 128]); din("eye_h", [128, 128], BF16)
    din("ones_col", [128, 1]); din("ones_row", [1, 128])
    din("ltri", [128, 128]); din("invf_dup", [1, 128]); din("rot_mat", [128, 128])
    din("conv_diag", [64, 128, 128])
    din("rank_oh_col", [128, 8]); din("taucol", [128, 8]); din("rowidcol", [128, 8])
    din("gath_rows", [128, 8], I32); din("slotcol", [128, 1])

    dout("out_base", [TLOC, D])
    dout("out_sel", [SLOTC, D])
    dout("out_rowid", [SLOTC, 1], I32)

    dint("d_xn_park", [XNROWS, D], BF16)
    dint("d_xln_park", [TLOC, D], BF16)
    dint("d_xln_lo", [TLOC, D], BF16)
    dint("d_left_park", [128, DCH * TLOC], BF16)
    dint("d_atn_park", [128, DCH * TLOC])
    dint("d_glu", [FCH, 128, TLOC], BF16)
    dint("d_xa_park", [TLOC, D], BF16)
    dint("d_ex_park", [TLOC, 8])
    dint("d_slotrec", [BB * SEQ, 4], I32)
    dint("d_send", [NSLOT, PAYW], BF16)
    dint("d_recv", [NSLOT, PAYW], BF16)
    dint("d_send_ex", [NSLOT, 8])
    dint("d_recv_ex", [NSLOT, 8])
    dint("d_scin", [TLOC, 1])
    dint("d_scag", [NC * TLOC, 1])
    dint("d_cin", [128, 64])
    dint("d_cout", [NC * 128, 64])
    dint("d_kvin", [1, KVPACK], BF16)
    dint("d_kvout", [NC, KVPACK], BF16)

    taps = []
    def tap(name, shape, dt=F32):
        if name in debug:
            g["tap_" + name] = nc.dram_tensor("tap_" + name, shape, dt,
                                              kind="ExternalOutput").ap()
            taps.append(name)
            return True
        return False

    es = ExitStack()
    with es:
        def sb(name, shape, dt=F32):
            g[name] = es.enter_context(nc.sbuf_tensor(name, shape, dt))

        def psm(name, shape, dt=F32):
            g[name] = es.enter_context(nc.psum_tensor(name, shape, dt))

        sb("cEYEf", [128, 128]); sb("cEYEh", [128, 128], BF16)
        sb("cONEc", [128, 1]); sb("cONEr", [1, 128]); sb("cLTRI", [128, 128])
        sb("cINVF", [1, 128]); sb("cROT", [128, 128], BF16)
        sb("cROH", [128, 8]); sb("cTAU", [128, 8]); sb("cROW", [128, 8])
        sb("cGRW", [128, 8], I32); sb("cSLOT", [128, 1])
        sb("BIAS", [128, 384])
        sb("ST1", [128, 16]); sb("ST2", [128, 16]); sb("ST3", [128, 16])
        sb("ROW1", [1, 128])
        sb("CST", [128, 8])
        sb("SCR", [128, SLOC]); sb("SCR2", [128, SLOC]); sb("SCR3", [128, SLOC])
        sb("WB0", [128, DCH * 128], BF16); sb("WB1", [128, DCH * 128], BF16)
        sb("WC0", [128, DCH * 128], BF16); sb("WC1", [128, DCH * 128], BF16)

        sb("ARA", [128, 9216])
        sb("ARB", [128, 8320])
        sb("ARC", [128, 8192])
        sb("ARD", [128, 8192])
        sb("ARE", [128, 8192])
        sb("ARF", [128, 4096])

        def carve(arena, f32_off, f32_len, dt=F32):
            ap = g[arena][:, f32_off:f32_off + f32_len]
            if dt is F32:
                return ap
            return ap.bitcast(dt)

        g["XN"] = carve("ARA", 0, 9216, BF16)
        g["IH"] = carve("ARB", 0, 8192, BF16)
        g["X1a"] = carve("ARA", 0, 8192)
        g["UEXT"] = carve("ARB", 0, 8240, BF16)
        g["X1b"] = carve("ARB", 0, 8192)
        g["W1BC"] = carve("ARC", 0, 2048)
        g["XTOK"] = carve("ARC", 2048, 2048)
        g["TOK2"] = carve("ARC", 4096, 2048)
        g["XTOKb"] = carve("ARC", 6144, 2048)
        g["TOK2b"] = carve("ARB", 0, 2048)
        g["TOKHb1"] = carve("ARB", 2048, 1024, BF16)
        g["YB"] = carve("ARC", 0, 8192, BF16)
        g["RGT"] = carve("ARA", 0, 8192, BF16)
        g["WD0"] = carve("ARC", 0, 4096, BF16)
        g["WD1"] = carve("ARC", 4096, 4096, BF16)
        g["XTOK6"] = carve("ARC", 0, 2048)
        g["TOK26"] = carve("ARC", 2048, 2048)
        g["WRBC"] = carve("ARC", 4096, 2048)
        g["LAB"] = carve("ARD", 0, 8192, BF16)
        g["XLT0"] = carve("ARD", 0, 512, BF16)
        g["XLT1"] = carve("ARD", 512, 512, BF16)
        g["SQ4a"] = carve("ARD", 1024, 1024)
        g["SQ4b"] = carve("ARD", 2048, 1024)
        g["SS4"] = carve("ARD", 3072, 1024)
        g["RSB"] = carve("ARD", 4096, 1024)
        g["GB0"] = carve("ARD", 0, 256, BF16)
        g["GB1"] = carve("ARD", 256, 256, BF16)
        g["GSTG0"] = carve("ARD", 512, 512, BF16)
        g["GSTG1"] = carve("ARD", 1024, 512, BF16)
        g["DSTG0"] = carve("ARD", 6144, 512)
        g["DSTG1"] = carve("ARD", 6656, 512)
        g["ATNTa"] = carve("ARA", 0, 8192)
        g["ATNTb"] = carve("ARB", 0, 8192)
        g["ATHF0"] = carve("ARF", 1024, 1024)
        g["ATHF1"] = carve("ARF", 2048, 1024)
        g["BBT"] = carve("ARE", 0, 8192, BF16)
        g["XN2"] = carve("ARE", 0, 8192, BF16)
        g["XTOK6b"] = carve("ARD", 0, 2048)
        g["SQ6b"] = carve("ARD", 2048, 2048)
        g["HID6b"] = carve("ARD", 4096, 2048)
        g["TOKH6b"] = carve("ARD", 6144, 1024, BF16)
        g["EXT6b"] = carve("ARD", 7168, 8)
        g["SQ6"] = carve("ARE", 0, 2048)
        g["HID6"] = carve("ARE", 2048, 2048)
        g["SCO6"] = carve("ARE", 4096, 8)
        g["EXT6"] = carve("ARE", 4104, 8)
        g["CDIAG"] = carve("ARF", 0, 4096, BF16)
        g["AB"] = carve("ARF", 0, 64)
        g["CAR"] = carve("ARF", 64, 512)
        g["HPRE"] = carve("ARF", 576, 288)
        g["HMINE"] = carve("ARF", 864, 32)
        g["TOKH"] = carve("ARF", 1024, 1024, BF16)
        g["STAGE"] = carve("ARF", 2048, 128)
        g["HROW"] = carve("ARF", 2048, 2048)[0:1, :]
        g["SC7"] = carve("ARE", 4104, 64)
        g["MG7"] = carve("ARE", 4168, 32)
        g["MEQ7"] = carve("ARE", 4200, 32)
        g["MASK7"] = carve("ARE", 4232, 32)
        g["CEX7"] = carve("ARE", 4264, 32)
        g["INCL7"] = carve("ARE", 4296, 32)
        g["T7a"] = carve("ARE", 4328, 32)
        g["T7b"] = carve("ARE", 4360, 32)
        g["BIG7"] = carve("ARE", 4392, 32)
        g["ZERO7"] = carve("ARE", 4424, 32)
        g["STAG7"] = carve("ARE", 4456, 128).bitcast(I32)
        g["IDXI7"] = carve("ARE", 4584, 32).bitcast(I32)
        g["IDG7"] = carve("ARE", 4616, 4).bitcast(I32)
        g["EOFF7"] = carve("ARE", 4620, 1)
        g["MIDB7"] = carve("ARE", 4621, 1)
        g["HIB7"] = carve("ARE", 4622, 1)
        g["NEEDB7"] = carve("ARE", 4623, 1)
        g["CNTC7"] = carve("ARE", 4624, 1)
        g["P7A"] = carve("ARE", 3072, 1024, BF16)
        g["P7E"] = carve("ARE", 4626, 8)
        g["SCO8R"] = carve("ARE", 4640, 128)
        g["HID"] = carve("ARA", 0, 2048)
        g["XSEL"] = carve("ARA", 2048, 1024, BF16)
        g["EXF"] = carve("ARA", 3080, 8)
        g["EXI"] = carve("ARA", 3088, 8).bitcast(I32)
        g["QR"] = carve("ARA", 3096, 1024, BF16)
        g["KR"] = carve("ARA", 4120, 512, BF16)
        g["VTt"] = carve("ARA", 4632, 512, BF16)
        g["ANG"] = carve("ARA", 5144, 128)
        g["SINt"] = carve("ARA", 5272, 128)
        g["COSt"] = carve("ARA", 5400, 128)
        g["X1B"] = carve("ARA", 5528, 2048)
        g["QKVf"] = carve("ARA", 7576, 1024, BF16)
        g["T8"] = carve("ARA", 8600, 128)
        g["T8b"] = carve("ARA", 8728, 128)
        g["MHB8"] = carve("ARA", 8856, 1)
        g["KBH"] = carve("ARB", 0, 512, BF16)
        g["VBH"] = carve("ARB", 512, 512, BF16)
        g["ATTS"] = carve("ARB", 1024, 1024)
        g["ETS"] = carve("ARB", 2048, 512, BF16)
        g["EHS"] = carve("ARB", 2560, 512, BF16)
        g["AVO"] = carve("ARB", 3072, 1024, BF16)
        g["MASKM"] = carve("ARB", 4096, 1024)
        g["NEGT"] = carve("ARB", 5120, 1024)
        g["MTCH"] = carve("ARB", 6144, 512, BF16)
        g["QROW8"] = carve("ARB", 6656, 128)
        g["RBC8"] = carve("ARB", 6784, 128)
        g["KS8"] = carve("ARB", 6912, 8)
        g["DEN8"] = carve("ARB", 6920, 16)
        g["REC8"] = carve("ARB", 6936, 16)
        g["OSEL"] = carve("ARB", 6952, 1).bitcast(I32)
        g["H2"] = carve("ARD", 0, 1024, BF16)
        g["GLB"] = carve("ARD", 1024, 4096, BF16)
        g["PROC"] = carve("ARD", 5120, 2048)
        g["AOF"] = carve("ARD", 7168, 1024, BF16)
        g["MLPf"] = carve("ARE", 0, 1024, BF16)
        g["ATTV"] = carve("ARE", 1024, 2048)

        psm("PMA", [128, 1024]); psm("PMB", [128, 1024])
        psm("PMC", [128, 1024]); psm("PMD", [128, 1024])
        g["PM0"] = g["PMA"][:, :]
        g["PM1"] = g["PMB"][:, :]
        g["PN0"] = g["PMC"][:, 0:512]
        g["PN1"] = g["PMC"][:, 512:1024]
        g["PX"] = g["PMD"][:, 0:128]
        g["PS1"] = g["PMD"][0:1, 512:1024]

        P = Plan(nc)
        for s in ("mm", "dv", "ac", "gp", "cc", "qc", "qx", "qp", "qn",
                  "w0", "w1", "c0", "c1", "d0", "d1",
                  "q1", "q2", "q3", "q4", "q5", "q6", "q7", "q8", "q9",
                  "qa", "qb", "qd", "qe", "qf", "qg",
                  "qo1", "qo2", "qo3", "qtap"):
            P.new_sem(es, s)

        block = es.enter_context(nc.Block())
        _emit_all(nc, P, g, tap)
        P.replay(block)
    return nc, taps


def _bias_ap(g, name):
    off = g["_bias_off"][name]
    return g["BIAS"][:, off:off + g["_bias_n"][name]]



def _emit_all(nc, P, g, tap):
    phases = [_p1, _p2, _p3, _p4, _p5, _p6, _p7, _p8]
    _pre(nc, P, g)
    for i in range(min(_PHASES, 8)):
        phases[i](nc, P, g, tap)
    for sem in ("mm", "dv", "ac", "gp", "cc", "qc", "qx", "qp", "qn",
                "w0", "w1", "c0", "c1", "d0", "d1",
                "q1", "q2", "q3", "q4", "q5", "q6", "q7", "q8", "q9",
                "qa", "qb", "qd", "qe", "qf", "qg",
                "qo1", "qo2", "qo3", "qtap"):
        P.wait("gpsimd", sem)


CST_COLS = {}

def _cst(g, val, base=0, n=128):
    return g["CST"][base:base + n, CST_COLS[val]:CST_COLS[val] + 1]


def _pre(nc, P, g):
    for i, v in enumerate([EPS_LN, EPS_RMS_A, EPS_RMS_B, 1.0, 0.0]):
        CST_COLS[v] = i
        P.op("vector", lambda e, i_=i, v_=v: e.memset(g["CST"][:, i_:i_ + 1], v_),
             inc="dv")
    for n, dst in [("eye_f", "cEYEf"), ("eye_h", "cEYEh"), ("ones_col", "cONEc"),
                   ("ones_row", "cONEr"), ("ltri", "cLTRI"), ("invf_dup", "cINVF"),
                   ("rank_oh_col", "cROH"), ("taucol", "cTAU"), ("rowidcol", "cROW"),
                   ("gath_rows", "cGRW"), ("slotcol", "cSLOT")]:
        dma(P, "sync", "qc", g[dst][:], g[n])
    dma(P, "gpsimd", "q9", g["cROT"][:], g["rot_mat"])

    bias_vecs = [("a_norm1_w", DCH), ("a_left_b", DCH), ("a_right_b", DCH),
                 ("right_b_halo", DCH), ("a_conv_b", DCH), ("a_ga_b", DCH),
                 ("a_gx_b", DCH), ("a_a_param", DCH), ("a_out_b", DCH),
                 ("a_norm2_w", DCH), ("a_up1_b", FCH), ("a_up2_b", FCH),
                 ("a_down_b", DCH), ("f_bq", DCH), ("f_bk", NKV), ("f_bv", NKV),
                 ("f_ln1_w", DCH), ("f_ln2_w", DCH)]
    g["_bias_off"], g["_bias_n"] = {}, {}
    off = 0
    vz = P.op("vector", lambda e: e.memset(g["STAGE"][:], 0.0), inc="dv")
    P.wait("sync", "dv", vz)
    for name, nch in bias_vecs:
        g["_bias_off"][name], g["_bias_n"][name] = off, nch
        v = dma(P, "sync", "qc", g["STAGE"][0:nch, 0:128],
                g[name].rearrange("(a b) -> a b", b=128))
        P.wait("tensor", "qc", v)
        P.op("tensor", lambda e: e.transpose(g["PX"][:], g["STAGE"][:], g["cEYEf"][:]),
             inc="mm")
        P.wait("vector", "mm")
        v2 = P.op("vector", lambda e, o_=off, n_=nch:
                  e.tensor_copy(out=g["BIAS"][:, o_:o_ + n_], in_=g["PX"][:, 0:n_]),
                  inc="dv")
        P.wait("tensor", "dv", v2)
        P.wait("sync", "dv", v2)
        off += nch
    assert off <= 384

    ap = _bias_ap(g, "a_a_param")
    s1, s2 = g["ST1"][:, 0:DCH], g["ST2"][:, 0:DCH]
    P.wait("scalar", "dv")
    P.op("scalar", lambda e: e.activation(out=s1, in_=ap, func=AF.Abs), inc="ac")
    P.wait("scalar", "ac")
    P.op("scalar", lambda e: e.activation(out=s1, in_=s1, func=AF.Exp, scale=-1.0),
         inc="ac")
    P.wait("scalar", "ac")
    P.op("scalar", lambda e: e.activation(out=s1, in_=s1, func=AF.Ln, bias=_cst(g, 1.0)),
         inc="ac")
    P.wait("vector", "ac")
    P.op("vector", lambda e: e.tensor_scalar_max(s2, ap, 0.0), inc="dv")
    P.wait("vector", "dv")
    P.op("vector", lambda e: e.tensor_add(s2, s2, s1), inc="dv")
    P.wait("vector", "dv")
    P.op("vector", lambda e: e.tensor_scalar_mul(ap, s2, -C_RGLRU), inc="dv")

    _bcast_row(nc, P, g, g["a_norm1_w"], g["W1BC"], D)


def _bcast_row(nc, P, g, dram_vec, dst, n):
    v = dma(P, "sync", "qc", g["HROW"][0:1, 0:n], dram_vec.rearrange("(a b) -> a b", a=1))
    P.wait("tensor", "qc", v)
    for h in range(0, n, 1024):
        hw = min(1024, n - h)
        for j in range(0, hw, 512):
            jw = min(512, hw - j)
            P.op("tensor", lambda e, h_=h, j_=j, jw_=jw:
                 e.matmul(g["PM1"][:, j_:j_ + jw_], g["cONEr"][:],
                          g["HROW"][0:1, h_ + j_:h_ + j_ + jw_],
                          start=True, stop=True), inc="mm")
        P.wait("vector", "mm")
        v2 = P.op("vector", lambda e, h_=h, hw_=hw:
                  e.tensor_copy(out=dst[:, h_:h_ + hw_], in_=g["PM1"][:, 0:hw_]),
                  inc="dv")
        P.wait("tensor", "dv", v2)
    P.wait("sync", "dv", P.val("dv"))


def _ln_tile(nc, P, g, xap, base, n, eps, rms_eps=None, sq=None, out=None, col=0,
             pre_dv="curr"):
    st_m = g["ST1"][base:base + n, col:col + 1]
    st_r = g["ST2"][base:base + n, col:col + 1]
    st_3 = g["ST3"][base:base + n, col:col + 1]
    sq = g["TOK2"][base:base + n, :] if sq is None else sq
    P.op("vector", lambda e: e.reduce_sum(st_m, xap, axis=AXX), inc="dv")
    P.wait("vector", "dv")
    P.op("vector", lambda e: e.tensor_scalar_mul(st_m, st_m, 1.0 / D), inc="dv")
    if pre_dv == "curr":
        P.wait("scalar", "dv")
    elif pre_dv:
        P.wait("scalar", "dv", pre_dv)
    P.op("scalar", lambda e: e.activation(out=sq, in_=xap, func=AF.Square,
                                          accum_out=st_r), inc="ac")
    P.wait("vector", "ac")
    P.op("vector", lambda e: e.tensor_scalar_mul(st_r, st_r, 1.0 / D), inc="dv")
    P.op("vector", lambda e: e.tensor_mul(st_3, st_m, st_m), inc="dv")
    P.wait("vector", "dv")
    P.op("vector", lambda e: e.tensor_sub(st_r, st_r, st_3), inc="dv")
    P.wait("scalar", "dv")
    P.op("scalar", lambda e, e_=eps: e.activation(out=st_r, in_=st_r, func=AF.Ln,
                                                  bias=_cst(g, e_, base, n)), inc="ac")
    P.wait("scalar", "ac")
    P.op("scalar", lambda e: e.activation(out=st_r, in_=st_r, func=AF.Exp, scale=-0.5),
         inc="ac")
    P.wait("vector", "ac")
    out = xap if out is None else out
    P.op("vector", lambda e, o_=out: e.tensor_scalar(out=o_, in0=xap, scalar1=st_m,
                                                     scalar2=st_r, op0=ALU.subtract,
                                                     op1=ALU.mult), inc="dv")
    if rms_eps is not None:
        P.wait("scalar", "dv")
        P.op("scalar", lambda e, o_=out: e.activation(out=sq, in_=o_, func=AF.Square,
                                                      accum_out=st_3), inc="ac")
        P.wait("vector", "ac")
        P.op("vector", lambda e: e.tensor_scalar_mul(st_3, st_3, 1.0 / D), inc="dv")
        P.wait("scalar", "dv")
        P.op("scalar", lambda e, re_=rms_eps: e.activation(out=st_3, in_=st_3,
                                                           func=AF.Ln,
                                                           bias=_cst(g, re_, base, n)),
             inc="ac")
        P.wait("scalar", "ac")
        P.op("scalar", lambda e: e.activation(out=st_3, in_=st_3, func=AF.Exp,
                                              scale=-0.5), inc="ac")
        P.wait("vector", "ac")


def _p1(nc, P, g, tap):
    x = g["x"]
    tap
    tokh_free = [0, 0]
    xtok_free = [0, 0]
    for tt in range(8):
        b, q = tt // 4, tt % 4
        par = tt % 2
        XT = (g["XTOK"], g["XTOKb"])[par][:]
        T2 = (g["TOK2"], g["TOK2b"])[par]
        TH = (g["TOKH"], g["TOKHb1"])[par]
        qls = ("qx", "q8")[par]
        if xtok_free[par]:
            P.wait("sync", "dv", xtok_free[par])
        vld = dma(P, "sync", qls, XT, x[b, HALO + q * 128: HALO + (q + 1) * 128, :])
        P.wait("vector", qls, vld)
        P.wait("scalar", qls, vld)
        qps = ("qp", "qc")[par]
        if tokh_free[par]:
            P.wait("scalar", qps, tokh_free[par])
        _ln_tile(nc, P, g, XT, 0, 128, EPS_LN, EPS_RMS_A, sq=T2[:], col=par,
                 pre_dv=xtok_free[par] or None)
        if tokh_free[par]:
            P.wait("vector", qps, tokh_free[par])
        P.op("vector", lambda e, x_=XT, t_=TH: e.tensor_copy(out=t_[:], in_=x_),
             inc="dv")
        vxl = P.val("dv")
        P.wait("scalar", "dv", vxl)
        dma(P, "scalar", qps, g["d_xln_park"][tt * 128:(tt + 1) * 128, :], TH[:])
        tokh_free[par] = P.val(qps)
        P.op("vector", lambda e, x_=XT, t_=TH, s_=T2: e.tensor_sub(
            s_.bitcast(BF16)[:, 0:D], x_, t_[:]), inc="dv")
        vlo = P.val("dv")
        P.wait("scalar", "dv", vlo)
        dma(P, "scalar", qps, g["d_xln_lo"][tt * 128:(tt + 1) * 128, :],
            T2.bitcast(BF16)[:, 0:D])
        tokh_free[par] = P.val(qps)
        P.wait("vector", qps, tokh_free[par])
        P.op("vector", lambda e, x_=XT, t_=TH, p_=par: e.scalar_tensor_tensor(
            out=t_[:], in0=x_, scalar=g["ST3"][:, p_:p_ + 1],
            in1=g["W1BC"][:], op0=ALU.mult, op1=ALU.mult), inc="dv")
        xtok_free[par] = P.val("dv")
        P.wait("scalar", "dv", xtok_free[par])
        dma(P, "scalar", qps, g["d_xn_park"][tt * 128:(tt + 1) * 128, :], TH[:])
        tokh_free[par] = P.val(qps)
    P.wait("sync", "dv", xtok_free[0])
    P.wait("sync", "dv", xtok_free[1])
    dma(P, "sync", "qx", g["XTOK"][0:HALO, :], x[0, 0:HALO, :])
    vld = dma(P, "sync", "qx", g["XTOK"][32:32 + HALO, :], x[1, 0:HALO, :])
    P.wait("vector", "qx", vld)
    P.wait("scalar", "qx", vld)
    P.wait("vector", "qp", tokh_free[0])
    for base in (0, 32):
        _ln_tile(nc, P, g, g["XTOK"][base:base + HALO, :], base, HALO, EPS_LN,
                 EPS_RMS_A, sq=g["TOK2"][base:base + HALO, :])
        P.op("vector", lambda e, b_=base: e.scalar_tensor_tensor(
            out=g["TOKH"][b_:b_ + HALO, :], in0=g["XTOK"][b_:b_ + HALO, :],
            scalar=g["ST3"][b_:b_ + HALO, 0:1], in1=g["W1BC"][b_:b_ + HALO, :],
            op0=ALU.mult, op1=ALU.mult), inc="dv")
    vh = P.val("dv")
    P.wait("scalar", "dv", vh)
    dma(P, "scalar", "qp", g["d_xn_park"][1024:1024 + HALO, :], g["TOKH"][0:HALO, :])
    vhp = dma(P, "scalar", "qp", g["d_xn_park"][1027:1027 + HALO, :], g["TOKH"][32:32 + HALO, :])
    P.wait("vector", "qp", vhp)
    vz = P.op("vector", lambda e: e.memset(g["TOKH"][:], 0.0), inc="dv")
    P.wait("scalar", "dv", vz)
    dma(P, "scalar", "qp", g["d_xn_park"][1030:XNROWS, :], g["TOKH"][0:XNROWS - 1030, :])
    g["_p1_qt"] = P.val("qp")
    g["_p1_qt2"] = P.val("qc")
    g["_p1_dv"] = P.val("dv")
    if tap("xlnpark", [TLOC, D], BF16):
        P.wait("gpsimd", "qp", g["_p1_qt"])
        dma(P, "gpsimd", "qtap", g["tap_xlnpark"], g["d_xln_park"])
    if tap("xnpark", [XNROWS, D], BF16):
        P.wait("gpsimd", "qp", g["_p1_qt"])
        dma(P, "gpsimd", "qtap", g["tap_xnpark"], g["d_xn_park"])


def _p2(nc, P, g, tap):
    P.wait("scalar", "qp", g["_p1_qt"])
    P.wait("scalar", "qc", g["_p1_qt2"])
    for ch in range(DCH):
        dma(P, "scalar", "qn", g["XN"][:, ch * XNROWS:(ch + 1) * XNROWS],
            g["d_xn_park"][:, ch * 128:(ch + 1) * 128], transpose=True)
    g["_xn_qt"] = P.val("qn")

    pool = WPool(P, g, ["WB0", "WB1"], ["w0", "w1"])
    g["_wpool"] = pool
    pool2 = WPool(P, g, ["WC0", "WC1"], ["c0", "c1"])
    g["_wpool2"] = pool2
    rb = _bias_ap(g, "a_right_b")
    rbh = _bias_ap(g, "right_b_halo")
    lb = _bias_ap(g, "a_left_b")
    psfree = g.setdefault("_psf", {})

    pool.prefetch(g["a_right_W"][0, :, :], DCH * 128)
    for oh in range(DCH):
        if oh + 1 < DCH:
            pool.prefetch(g["a_right_W"][oh + 1, :, :], DCH * 128)
        bn, lv = pool.take()
        pm = (g["PM0"], g["PM1"])[oh % 2]
        key = ("PM0", "PM1")[oh % 2]
        if psfree.get(key):
            P.wait("tensor", psfree[key][0], psfree[key][1])
        P.wait("tensor", pool.sems[bn], lv)
        if oh == 0:
            P.wait("tensor", "qn", g["_xn_qt"])
        if psfree.get("PH"):
            P.wait("tensor", psfree["PH"][0], psfree["PH"][1])
        for k in range(DCH):
            for (po, mo, wd) in [(0, 0, 512), (512, 512, 512)]:
                P.op("tensor", lambda e, k_=k, po_=po, mo_=mo, wd_=wd, pm_=pm, bn_=bn:
                     e.matmul(pm_[:, po_:po_ + wd_],
                              g[bn_][:, k_ * 128:(k_ + 1) * 128],
                              g["XN"][:, k_ * XNROWS + mo_: k_ * XNROWS + mo_ + wd_],
                              start=(k_ == 0), stop=(k_ == DCH - 1)), inc=None)
            P.op("tensor", lambda e, k_=k, bn_=bn:
                 e.matmul(g["PX"][:, 0:2 * HALO],
                          g[bn_][:, k_ * 128:(k_ + 1) * 128],
                          g["XN"][:, k_ * XNROWS + 1024: k_ * XNROWS + 1024 + 2 * HALO],
                          start=(k_ == 0), stop=(k_ == DCH - 1)),
                 inc="mm" if k == DCH - 1 else None)
        pool.consumed(bn, "mm", P.val("mm"))
        P.wait("vector", "mm")
        for (p0, u0) in [(0, HALO), (512, HS + HALO)]:
            P.op("vector", lambda e, oh_=oh, p0_=p0, u0_=u0, pm_=pm:
                 e.tensor_scalar_add(g["UEXT"][:, oh_ * BB * HS + u0_:
                                               oh_ * BB * HS + u0_ + 512],
                                     pm_[:, p0_:p0_ + 512],
                                     rb[:, oh_:oh_ + 1]), inc="dv")
        for b in range(BB):
            P.op("vector", lambda e, oh_=oh, b_=b:
                 e.tensor_scalar_add(g["UEXT"][:, oh_ * BB * HS + b_ * HS:
                                               oh_ * BB * HS + b_ * HS + HALO],
                                     g["PX"][:, b_ * HALO:(b_ + 1) * HALO],
                                     rbh[:, oh_:oh_ + 1]), inc="dv")
        vac = P.val("dv")
        psfree[key] = ("dv", vac)
        psfree["PH"] = ("dv", vac)
    g["_uext_dv"] = P.val("dv")

    g["_uext_done"] = True
    g["_p2_mm"] = P.val("mm")

    if tap("xn", [128, DCH * XNROWS], BF16):
        P.wait("gpsimd", "qn", g["_xn_qt"])
        dma(P, "gpsimd", "qtap", g["tap_xn"], g["XN"][:])
    if tap("uext", [128, DCH * BB * HS], BF16):
        P.wait("gpsimd", "dv", g["_uext_dv"])
        dma(P, "gpsimd", "qtap", g["tap_uext"], g["UEXT"][:, 0:DCH * BB * HS])



def _p3(nc, P, g, tap):
    pool, pool2 = g["_wpool"], g["_wpool2"]
    psf = g["_psf"]

    P.wait("gpsimd", "q9", 16)
    vcd = dma(P, "gpsimd", "q9", g["CDIAG"][:],
              g["conv_diag"].rearrange("n p f -> p n f"))

    cb = _bias_ap(g, "a_conv_b")
    pmh = [(g["PM0"], 0), (g["PM0"], 512), (g["PM1"], 0), (g["PM1"], 512)]
    P.wait("tensor", "q9", vcd)
    P.wait("tensor", "dv", g["_uext_dv"])
    i = 0
    for ch in range(DCH):
        for b in range(BB):
            pm, po = pmh[i % 4]
            key = "conv" + str(i % 4)
            if psf.get(key):
                P.wait("tensor", psf[key][0], psf[key][1])
            for k in range(4):
                P.op("tensor", lambda e, ch_=ch, b_=b, k_=k, pm_=pm, po_=po:
                     e.matmul(pm_[:, po_:po_ + 512],
                              g["CDIAG"][:, (ch_ * 4 + k_) * 128:(ch_ * 4 + k_ + 1) * 128],
                              g["UEXT"][:, ch_ * BB * HS + b_ * HS + k_:
                                        ch_ * BB * HS + b_ * HS + k_ + 512],
                              start=(k_ == 0), stop=(k_ == 3)),
                     inc="mm" if k == 3 else None)
            P.wait("vector", "mm")
            v = P.op("vector", lambda e, ch_=ch, b_=b, pm_=pm, po_=po:
                     e.tensor_scalar_add(g["YB"][:, ch_ * TLOC + b_ * 512:
                                                 ch_ * TLOC + b_ * 512 + 512],
                                         pm_[:, po_:po_ + 512],
                                         cb[:, ch_:ch_ + 1]), inc="dv")
            psf[key] = ("dv", v)
            i += 1
    g["_y_dv"] = P.val("dv")
    if tap("ym", [128, DCH * TLOC], BF16):
        P.wait("gpsimd", "dv", g["_y_dv"])
        dma(P, "gpsimd", "qtap", g["tap_ym"], g["YB"][:])

    gab = _bias_ap(g, "a_ga_b")
    gxb = _bias_ap(g, "a_gx_b")
    spm8 = _bias_ap(g, "a_a_param")
    pool.prefetch(g["a_ga_W"][0, :, :], DCH * 128)
    pool2.prefetch(g["a_gx_W"][0, :, :], DCH * 128)
    for oh in range(DCH):
        if oh + 1 < DCH:
            pool.prefetch(g["a_ga_W"][oh + 1, :, :], DCH * 128)
            pool2.prefetch(g["a_gx_W"][oh + 1, :, :], DCH * 128)
        else:
            pool.prefetch(g["a_left_W"][0, :, :], DCH * 128)
        bga, vga = pool.take()
        bgx, vgx = pool2.take()
        pma = (g["PM0"], g["PMC"])[oh % 2]
        pmx = (g["PM1"], g["PMD"])[oh % 2]
        keya, keyx = "ga" + str(oh % 2), "gx" + str(oh % 2)
        vmm_grp = []
        for kk, (bname, vv, pm_, pl_) in enumerate(
                [(bga, vga, pma, pool), (bgx, vgx, pmx, pool2)]):
            key = (keya, keyx)[kk]
            if psf.get(key):
                P.wait("tensor", psf[key][0], psf[key][1])
            P.wait("tensor", pl_.sems[bname], vv)
            if oh == 0 and kk == 0:
                P.wait("tensor", "dv", g["_y_dv"])
            for k in range(DCH):
                for hb in range(BB):
                    P.op("tensor", lambda e, k_=k, b_=bname, pm2=pm_, hb_=hb:
                         e.matmul(pm2[:, hb_ * 512:hb_ * 512 + 512],
                                  g[b_][:, k_ * 128:(k_ + 1) * 128],
                                  g["YB"][:, k_ * TLOC + hb_ * 512:
                                          k_ * TLOC + hb_ * 512 + 512],
                                  start=(k_ == 0), stop=(k_ == DCH - 1)),
                         inc="mm" if (k == DCH - 1 and hb == BB - 1) else None)
            pl_.consumed(bname, "mm", P.val("mm"))
            vmm_grp.append(P.val("mm"))
        for ss, skey in ((g["SCR"], "scA"), (g["SCR3"], "scB")):
            if psf.get(skey):
                P.wait("scalar", psf[skey][0], psf[skey][1])
        P.wait("scalar", "mm", vmm_grp[0])
        va1 = P.op("scalar", lambda e, oh_=oh, pm_=pma:
                   e.activation(out=g["SCR"][:], in_=pm_[:, 0:512],
                                func=AF.Sigmoid, bias=gab[:, oh_:oh_ + 1]),
                   inc="ac")
        va1b = P.op("scalar", lambda e, oh_=oh, pm_=pma:
                    e.activation(out=g["SCR3"][:], in_=pm_[:, 512:1024],
                                 func=AF.Sigmoid, bias=gab[:, oh_:oh_ + 1]),
                    inc="ac")
        P.wait("scalar", "mm", vmm_grp[1])
        va2 = P.op("scalar", lambda e, oh_=oh, pm_=pmx:
                   e.activation(out=g["IH"][:, oh_ * 1024:oh_ * 1024 + 512],
                                in_=pm_[:, 0:512],
                                func=AF.Sigmoid, bias=gxb[:, oh_:oh_ + 1]),
                   inc="ac")
        va2b = P.op("scalar", lambda e, oh_=oh, pm_=pmx:
                    e.activation(out=g["IH"][:, oh_ * 1024 + 512:oh_ * 1024 + 1024],
                                 in_=pm_[:, 512:1024],
                                 func=AF.Sigmoid, bias=gxb[:, oh_:oh_ + 1]),
                    inc="ac")
        psf[keya] = ("ac", va1b)
        psf[keyx] = ("ac", va2b)
        P.wait("vector", "ac", va1)
        vla0 = P.op("vector", lambda e, oh_=oh:
                    e.tensor_scalar_mul(
                        g["LAB"][:, oh_ * TLOC: oh_ * TLOC + 512],
                        g["SCR"][:], spm8[:, oh_:oh_ + 1]), inc="dv")
        P.wait("vector", "ac", va1b)
        vla1 = P.op("vector", lambda e, oh_=oh:
                    e.tensor_scalar_mul(
                        g["LAB"][:, oh_ * TLOC + 512: oh_ * TLOC + 1024],
                        g["SCR3"][:], spm8[:, oh_:oh_ + 1]), inc="dv")
        psf["scA"] = ("dv", vla0)
        psf["scB"] = ("dv", vla1)
    for gb in range(2 * DCH):
        oh, hb = gb // 2, gb % 2
        la_ap = g["LAB"][:, oh * TLOC + hb * 512: oh * TLOC + hb * 512 + 512]
        scr_m = g["SCR"] if gb % 2 == 0 else g["SCR3"]
        key = "scr" + str(gb % 2)
        if psf.get(key):
            P.wait("scalar", psf[key][0], psf[key][1])
        if gb == 0:
            P.wait("scalar", "dv")
        P.op("scalar", lambda e, la_=la_ap, s_=scr_m:
             e.activation(out=s_[:], in_=la_, func=AF.Exp, scale=2.0), inc="ac")
        P.wait("scalar", "ac")
        P.op("scalar", lambda e, s_=scr_m:
             e.activation(out=s_[:], in_=s_[:], func=AF.Ln, scale=-1.0,
                          bias=_cst(g, 1.0)), inc="ac")
        P.wait("scalar", "ac")
        vmu = P.op("scalar", lambda e, s_=scr_m:
                   e.activation(out=s_[:], in_=s_[:], func=AF.Exp, scale=0.5),
                   inc="ac")
        P.wait("vector", "ac", vmu)
        P.op("vector", lambda e, oh_=oh, hb_=hb:
             e.tensor_mul(g["SCR2"][:],
                          g["IH"][:, oh_ * 1024 + hb_ * 512:
                                  oh_ * 1024 + hb_ * 512 + 512],
                          g["YB"][:, oh_ * TLOC + hb_ * 512:
                                  oh_ * TLOC + hb_ * 512 + 512]), inc="dv")
        P.wait("vector", "dv")
        vb = P.op("vector", lambda e, oh_=oh, hb_=hb, s_=scr_m:
                  e.tensor_mul(g["BBT"][:, oh_ * TLOC + hb_ * 512:
                                        oh_ * TLOC + hb_ * 512 + 512],
                               g["SCR2"][:], s_[:]), inc="dv")
        psf[key] = ("dv", vb)
    g["_b_dv"] = P.val("dv")
    g["_gates_ac"] = P.val("ac")
    lb = _bias_ap(g, "a_left_b")
    for oh in range(DCH):
        if oh + 1 < DCH:
            pool.prefetch(g["a_left_W"][oh + 1, :, :], DCH * 128)
        bn, lv = pool.take()
        pm = (g["PM0"], g["PM1"])[oh % 2]
        key = ("PM0", "PM1")[oh % 2]
        if oh < 2:
            P.wait("tensor", "ac", g["_gates_ac"])
        if psf.get(key):
            P.wait("tensor", psf[key][0], psf[key][1])
        P.wait("tensor", pool.sems[bn], lv)
        for k in range(DCH):
            for ri, (po, mo, wd) in enumerate([(0, 0, 512), (512, 512, 512)]):
                P.op("tensor", lambda e, k_=k, po_=po, mo_=mo, wd_=wd, pm_=pm, bn_=bn:
                     e.matmul(pm_[:, po_:po_ + wd_],
                              g[bn_][:, k_ * 128:(k_ + 1) * 128],
                              g["XN"][:, k_ * XNROWS + mo_: k_ * XNROWS + mo_ + wd_],
                              start=(k_ == 0), stop=(k_ == DCH - 1)),
                     inc="mm" if (k == DCH - 1 and ri == 1) else None)
        pool.consumed(bn, "mm", P.val("mm"))
        P.wait("scalar", "mm")
        if oh == 0:
            P.wait("scalar", "dv", g["_b_dv"])
            P.wait("scalar", "qtap")
        for p0 in (0, 512):
            P.op("scalar", lambda e, oh_=oh, p0_=p0, pm_=pm:
                 e.activation(out=g["YB"][:, oh_ * TLOC + p0_: oh_ * TLOC + p0_ + 512],
                              in_=pm_[:, p0_:p0_ + 512], func=AF.Gelu,
                              bias=lb[:, oh_:oh_ + 1]), inc="ac")
        psf[key] = ("ac", P.val("ac"))
    g["_left_ac"] = P.val("ac")
    g["_left_mm"] = P.val("mm")
    if tap("la", [128, DCH * TLOC], BF16):
        P.wait("gpsimd", "dv", g["_b_dv"])
        dma(P, "gpsimd", "qtap", g["tap_la"], g["LAB"][:])
    if tap("bb", [128, DCH * TLOC], BF16):
        P.wait("gpsimd", "dv", g["_b_dv"])
        dma(P, "gpsimd", "qtap", g["tap_bb"], g["BBT"][:])

    for gb in range(32):
        oh, b = gb // 2, gb % 2
        P.op("vector", lambda e, oh_=oh, b_=b, gb_=gb:
             e.reduce_sum(g["AB"][:, gb_:gb_ + 1],
                          g["LAB"][:, oh_ * TLOC + b_ * 512:
                                   oh_ * TLOC + b_ * 512 + 512], axis=AXX),
             inc="dv")
    P.wait("scalar", "dv")
    P.op("scalar", lambda e: e.activation(out=g["AB"][:, 0:32], in_=g["AB"][:, 0:32],
                                          func=AF.Exp), inc="ac")
    vab = P.val("ac")
    for gb in range(32):
        oh, b = gb // 2, gb % 2
        scr_a = g["SCR"] if gb % 2 == 0 else g["SCR3"]
        key = "scn" + str(gb % 2)
        if psf.get(key):
            P.wait("scalar", psf[key][0], psf[key][1])
        P.op("scalar", lambda e, oh_=oh, b_=b, s_=scr_a:
             e.activation(out=s_[:], in_=g["LAB"][:, oh_ * TLOC + b_ * 512:
                                                  oh_ * TLOC + b_ * 512 + 512],
                          func=AF.Exp), inc="ac")
        P.wait("vector", "ac")
        P.op("vector", lambda e, oh_=oh, b_=b, s_=scr_a:
             e.tensor_tensor_scan(out=g["SCR2"][:], data0=s_[:],
                                  data1=g["BBT"][:, oh_ * TLOC + b_ * 512:
                                                 oh_ * TLOC + b_ * 512 + 512],
                                  initial=0.0, op0=ALU.mult, op1=ALU.add),
             inc="dv")
        P.wait("vector", "dv")
        vcp = P.op("vector", lambda e, gb_=gb:
                   e.tensor_copy(out=g["AB"][:, 32 + gb_:33 + gb_],
                                 in_=g["SCR2"][:, 511:512]), inc="dv")
        psf[key] = ("dv", vcp)
    P.wait("gpsimd", "ac", vab)
    P.wait("gpsimd", "dv")
    vci = dma(P, "gpsimd", "q2", g["d_cin"], g["AB"][:])
    P.wait("gpsimd", "q2", vci)
    P.op("gpsimd", lambda e: e.collective_compute(
        "AllGather", ALU.bypass, replica_groups=[list(range(NC))],
        ins=[g["d_cin"].tensor.ap().opt()], outs=[g["d_cout"].tensor.ap().opt()]),
        inc="cc")
    P.wait("gpsimd", "cc")
    vcar = dma(P, "gpsimd", "q3", g["CAR"][:], g["d_cout"])
    P.wait("vector", "q3", vcar)
    P.op("vector", lambda e: e.memset(g["HPRE"][:, 0:32], 0.0), inc="dv")
    for r in range(NC):
        P.wait("vector", "dv")
        P.op("vector", lambda e, r_=r:
             e.tensor_mul(g["HPRE"][:, (r_ + 1) * 32:(r_ + 2) * 32],
                          g["CAR"][:, r_ * 64: r_ * 64 + 32],
                          g["HPRE"][:, r_ * 32:(r_ + 1) * 32]), inc="dv")
        P.wait("vector", "dv")
        P.op("vector", lambda e, r_=r:
             e.tensor_add(g["HPRE"][:, (r_ + 1) * 32:(r_ + 2) * 32],
                          g["HPRE"][:, (r_ + 1) * 32:(r_ + 2) * 32],
                          g["CAR"][:, r_ * 64 + 32: r_ * 64 + 64]), inc="dv")
    P.op("vector", lambda e: e.memset(g["HMINE"][:], 0.0), inc="dv")
    for r in range(NC):
        P.wait("vector", "dv")
        P.op("vector", lambda e, r_=r:
             e.scalar_tensor_tensor(out=g["HMINE"][:],
                                    in0=g["HPRE"][:, r_ * 32:(r_ + 1) * 32],
                                    scalar=g["cROH"][:, r_:r_ + 1],
                                    in1=g["HMINE"][:], op0=ALU.mult, op1=ALU.add),
             inc="dv")
    P.wait("vector", "qtap")
    P.wait("vector", "mm", g.get("_left_mm", 0))
    for gb in range(32):
        oh, b = gb // 2, gb % 2
        scr_a = g["SCR"] if gb % 2 == 0 else g["SCR3"]
        key = "scn" + str(gb % 2)
        if psf.get(key):
            P.wait("scalar", psf[key][0], psf[key][1])
        if gb == 0:
            P.wait("scalar", "dv")
        P.op("scalar", lambda e, oh_=oh, b_=b, s_=scr_a:
             e.activation(out=s_[:], in_=g["LAB"][:, oh_ * TLOC + b_ * 512:
                                                  oh_ * TLOC + b_ * 512 + 512],
                          func=AF.Exp), inc="ac")
        P.wait("vector", "ac")
        vsc = P.op("vector", lambda e, oh_=oh, b_=b, gb_=gb, s_=scr_a:
                   e.tensor_tensor_scan(
                       out=g["RGT"][:, oh_ * TLOC + b_ * 512:
                                    oh_ * TLOC + b_ * 512 + 512],
                       data0=s_[:],
                       data1=g["BBT"][:, oh_ * TLOC + b_ * 512:
                                      oh_ * TLOC + b_ * 512 + 512],
                       initial=g["HMINE"][:, gb_:gb_ + 1],
                       op0=ALU.mult, op1=ALU.add), inc="dv")
        psf[key] = ("dv", vsc)
    g["_scan2_dv"] = P.val("dv")
    if tap("right", [128, DCH * TLOC], BF16):
        P.wait("gpsimd", "dv", g["_scan2_dv"])
        dma(P, "gpsimd", "qtap", g["tap_right"], g["RGT"][:])
    P.wait("vector", "ac", g["_left_ac"])
    for ch in range(DCH):
        vmu = P.op("vector", lambda e, ch_=ch:
                   e.tensor_mul(g["YB"][:, ch_ * TLOC:(ch_ + 1) * TLOC],
                                g["YB"][:, ch_ * TLOC:(ch_ + 1) * TLOC],
                                g["RGT"][:, ch_ * TLOC:(ch_ + 1) * TLOC]), inc="dv")
    g["_lr_dv"] = P.val("dv")
    if tap("lr", [128, DCH * TLOC], BF16):
        P.wait("gpsimd", "dv", g["_lr_dv"])
        dma(P, "gpsimd", "qtap", g["tap_lr"], g["YB"][:])


def _x1c(g, ch):
    if ch < 8:
        return g["X1a"][:, ch * 1024:(ch + 1) * 1024]
    return g["X1b"][:, (ch - 8) * 1024:(ch - 7) * 1024]


def _p4(nc, P, g, tap):
    pool = g["_wpool"]
    psf = g["_psf"]
    ob = _bias_ap(g, "a_out_b")
    n2w = _bias_ap(g, "a_norm2_w")

    P.wait("scalar", "qp", g["_p1_qt"])
    P.wait("scalar", "qc", g["_p1_qt2"])
    P.wait("scalar", "qtap")
    P.wait("vector", "qtap")
    for oh in range(DCH):
        xbuf = ("XLT0", "XLT1")[oh % 2]
        lbuf = ("SQ4a", "SQ4b")[oh % 2]
        qsem = ("q4", "q5")[oh % 2]
        key = "xlt" + str(oh % 2)
        if psf.get(key):
            P.wait("scalar", psf[key][0], psf[key][1])
        vxl = dma(P, "scalar", qsem, g[xbuf][:, 0:1024],
                  g["d_xln_park"][:, oh * 128:(oh + 1) * 128], transpose=True)
        vxl = dma(P, "scalar", qsem, g[lbuf].bitcast(BF16)[:, 0:1024],
                  g["d_xln_lo"][:, oh * 128:(oh + 1) * 128], transpose=True)
        if oh == 0:
            pool.prefetch(g["a_out_W"][0, :, :], DCH * 128)
        if oh + 1 < DCH:
            pool.prefetch(g["a_out_W"][oh + 1, :, :], DCH * 128)
        bn, lv = pool.take()
        pm = (g["PM0"], g["PM1"])[oh % 2]
        pkey = "PM" + str(oh % 2)
        if psf.get(pkey):
            P.wait("tensor", psf[pkey][0], psf[pkey][1])
        P.wait("tensor", pool.sems[bn], lv)
        if oh == 0:
            P.wait("tensor", "dv", g["_lr_dv"])
        for k in range(DCH):
            for ri, (po, mo, wd) in enumerate([(0, 0, 512), (512, 512, 512)]):
                P.op("tensor", lambda e, k_=k, po_=po, mo_=mo, wd_=wd, pm_=pm, bn_=bn:
                     e.matmul(pm_[:, po_:po_ + wd_],
                              g[bn_][:, k_ * 128:(k_ + 1) * 128],
                              g["YB"][:, k_ * TLOC + mo_: k_ * TLOC + mo_ + wd_],
                              start=(k_ == 0), stop=(k_ == DCH - 1)),
                     inc="mm" if (k == DCH - 1 and ri == 1) else None)
        pool.consumed(bn, "mm", P.val("mm"))
        P.wait("vector", "mm")
        P.wait("vector", qsem, vxl)
        P.op("vector", lambda e, oh_=oh, pm_=pm, b_=xbuf:
             e.scalar_tensor_tensor(out=_x1c(g, oh_), in0=pm_[:],
                                    scalar=ob[:, oh_:oh_ + 1],
                                    in1=g[b_][:, 0:1024],
                                    op0=ALU.add, op1=ALU.add), inc="dv")
        vev = P.op("vector", lambda e, oh_=oh, b_=lbuf:
                   e.tensor_add(_x1c(g, oh_), _x1c(g, oh_),
                                g[b_].bitcast(BF16)[:, 0:1024]), inc="dv")
        psf[pkey] = ("dv", vev)
        psf[key] = ("dv", vev)
    g["_x1_dv"] = P.val("dv")
    if tap("x1a", [128, 8192]):
        P.wait("gpsimd", "dv", g["_x1_dv"])
        dma(P, "gpsimd", "qtap", g["tap_x1a"], g["X1a"])
    if tap("x1b", [128, 8192]):
        P.wait("gpsimd", "dv", g["_x1_dv"])
        dma(P, "gpsimd", "qtap", g["tap_x1b"], g["X1b"])

    P.wait("scalar", "dv", g["_x1_dv"])
    for half in range(2):
        for oh in range(DCH):
            sq = (g["SQ4a"], g["SQ4b"])[oh % 2]
            key = "sq" + str(oh % 2)
            if psf.get(key):
                P.wait("scalar", psf[key][0], psf[key][1])
            P.op("scalar", lambda e, oh_=oh, s_=sq, h_=half:
                 e.activation(out=s_[:, 0:512],
                              in_=_x1c(g, oh_)[:, h_ * 512:(h_ + 1) * 512],
                              func=AF.Square), inc="ac")
            P.wait("tensor", "ac")
            if oh == 0:
                P.wait("tensor", "dv")
            P.op("tensor", lambda e, s_=sq, oh_=oh:
                 e.matmul(g["PS1"][:], g["cONEc"][:], s_[:, 0:512],
                          start=(oh_ == 0), stop=(oh_ == DCH - 1)), inc="mm")
            psf[key] = ("mm", P.val("mm"))
        P.wait("vector", "mm")
        vss = P.op("vector", lambda e, h_=half:
                   e.tensor_copy(out=g["SS4"][0:1, h_ * 512:(h_ + 1) * 512],
                                 in_=g["PS1"][:]), inc="dv")
    P.wait("scalar", "dv", vss)
    P.op("scalar", lambda e: e.activation(out=g["SS4"][0:1, :], in_=g["SS4"][0:1, :],
                                          func=AF.Ln, scale=1.0 / D,
                                          bias=_cst(g, EPS_RMS_A, 0, 1)), inc="ac")
    P.wait("scalar", "ac")
    vrs = P.op("scalar", lambda e: e.activation(out=g["SS4"][0:1, :],
                                                in_=g["SS4"][0:1, :],
                                                func=AF.Exp, scale=-0.5), inc="ac")
    P.wait("tensor", "ac", vrs)
    for half in range(2):
        P.op("tensor", lambda e, h_=half:
             e.matmul(g["PM0"][:, h_ * 512:(h_ + 1) * 512], g["cONEr"][:],
                      g["SS4"][0:1, h_ * 512:(h_ + 1) * 512], start=True, stop=True),
             inc="mm")
    P.wait("vector", "mm")
    vrsb = P.op("vector", lambda e: e.tensor_copy(out=g["RSB"][:], in_=g["PM0"][:]),
                inc="dv")
    for oh in range(DCH):
        P.op("vector", lambda e, oh_=oh:
             e.scalar_tensor_tensor(out=g["XN2"][:, oh_ * 1024:(oh_ + 1) * 1024],
                                    in0=_x1c(g, oh_), scalar=n2w[:, oh_:oh_ + 1],
                                    in1=g["RSB"][:], op0=ALU.mult, op1=ALU.mult),
             inc="dv")
    g["_xn2_dv"] = P.val("dv")
    g["_pm0_free"] = ("dv", vrsb)
    if tap("xn2", [128, DCH * TLOC], BF16):
        P.wait("gpsimd", "dv", g["_xn2_dv"])
        dma(P, "gpsimd", "qtap", g["tap_xn2"], g["XN2"][:])


def _p5(nc, P, g, tap):
    pool, pool2 = g["_wpool"], g["_wpool2"]
    psf = g["_psf"]
    u1b = _bias_ap(g, "a_up1_b")
    u2b = _bias_ap(g, "a_up2_b")
    dnb = _bias_ap(g, "a_down_b")

    P.wait("scalar", "qtap")
    glu_w = []
    pool.prefetch(g["a_up1_W"][0, :, :], DCH * 128)
    pool2.prefetch(g["a_up2_W"][0, :, :], DCH * 128)
    for of in range(FCH):
        if of + 1 < FCH:
            pool.prefetch(g["a_up1_W"][of + 1, :, :], DCH * 128)
            pool2.prefetch(g["a_up2_W"][of + 1, :, :], DCH * 128)
        b1, v1 = pool.take()
        b2, v2 = pool2.take()
        pmg = (g["PM0"], g["PM1"])[of % 2]
        pmu = (g["PN0"], g["PN1"])[of % 2]
        keyg, keyu = "pmg" + str(of % 2), "pmu" + str(of % 2)
        if psf.get(keyg):
            P.wait("tensor", psf[keyg][0], psf[keyg][1])
        P.wait("tensor", pool.sems[b1], v1)
        if of == 0:
            P.wait("tensor", "dv", g["_xn2_dv"])
        for k in range(DCH):
            for ri, (po, mo, wd) in enumerate([(0, 0, 512), (512, 512, 512)]):
                P.op("tensor", lambda e, k_=k, po_=po, mo_=mo, wd_=wd, pm_=pmg, b_=b1:
                     e.matmul(pm_[:, po_:po_ + wd_],
                              g[b_][:, k_ * 128:(k_ + 1) * 128],
                              g["XN2"][:, k_ * TLOC + mo_: k_ * TLOC + mo_ + wd_],
                              start=(k_ == 0), stop=(k_ == DCH - 1)),
                     inc="mm" if (k == DCH - 1 and ri == 1) else None)
        pool.consumed(b1, "mm", P.val("mm"))
        P.wait("scalar", "mm")
        scr_g = (g["SCR"], g["SCR3"])[of % 2]
        skey = "scr5" + str(of % 2)
        if psf.get(skey):
            P.wait("scalar", psf[skey][0], psf[skey][1])
        vg0 = P.op("scalar", lambda e, of_=of, pm_=pmg, s_=scr_g:
                   e.activation(out=s_[:], in_=pm_[:, 0:512], func=AF.Gelu,
                                bias=u1b[:, of_:of_ + 1]), inc="ac")
        vg1 = P.op("scalar", lambda e, of_=of, pm_=pmg:
                   e.activation(out=g["SCR2"][:], in_=pm_[:, 512:1024], func=AF.Gelu,
                                bias=u1b[:, of_:of_ + 1]), inc="ac")
        psf[keyg] = ("ac", vg1)
        gst = (g["GSTG0"], g["GSTG1"])[of % 2]
        gkey = "gst" + str(of % 2)
        for hb in range(2):
            if psf.get(keyu):
                P.wait("tensor", psf[keyu][0], psf[keyu][1])
            P.wait("tensor", pool2.sems[b2], v2)
            for k in range(DCH):
                P.op("tensor", lambda e, k_=k, hb_=hb, pm_=pmu, b_=b2:
                     e.matmul(pm_[:], g[b_][:, k_ * 128:(k_ + 1) * 128],
                              g["XN2"][:, k_ * TLOC + hb_ * 512:
                                       k_ * TLOC + hb_ * 512 + 512],
                              start=(k_ == 0), stop=(k_ == DCH - 1)),
                     inc="mm" if k == DCH - 1 else None)
            P.wait("vector", "mm")
            P.wait("vector", "ac", (vg0, vg1)[hb])
            if hb == 0 and psf.get(gkey):
                P.wait("vector", psf[gkey][0], psf[gkey][1])
            vgl = P.op("vector", lambda e, of_=of, hb_=hb, pm_=pmu, gs_=gst,
                       s_=(scr_g, g["SCR2"])[hb]:
                       e.scalar_tensor_tensor(out=gs_[:, hb_ * 512:(hb_ + 1) * 512],
                                              in0=pm_[:], scalar=u2b[:, of_:of_ + 1],
                                              in1=s_[:], op0=ALU.add, op1=ALU.mult),
                       inc="dv")
            psf[keyu] = ("dv", vgl)
            psf["scr5" + str(of % 2) if hb == 0 else skey] = ("dv", vgl)
        pool2.consumed(b2, "mm", P.val("mm"))
        P.wait("scalar", "dv", vgl)
        gq = ("q7", "qo2")[of % 2]
        vgw = dma(P, "scalar", gq, g["d_glu"][of, :, :], gst[:, 0:TLOC])
        psf[gkey] = (gq, vgw)
    g["_glu_q7"] = P.val("q7")
    g["_glu_qo2"] = P.val("qo2")

    slots = [(g["PMA"], 0), (g["PMA"], 512), (g["PMB"], 0), (g["PMB"], 512),
             (g["PMC"], 0), (g["PMC"], 512), (g["PMD"], 0), (g["PMD"], 512)]
    P.wait("sync", "q7", g["_glu_q7"])
    P.wait("sync", "qo2", g["_glu_qo2"])
    P.wait("sync", "dv")
    P.wait("gpsimd", "dv")
    atn_q = 0
    for hb in range(2):
        for grp in range(2):
            if psf.get("down"):
                P.wait("tensor", psf["down"][0], psf["down"][1])
            def _ld_gb(kf_, hb_):
                gb_ = ("GB0", "GB1")[kf_ % 2]
                gq_ = ("q4", "q5")[kf_ % 2]
                gkey_ = "gb" + str(kf_ % 2)
                if psf.get(gkey_):
                    P.wait("sync", psf[gkey_][0], psf[gkey_][1])
                    psf[gkey_] = None
                v_ = dma(P, "sync", gq_, g[gb_][:, 0:512],
                         g["d_glu"][kf_, :, hb_ * 512:(hb_ + 1) * 512])
                wd_ = ("WD0", "WD1")[kf_ % 2]
                wq_ = ("d0", "d1")[kf_ % 2]
                wkey_ = "wd" + str(kf_ % 2)
                if psf.get(wkey_):
                    P.wait("gpsimd", psf[wkey_][0], psf[wkey_][1])
                    psf[wkey_] = None
                v2_ = dma(P, "gpsimd", wq_, g[wd_][:, 0:1024],
                          g["a_down_W_r"][kf_, :, grp * 1024:(grp + 1) * 1024])
                return (gb_, gq_, v_, wd_, wq_, v2_)
            pend_d = [_ld_gb(0, hb)]
            for kf in range(FCH):
                if kf + 1 < FCH:
                    pend_d.append(_ld_gb(kf + 1, hb))
                gb, gq, vgb, wd, wq, vwd = pend_d.pop(0)
                gkey = "gb" + str(kf % 2)
                wkey = "wd" + str(kf % 2)
                P.wait("tensor", gq, vgb)
                P.wait("tensor", wq, vwd)
                for j in range(8):
                    P.op("tensor", lambda e, j_=j, kf_=kf, gb_=gb, wd_=wd:
                         e.matmul(slots[j_][0][:, slots[j_][1]:slots[j_][1] + 512],
                                  g[wd_][:, j_ * 128:(j_ + 1) * 128],
                                  g[gb_][:, 0:512],
                                  start=(kf_ == 0), stop=(kf_ == FCH - 1)),
                         inc="mm" if j == 7 else None)
                mmv = P.val("mm")
                psf[gkey] = ("mm", mmv)
                psf[wkey] = ("mm", mmv)
            P.wait("vector", "mm")
            for j in range(8):
                oh = grp * 8 + j
                dst = (g["DSTG0"], g["DSTG1"])[j % 2]
                dkey = "dstg" + str(j % 2)
                if psf.get(dkey):
                    P.wait("vector", psf[dkey][0], psf[dkey][1])
                P.op("vector", lambda e, j_=j, oh_=oh, d_=dst, hb_=hb:
                     e.scalar_tensor_tensor(
                         out=d_[:, 0:512],
                         in0=slots[j_][0][:, slots[j_][1]:slots[j_][1] + 512],
                         scalar=dnb[:, oh_:oh_ + 1],
                         in1=_x1c(g, oh_)[:, hb_ * 512:(hb_ + 1) * 512],
                         op0=ALU.add, op1=ALU.add), inc="dv")
                vdv = P.val("dv")
                P.wait("scalar", "dv", vdv)
                dq = ("q6", "qo3")[j % 2]
                vq = dma(P, "scalar", dq, g["d_atn_park"][:, oh * TLOC + hb * 512:
                                                          oh * TLOC + hb * 512 + 512],
                         dst[:, 0:512])
                psf[dkey] = (dq, vq)
            psf["down"] = ("dv", P.val("dv"))
    g["_atn_q6"] = P.val("q6")
    g["_atn_qo3"] = P.val("qo3")


def _p6(nc, P, g, tap):
    psf = g["_psf"]
    P.wait("tensor", "dv")
    P.wait("tensor", "qc")
    _bcast_row(nc, P, g, g["f_w_router"], g["WRBC"], D)

    P.wait("sync", "q6", g["_atn_q6"])
    P.wait("sync", "qo3", g["_atn_qo3"])
    def atnt(tt):
        if tt < 4:
            return g["ATNTa"][:, tt * D:(tt + 1) * D]
        return g["ATNTb"][:, (tt - 4) * D:(tt - 3) * D]
    for oh in range(DCH):
        half = oh % 2
        qsem = ("q4", "q5")[half]
        key = "ath" + str(half)
        if psf.get(key):
            P.wait("sync", psf[key][0], psf[key][1])
        abuf = ("ATHF0", "ATHF1")[half]
        vld = dma(P, "sync", qsem, g[abuf][:],
                  g["d_atn_park"][:, oh * TLOC:(oh + 1) * TLOC])
        P.wait("tensor", qsem, vld)
        for tt in range(8):
            if psf.get("pxh"):
                P.wait("tensor", psf["pxh"][0], psf["pxh"][1])
            P.op("tensor", lambda e, tt_=tt, a_=abuf:
                 e.transpose(g["PX"][:], g[a_][:, tt_ * 128:(tt_ + 1) * 128],
                             g["cEYEf"][:]), inc="mm")
            P.wait("vector", "mm")
            vcp = P.op("vector", lambda e, oh_=oh, tt_=tt:
                       e.tensor_copy(out=atnt(tt_)[:, oh_ * 128:(oh_ + 1) * 128],
                                     in_=g["PX"][:]), inc="dv")
            psf["pxh"] = ("dv", vcp)
        psf[key] = ("dv", P.val("dv"))
    g["_atnt_dv"] = P.val("dv")

    x = g["x"]
    P.wait("scalar", "q6", g["_atn_q6"])
    P.wait("scalar", "qo3", g["_atn_qo3"])
    P.wait("vector", "q6", g["_atn_q6"])
    P.wait("vector", "qo3", g["_atn_qo3"])
    P.op("vector", lambda e: e.memset(g["EXT6"][:], 0.0), inc="dv")
    P.op("vector", lambda e: e.memset(g["EXT6b"][:], 0.0), inc="dv")
    xa_q, ex_q, ob_q, free_v = [0, 0], [0, 0], [0, 0], [0, 0]
    for tt in range(8):
        b, q = tt // 4, tt % 4
        par = tt % 2
        XT = (g["XTOK6"], g["XTOK6b"])[par][:]
        SQ = (g["SQ6"], g["SQ6b"])[par][:]
        HD = (g["HID6"], g["HID6b"])[par][:]
        TH = (g["TOKH"], g["TOKH6b"])[par]
        EX = (g["EXT6"], g["EXT6b"])[par][:]
        q8s = ("q8", "qx")[par]
        q6s = ("q6", "q7")[par]
        qo1s = ("qo1", "qo2")[par]
        qas = ("qa", "qb")[par]
        if free_v[par]:
            P.wait("sync", "dv", free_v[par])
        vld = dma(P, "sync", q8s, XT,
                  x[b, HALO + q * 128: HALO + (q + 1) * 128, :])
        P.wait("vector", q8s, vld)
        P.wait("scalar", q8s, vld)
        if tt == 0:
            P.wait("vector", "qtap")
        P.op("vector", lambda e, tt_=tt, x_=XT: e.tensor_add(
            x_, x_, atnt(tt_)), inc="dv")
        vadd = P.val("dv")
        if xa_q[par]:
            P.wait("vector", q6s, xa_q[par])
        P.op("vector", lambda e, x_=XT, t_=TH: e.tensor_copy(out=t_[:, 0:D],
                                                             in_=x_), inc="dv")
        vpk = P.val("dv")
        P.wait("scalar", "dv", vpk)
        dma(P, "scalar", q6s, g["d_xa_park"][tt * 128:(tt + 1) * 128, :],
            TH[:, 0:D])
        xa_q[par] = P.val(q6s)
        if ob_q[par]:
            P.wait("scalar", qo1s, ob_q[par])
            P.wait("vector", qo1s, ob_q[par])
        _ln_tile(nc, P, g, XT, 0, 128, EPS_LN, None, sq=SQ, out=HD, col=par,
                 pre_dv=vadd)
        P.op("vector", lambda e, s_=SQ, h_=HD: e.tensor_mul(s_, h_, g["WRBC"][:]),
             inc="dv")
        P.op("vector", lambda e, tt_=tt, s_=SQ: e.reduce_sum(
            g["SCO6"][:, tt_:tt_ + 1], s_, axis=AXX), inc="dv")
        P.op("vector", lambda e, s_=SQ, x_=XT, h_=HD: e.tensor_add(s_, x_, h_),
             inc="dv")
        vob = P.val("dv")
        free_v[par] = vob
        P.wait("scalar", "dv", vob)
        dma(P, "scalar", qo1s, g["out_base"][tt * 128:(tt + 1) * 128, :], SQ)
        ob_q[par] = P.val(qo1s)
        if ex_q[par]:
            P.wait("vector", qas, ex_q[par])
        for col, srcap in [(0, g["ST1"][:, par:par + 1]),
                           (1, g["ST2"][:, par:par + 1]),
                           (2, g["cTAU"]), (3, g["SCO6"]), (4, g["cROW"])]:
            if col in (2, 4):
                P.op("vector", lambda e, c_=col, s_=srcap, tt_=tt, e_=EX:
                     e.tensor_copy(out=e_[:, c_:c_ + 1], in_=s_[:, tt_:tt_ + 1]),
                     inc="dv")
            elif col == 3:
                P.op("vector", lambda e, c_=col, tt_=tt, e_=EX:
                     e.tensor_copy(out=e_[:, c_:c_ + 1],
                                   in_=g["SCO6"][:, tt_:tt_ + 1]), inc="dv")
            else:
                P.op("vector", lambda e, c_=col, s_=srcap, e_=EX:
                     e.tensor_copy(out=e_[:, c_:c_ + 1], in_=s_), inc="dv")
        vex = P.val("dv")
        P.wait("sync", "dv", vex)
        dma(P, "sync", qas, g["d_ex_park"][tt * 128:(tt + 1) * 128, :], EX)
        ex_q[par] = P.val(qas)
    g["_xa_q6v"] = P.val("q6")
    g["_xa_q7v"] = P.val("q7")
    psf2 = g["_psf"]
    if psf2.get("pxh"):
        P.wait("tensor", psf2["pxh"][0], psf2["pxh"][1])
    P.wait("tensor", "dv")
    P.op("tensor", lambda e: e.transpose(g["PX"][0:8, 0:128], g["SCO6"][:],
                                         g["cEYEf"][:]), inc="mm")
    P.wait("vector", "mm")
    P.op("vector", lambda e: e.tensor_copy(out=g["ST2"][0:8, 0:16] if False else
                                           g["SCO8R"][0:8, :], in_=g["PX"][0:8, 0:128]),
         inc="dv")
    P.wait("sync", "dv")
    vsc = dma(P, "sync", "qb", g["d_scin"].rearrange("(a b) o -> a (b o)", a=8),
              g["SCO8R"][0:8, :])
    P.wait("gpsimd", "qb", P.val("qb"))
    P.wait("gpsimd", "qa", P.val("qa"))
    P.wait("gpsimd", "qo1")
    P.wait("gpsimd", "qo2")
    P.op("gpsimd", lambda e: e.collective_compute(
        "AllGather", ALU.bypass, replica_groups=[list(range(NC))],
        ins=[g["d_scin"].tensor.ap().opt()], outs=[g["d_scag"].tensor.ap().opt()]),
        inc="cc")
    g["_p6_cc"] = P.val("cc")
    if tap("hidden", [128, D]):
        P.wait("gpsimd", "dv")
        dma(P, "gpsimd", "qtap", g["tap_hidden"], g["HID6"][:])
    if tap("xapark", [TLOC, D], BF16):
        P.wait("gpsimd", "q6", g["_xa_q6v"])
        P.wait("gpsimd", "q7", g["_xa_q7v"])
        dma(P, "gpsimd", "qtap", g["tap_xapark"], g["d_xa_park"])
    if tap("expark", [TLOC, 8]):
        P.wait("gpsimd", "qa")
        P.wait("gpsimd", "qb")
        dma(P, "gpsimd", "qtap", g["tap_expark"], g["d_ex_park"])
    if tap("scag", [NC * TLOC, 1]):
        P.wait("gpsimd", "cc", g["_p6_cc"])
        dma(P, "gpsimd", "qtap", g["tap_scag"], g["d_scag"])


def _p7(nc, P, g, tap):
    psf = g["_psf"]
    P.wait("sync", "cc", g["_p6_cc"])
    for b in range(BB):
        dma(P, "sync", "qd", g["SC7"][:, b * 32:(b + 1) * 32],
            g["d_scag"].rearrange("(po pi f o) -> (po pi) f o", po=8, pi=16, f=32)[
                :, :, 0] if False else
            _scag_ap(g, b))
    vsc7 = P.val("qd")
    P.op("vector", lambda e: e.memset(g["BIG7"][:], 4000.0), inc="dv")
    P.op("vector", lambda e: e.memset(g["ZERO7"][:], 0.0), inc="dv")
    lo, hi, mid = g["ST1"][0:1, 0:1], g["ST1"][0:1, 1:2], g["ST1"][0:1, 2:3]
    cnt, prd, tt7 = g["ST1"][0:1, 3:4], g["ST1"][0:1, 4:5], g["ST1"][0:1, 5:6]
    cgt, need = g["ST1"][0:1, 6:7], g["ST1"][0:1, 7:8]
    row128 = g["ROW1"][:]
    P.wait("vector", "qd", vsc7)

    def mmt_colrow(col_ap, out_row):
        P.wait("tensor", "dv")
        P.op("tensor", lambda e: e.matmul(g["PS1"][0:1, 0:128], col_ap,
                                          g["cEYEf"][:], start=True, stop=True),
             inc="mm")
        P.wait("vector", "mm")
        P.op("vector", lambda e: e.tensor_copy(out=out_row, in_=g["PS1"][0:1, 0:128]),
             inc="dv")

    def bcast_col(scalar_ap, out_col):
        P.wait("tensor", "dv")
        P.op("tensor", lambda e: e.matmul(g["PX"][:, 0:1], g["cONEr"][:], scalar_ap,
                                          start=True, stop=True), inc="mm")
        P.wait("vector", "mm")
        P.op("vector", lambda e: e.tensor_copy(out=out_col, in_=g["PX"][:, 0:1]),
             inc="dv")

    def reduce_total(col_ap, out_sc):
        P.wait("tensor", "dv")
        P.op("tensor", lambda e: e.matmul(g["PS1"][0:1, 128:129], col_ap,
                                          g["cONEc"][:], start=True, stop=True),
             inc="mm")
        P.wait("vector", "mm")
        P.op("vector", lambda e: e.tensor_copy(out=out_sc,
                                               in_=g["PS1"][0:1, 128:129]), inc="dv")

    def st(b, c):
        return (g["ST1"], g["ST2"])[b][0:1, c:c + 1]
    scb_l = [g["SC7"][:, 0:32], g["SC7"][:, 32:64]]
    cnt_l = [g["CNTC7"][:], g["MG7"][:, 0:1]]
    midb_l = [g["MIDB7"][:], g["MEQ7"][:, 0:1]]
    dummy_l = [g["T7a"][:], g["MASK7"][:]]
    px_l = [g["PX"][:, 0:1], g["PX"][:, 1:2]]
    ps_l = [g["PS1"][0:1, 128:129], g["PS1"][0:1, 129:130]]
    for b in range(BB):
        scb = scb_l[b]
        lo_b, hi_b = st(b, 0), st(b, 1)
        P.op("vector", lambda e, s_=scb: e.reduce_max(g["CNTC7"][:], s_, axis=AXX), inc="dv")
        mmt_colrow(g["CNTC7"][:], row128)
        P.op("vector", lambda e, h_=hi_b: e.reduce_max(h_, row128, axis=AXX), inc="dv")
        P.op("vector", lambda e, s_=scb: e.tensor_scalar_mul(g["T7a"][:], s_, -1.0), inc="dv")
        P.op("vector", lambda e: e.reduce_max(g["CNTC7"][:], g["T7a"][:], axis=AXX),
             inc="dv")
        mmt_colrow(g["CNTC7"][:], row128)
        P.op("vector", lambda e, l_=lo_b: e.reduce_max(l_, row128, axis=AXX), inc="dv")
        P.op("vector", lambda e, l_=lo_b: e.tensor_scalar_mul(l_, l_, -1.0), inc="dv")
    for it in range(BISECT_ITERS):
        mids, gts = [0, 0], [0, 0]
        for b in range(BB):
            P.op("vector", lambda e, h_=st(b, 1), l_=st(b, 0), t_=st(b, 5):
                 e.tensor_sub(t_, h_, l_), inc="dv")
            mids[b] = P.op("vector", lambda e, m_=st(b, 2), t_=st(b, 5), l_=st(b, 0):
                           e.scalar_tensor_tensor(out=m_, in0=t_, scalar=0.5, in1=l_,
                                                  op0=ALU.mult, op1=ALU.add), inc="dv")
        for b in range(BB):
            P.wait("tensor", "dv", mids[b])
            P.op("tensor", lambda e, m_=st(b, 2), p_=px_l[b]:
                 e.matmul(p_, g["cONEr"][:], m_, start=True, stop=True), inc="mm")
            P.wait("vector", "mm")
            P.op("vector", lambda e, p_=px_l[b], o_=midb_l[b]:
                 e.tensor_copy(out=o_, in_=p_), inc="dv")
            gts[b] = P.op("vector", lambda e, s_=scb_l[b], d_=dummy_l[b],
                          m_=midb_l[b], c_=cnt_l[b]:
                          e.tensor_scalar(out=d_, in0=s_, scalar1=m_, scalar2=0.0,
                                          op0=ALU.is_gt, op1=ALU.add, accum_out=c_),
                          inc="dv")
        for b in range(BB):
            P.wait("tensor", "dv", gts[b])
            P.op("tensor", lambda e, c_=cnt_l[b], p_=ps_l[b]:
                 e.matmul(p_, c_, g["cONEc"][:], start=True, stop=True), inc="mm")
            P.wait("vector", "mm")
            P.op("vector", lambda e, p_=ps_l[b], o_=st(b, 3):
                 e.tensor_copy(out=o_, in_=p_), inc="dv")
            P.op("vector", lambda e, c_=st(b, 3), o_=st(b, 4):
                 e.tensor_scalar(out=o_, in0=c_, scalar1=float(KSEL) - 0.5,
                                 scalar2=None, op0=ALU.is_gt), inc="dv")
            P.op("vector", lambda e, m_=st(b, 2), l_=st(b, 0), t_=st(b, 5):
                 e.tensor_sub(t_, m_, l_), inc="dv")
            P.op("vector", lambda e, t_=st(b, 5), p_=st(b, 4), l_=st(b, 0):
                 e.scalar_tensor_tensor(out=l_, in0=t_, scalar=p_, in1=l_,
                                        op0=ALU.mult, op1=ALU.add), inc="dv")
            P.op("vector", lambda e, h_=st(b, 1), m_=st(b, 2), t_=st(b, 5):
                 e.tensor_sub(t_, h_, m_), inc="dv")
            P.op("vector", lambda e, t_=st(b, 5), p_=st(b, 4), m_=st(b, 2),
                 h_=st(b, 1):
                 e.scalar_tensor_tensor(out=h_, in0=t_, scalar=p_, in1=m_,
                                        op0=ALU.mult, op1=ALU.add), inc="dv")
    for b in range(BB):
        scb = scb_l[b]
        hi, cgt, need = st(b, 1), st(b, 6), st(b, 7)
        if b:
            P.wait("vector", "qd")
        bcast_col(hi, g["HIB7"][:])
        P.op("vector", lambda e, s_=scb: e.tensor_scalar(
            out=g["MG7"][:], in0=s_, scalar1=g["HIB7"][:], scalar2=0.0,
            op0=ALU.is_gt, op1=ALU.add, accum_out=g["CNTC7"][:]), inc="dv")
        reduce_total(g["CNTC7"][:], cgt)
        P.op("vector", lambda e, s_=scb: e.tensor_scalar(
            out=g["MEQ7"][:], in0=s_, scalar1=g["HIB7"][:], scalar2=0.0,
            op0=ALU.is_equal, op1=ALU.add, accum_out=g["CNTC7"][:]), inc="dv")
        P.wait("tensor", "dv")
        P.op("tensor", lambda e: e.matmul(g["PX"][:, 0:1], g["cLTRI"][:],
                                          g["CNTC7"][:], start=True, stop=True),
             inc="mm")
        P.wait("vector", "mm")
        P.op("vector", lambda e: e.tensor_copy(out=g["EOFF7"][:], in_=g["PX"][:, 0:1]),
             inc="dv")
        P.op("vector", lambda e: e.tensor_tensor_scan(
            out=g["INCL7"][:], data0=g["MEQ7"][:], data1=g["ZERO7"][:], initial=0.0,
            op0=ALU.add, op1=ALU.add), inc="dv")
        P.op("vector", lambda e: e.tensor_scalar_add(g["INCL7"][:], g["INCL7"][:],
                                                     g["EOFF7"][:]), inc="dv")
        P.op("vector", lambda e, n_=need, c_=cgt: e.tensor_scalar(
            out=n_, in0=c_, scalar1=-1.0, scalar2=float(KSEL), op0=ALU.mult,
            op1=ALU.add), inc="dv")
        bcast_col(need, g["NEEDB7"][:])
        P.op("vector", lambda e: e.tensor_scalar(
            out=g["T7b"][:], in0=g["INCL7"][:], scalar1=g["NEEDB7"][:], scalar2=None,
            op0=ALU.is_le), inc="dv")
        P.op("vector", lambda e: e.tensor_mul(g["T7b"][:], g["T7b"][:], g["MEQ7"][:]),
             inc="dv")
        P.op("vector", lambda e: e.tensor_add(g["MASK7"][:], g["MG7"][:], g["T7b"][:]),
             inc="dv")
        P.op("vector", lambda e: e.reduce_sum(g["CNTC7"][:], g["MASK7"][:], axis=AXX),
             inc="dv")
        P.wait("tensor", "dv")
        P.op("tensor", lambda e: e.matmul(g["PX"][:, 0:1], g["cLTRI"][:],
                                          g["CNTC7"][:], start=True, stop=True),
             inc="mm")
        P.wait("vector", "mm")
        P.op("vector", lambda e: e.tensor_copy(out=g["EOFF7"][:], in_=g["PX"][:, 0:1]),
             inc="dv")
        P.op("vector", lambda e: e.tensor_tensor_scan(
            out=g["INCL7"][:], data0=g["MASK7"][:], data1=g["ZERO7"][:], initial=0.0,
            op0=ALU.add, op1=ALU.add), inc="dv")
        P.op("vector", lambda e: e.tensor_scalar_add(g["INCL7"][:], g["INCL7"][:],
                                                     g["EOFF7"][:]), inc="dv")
        P.op("vector", lambda e: e.tensor_sub(g["CEX7"][:], g["INCL7"][:],
                                              g["MASK7"][:]), inc="dv")
        if b == 1:
            P.op("vector", lambda e: e.tensor_scalar_add(g["CEX7"][:], g["CEX7"][:],
                                                         float(KSEL)), inc="dv")
        P.op("vector", lambda e: e.tensor_mul(g["CEX7"][:], g["CEX7"][:],
                                              g["MASK7"][:]), inc="dv")
        P.op("vector", lambda e: e.tensor_scalar(
            out=g["T7b"][:], in0=g["MASK7"][:], scalar1=-4000.0, scalar2=4000.0,
            op0=ALU.mult, op1=ALU.add), inc="dv")
        P.op("vector", lambda e: e.tensor_add(g["CEX7"][:], g["CEX7"][:],
                                              g["T7b"][:]), inc="dv")
        P.op("vector", lambda e: e.tensor_copy(out=g["IDXI7"][:], in_=g["CEX7"][:]),
             inc="dv")
        P.op("vector", lambda e: e.tensor_copy(
            out=g["STAG7"].rearrange("p (f j) -> p f j", j=4),
            in_=g["IDXI7"][:].unsqueeze(2).to_broadcast([128, 32, 4])), inc="dv")
        vst = P.val("dv")
        P.wait("sync", "dv", vst)
        dma(P, "sync", "qd", _slotrec_ap(g, b), g["STAG7"][:])
        if b == 0 and tap("seldbg", [128, 176]):
            P.wait("gpsimd", "dv", vst)
            dma(P, "gpsimd", "qtap", g["tap_seldbg"][:, 0:64], g["SC7"][:])
            dma(P, "gpsimd", "qtap", g["tap_seldbg"][:, 64:80], g["ST1"][:])
            dma(P, "gpsimd", "qtap", g["tap_seldbg"][:, 80:112], g["MASK7"][:])
            dma(P, "gpsimd", "qtap", g["tap_seldbg"][:, 112:144], g["CEX7"][:])
            dma(P, "gpsimd", "qtap", g["tap_seldbg"][:, 144:176], g["MG7"][:])
            P.wait("vector", "qtap")
    g["_slotrec_qd"] = P.val("qd")

    P.op("vector", lambda e: e.memset(g["P7A"][:], 0.0), inc="dv")
    P.op("vector", lambda e: e.memset(g["P7E"][:], 0.0), inc="dv")
    vz = P.val("dv")
    P.wait("gpsimd", "dv", vz)
    for r in range(NC):
        dma(P, "gpsimd", "qe", g["d_send"][r * 128:(r + 1) * 128, :], g["P7A"][:])
        dma(P, "gpsimd", "qe", g["d_send_ex"][r * 128:(r + 1) * 128, :], g["P7E"][:])
    vzs = P.val("qe")
    P.wait("gpsimd", "qd", g["_slotrec_qd"])
    P.wait("gpsimd", "qe", vzs)
    P.wait("gpsimd", "q6", g["_xa_q6v"])
    P.wait("gpsimd", "q7", g["_xa_q7v"])
    P.wait("gpsimd", "qa")
    P.wait("gpsimd", "qb")
    scat_done = 0
    for t8 in range(8):
        if scat_done:
            P.wait("gpsimd", "qg", scat_done)
        dma(P, "gpsimd", "qe", g["P7A"][:],
            g["d_xa_park"][t8 * 128:(t8 + 1) * 128, :])
        vpe = dma(P, "gpsimd", "qe", g["P7E"][:],
                  g["d_ex_park"][t8 * 128:(t8 + 1) * 128, :])
        vig = P.op("gpsimd", lambda e, t_=t8: e.indirect_dma_start(
            out=g["IDG7"][:], out_offset=None, in_=g["d_slotrec"],
            in_offset=IndirectOffsetOnAxis(ap=g["cGRW"][:, t_:t_ + 1], axis=0)),
            inc="qf", inc_n=16)
        P.wait("gpsimd", "qe", vpe)
        P.wait("gpsimd", "qf", vig)
        P.op("gpsimd", lambda e: e.indirect_dma_start(
            out=g["d_send"], out_offset=IndirectOffsetOnAxis(ap=g["IDG7"][:, 0:1],
                                                             axis=0),
            in_=g["P7A"][:], in_offset=None,
            bounds_check=NSLOT - 1, oob_is_err=False), inc="qg", inc_n=16)
        scat_done = P.op("gpsimd", lambda e: e.indirect_dma_start(
            out=g["d_send_ex"], out_offset=IndirectOffsetOnAxis(ap=g["IDG7"][:, 0:1],
                                                                axis=0),
            in_=g["P7E"][:], in_offset=None,
            bounds_check=NSLOT - 1, oob_is_err=False), inc="qg", inc_n=16)
    P.wait("gpsimd", "qg", scat_done)
    from concourse.aot_env import allow_ici_all_to_all

    def a2a(e):
        with allow_ici_all_to_all():
            return e.collective_compute(
                "AllToAll", ALU.bypass, replica_groups=[list(range(NC))],
                ins=[g["d_send"].tensor.ap().opt()],
                outs=[g["d_recv"].tensor.ap().opt()])
    P.op("gpsimd", a2a, inc="cc")

    def a2a_ex(e):
        with allow_ici_all_to_all():
            return e.collective_compute(
                "AllToAll", ALU.bypass, replica_groups=[list(range(NC))],
                ins=[g["d_send_ex"].tensor.ap().opt()],
                outs=[g["d_recv_ex"].tensor.ap().opt()])
    P.op("gpsimd", a2a_ex, inc="cc")
    g["_p7_cc"] = P.val("cc")
    if tap("slotrec", [BB * SEQ, 4], I32):
        P.wait("gpsimd", "qd", g["_slotrec_qd"])
        dma(P, "gpsimd", "qtap", g["tap_slotrec"], g["d_slotrec"])
    if tap("recv", [NSLOT, PAYW], BF16):
        P.wait("gpsimd", "cc", g["_p7_cc"])
        dma(P, "gpsimd", "qtap", g["tap_recv"], g["d_recv"])


def _mk_ap(base, offset, dims):
    return type(base)(base.tensor, offset, dims)


def _scag_ap(g, b):
    return _mk_ap(g["d_scag"], b * 512, [[1024, 8], [32, 16], [1, 32]])


def _slotrec_ap(g, b):
    return _mk_ap(g["d_slotrec"], b * 16384, [[128, 128], [4, 32], [1, 4]])


def _p8(nc, P, g, tap):
    psf = g["_psf"]
    pool, pool2 = g["_wpool"], g["_wpool2"]
    pxh = g["PMD"][:, 0:64].bitcast(BF16)

    P.wait("gpsimd", "cc", g["_p7_cc"])
    for r in range(NC):
        v = dma(P, "gpsimd", "qe", g["P7A"][:], g["d_recv"][r * 128:(r + 1) * 128, :])
        v2 = dma(P, "gpsimd", "qe", g["P7E"][:],
                 g["d_recv_ex"][r * 128:(r + 1) * 128, :])
        P.wait("vector", "qe", v2)
        if r == 0:
            P.op("vector", lambda e: e.tensor_copy(out=g["XSEL"][:], in_=g["P7A"][:]),
                 inc="dv")
            P.op("vector", lambda e: e.tensor_copy(out=g["EXF"][:], in_=g["P7E"][:]),
                 inc="dv")
        else:
            P.op("vector", lambda e: e.tensor_add(g["XSEL"][:], g["XSEL"][:],
                                                  g["P7A"][:]), inc="dv")
            P.op("vector", lambda e: e.tensor_add(g["EXF"][:], g["EXF"][:],
                                                  g["P7E"][:]), inc="dv")
        P.wait("gpsimd", "dv")
    P.op("vector", lambda e: e.tensor_scalar(
        out=g["HID"][:], in0=g["XSEL"][:, 0:D], scalar1=g["EXF"][:, 0:1],
        scalar2=g["EXF"][:, 1:2], op0=ALU.subtract, op1=ALU.mult), inc="dv")
    P.wait("scalar", "dv")
    P.op("scalar", lambda e: e.activation(out=g["ATTV"][:], in_=g["HID"][:],
                                          func=AF.Square, accum_out=g["ST1"][:, 0:1]),
         inc="ac")
    P.wait("vector", "ac")
    P.op("vector", lambda e: e.tensor_scalar_mul(g["ST1"][:, 0:1], g["ST1"][:, 0:1],
                                                 1.0 / D), inc="dv")
    P.wait("scalar", "dv")
    P.op("scalar", lambda e: e.activation(out=g["ST1"][:, 0:1], in_=g["ST1"][:, 0:1],
                                          func=AF.Ln, bias=_cst(g, EPS_RMS_B)),
         inc="ac")
    P.op("scalar", lambda e: e.activation(out=g["ST1"][:, 0:1], in_=g["ST1"][:, 0:1],
                                          func=AF.Exp, scale=-0.5), inc="ac")
    P.wait("vector", "ac")
    P.op("vector", lambda e: e.tensor_scalar_mul(g["P7A"][:, 0:D], g["HID"][:],
                                                 g["ST1"][:, 0:1]), inc="dv")
    if _P8STOP < 1:
        return
    l1w = _bias_ap(g, "f_ln1_w")
    for ch in range(DCH):
        if psf.get("pxh"):
            P.wait("tensor", psf["pxh"][0], psf["pxh"][1])
        P.wait("tensor", "dv")
        P.op("tensor", lambda e, ch_=ch:
             e.transpose(pxh, g["P7A"][:, ch_ * 128:(ch_ + 1) * 128], g["cEYEh"][:]),
             inc="mm")
        P.wait("vector", "mm")
        vcp = P.op("vector", lambda e, ch_=ch:
                   e.tensor_scalar_mul(g["QKVf"][:, ch_ * 128:(ch_ + 1) * 128],
                                       pxh, l1w[:, ch_:ch_ + 1]), inc="dv")
        psf["pxh"] = ("dv", vcp)

    if _P8STOP < 2:
        return
    bq = _bias_ap(g, "f_bq")
    bk = _bias_ap(g, "f_bk")
    bv = _bias_ap(g, "f_bv")
    P.wait("tensor", "dv")
    qkv_seq = [("f_wq", oh, "QR", bq) for oh in range(DCH)] + \
              [("f_wk", oh, "KR", bk) for oh in range(KVCH)] + \
              [("f_wv", oh, "VTt", bv) for oh in range(KVCH)]
    pool.prefetch(g["f_wq"][0, :, :], DCH * 128)
    for qi, (wname, oh, dst, bias) in enumerate(qkv_seq):
        if True:
            if qi + 1 < len(qkv_seq):
                nw, noh = qkv_seq[qi + 1][0], qkv_seq[qi + 1][1]
                pool.prefetch(g[nw][noh, :, :], DCH * 128)
            bn, lv = pool.take()
            pm = (g["PMA"][:, 0:128], g["PMA"][:, 512:640])[oh % 2]
            key = "p8q" + str(oh % 2)
            if psf.get(key):
                P.wait("tensor", psf[key][0], psf[key][1])
            P.wait("tensor", pool.sems[bn], lv)
            for k in range(DCH):
                P.op("tensor", lambda e, k_=k, pm_=pm, b_=bn:
                     e.matmul(pm_, g[b_][:, k_ * 128:(k_ + 1) * 128],
                              g["QKVf"][:, k_ * 128:(k_ + 1) * 128],
                              start=(k_ == 0), stop=(k_ == DCH - 1)),
                     inc="mm" if k == DCH - 1 else None)
            pool.consumed(bn, "mm", P.val("mm"))
            P.wait("vector", "mm")
            vev = P.op("vector", lambda e, oh_=oh, pm_=pm, d_=dst, bi_=bias:
                       e.tensor_scalar_add(g[d_][:, oh_ * 128:(oh_ + 1) * 128], pm_,
                                           bi_[:, oh_:oh_ + 1]), inc="dv")
            psf[key] = ("dv", vev)

    if _P8STOP < 3:
        return
    P.wait("tensor", "dv")
    P.op("tensor", lambda e: e.matmul(g["PS1"][0:1, 0:128], g["EXF"][:, 2:3],
                                      g["cEYEf"][:], start=True, stop=True), inc="mm")
    P.wait("vector", "mm")
    P.op("vector", lambda e: e.tensor_copy(out=g["ROW1"][:],
                                           in_=g["PS1"][0:1, 0:128]), inc="dv")
    P.wait("tensor", "dv")
    P.op("tensor", lambda e: e.matmul(g["PX"][:], g["cINVF"][:], g["ROW1"][:],
                                      start=True, stop=True), inc="mm")
    P.wait("vector", "mm")
    P.op("vector", lambda e: e.tensor_copy(out=g["ANG"][:], in_=g["PX"][:]), inc="dv")
    ti = g["T8"].bitcast(I32)
    TWO_PI = float(2 * np.pi)
    for (dst, shift) in [("SINt", 0.0), ("COSt", float(np.pi / 2))]:
        P.op("vector", lambda e, s_=shift: e.tensor_scalar(
            out=g["T8b"][:], in0=g["ANG"][:], scalar1=s_,
            scalar2=float(1.0 / TWO_PI), op0=ALU.add, op1=ALU.mult), inc="dv")
        P.op("vector", lambda e: e.tensor_copy(out=ti[:, 0:128], in_=g["T8b"][:]),
             inc="dv")
        P.op("vector", lambda e: e.tensor_copy(out=g["T8b"][:], in_=ti[:, 0:128]),
             inc="dv")
        P.op("vector", lambda e, d_=dst, s_=shift: e.tensor_scalar_add(
            g[d_][:], g["ANG"][:], s_), inc="dv")
        P.op("vector", lambda e, d_=dst: e.scalar_tensor_tensor(
            out=g[d_][:], in0=g["T8b"][:], scalar=-TWO_PI, in1=g[d_][:],
            op0=ALU.mult, op1=ALU.add), inc="dv")
        P.op("vector", lambda e, d_=dst: e.tensor_scalar(
            out=g["T8b"][:], in0=g[d_][:], scalar1=float(np.pi), scalar2=None,
            op0=ALU.is_gt), inc="dv")
        P.op("vector", lambda e, d_=dst: e.tensor_scalar(
            out=ti[:, 0:128].bitcast(F32), in0=g[d_][:], scalar1=float(-np.pi),
            scalar2=None, op0=ALU.is_lt), inc="dv")
        P.op("vector", lambda e: e.tensor_sub(g["T8b"][:], ti[:, 0:128].bitcast(F32),
                                              g["T8b"][:]), inc="dv")
        P.op("vector", lambda e, d_=dst: e.scalar_tensor_tensor(
            out=g[d_][:], in0=g["T8b"][:], scalar=TWO_PI, in1=g[d_][:],
            op0=ALU.mult, op1=ALU.add), inc="dv")
        P.wait("scalar", "dv")
        P.op("scalar", lambda e, d_=dst: e.activation(out=g[d_][:], in_=g[d_][:],
                                                      func=AF.Sin), inc="ac")
    P.wait("vector", "ac")
    for (buf, nh, scale) in [("QR", NH, INV_SQRT_HD), ("KR", NKV, 1.0)]:
        for h in range(nh):
            hap = g[buf][:, h * 128:(h + 1) * 128]
            P.wait("tensor", "dv")
            P.op("tensor", lambda e, a_=hap: e.matmul(
                g["PX"][:], g["cROT"][:], a_, start=True, stop=True), inc="mm")
            P.wait("vector", "mm")
            P.op("vector", lambda e: e.tensor_mul(g["T8"][:], g["PX"][:], g["SINt"][:]),
                 inc="dv")
            P.op("vector", lambda e, a_=hap: e.tensor_mul(g["T8b"][:], a_,
                                                          g["COSt"][:]), inc="dv")
            P.op("vector", lambda e: e.tensor_add(g["T8"][:], g["T8"][:], g["T8b"][:]),
                 inc="dv")
            P.op("vector", lambda e, a_=hap, s_=scale: e.tensor_scalar_mul(
                a_, g["T8"][:], s_), inc="dv")

    if _P8STOP < 4:
        return
    vkw = P.val("dv")
    P.wait("scalar", "dv", vkw)
    dma(P, "scalar", "q7", _kvin_k_ap(g), g["KR"][:])
    P.wait("tensor", "dv")
    for kv in range(NKV):
        if psf.get("pxh"):
            P.wait("tensor", psf["pxh"][0], psf["pxh"][1])
        P.op("tensor", lambda e, k_=kv: e.transpose(
            pxh, g["VTt"][:, k_ * 128:(k_ + 1) * 128], g["cEYEh"][:]), inc="mm")
        P.wait("vector", "mm")
        vcp = P.op("vector", lambda e, k_=kv: e.tensor_copy(
            out=g["P7A"][:, k_ * 128:(k_ + 1) * 128], in_=pxh), inc="dv")
        psf["pxh"] = ("dv", vcp)
    P.wait("scalar", "dv")
    dma(P, "scalar", "q7", _kvin_v_ap(g), g["P7A"][:, 0:NKV * 128])
    P.wait("gpsimd", "q7")
    P.op("gpsimd", lambda e: e.collective_compute(
        "AllGather", ALU.bypass, replica_groups=[list(range(NC))],
        ins=[g["d_kvin"].tensor.ap().opt()], outs=[g["d_kvout"].tensor.ap().opt()]),
        inc="cc")
    vkv_cc = P.val("cc")

    P.op("gpsimd", lambda e: e.iota(g["ATTS"][:], pattern=[[1, 1024]], base=0,
                                    channel_multiplier=0,
                                    allow_small_or_imprecise_dtypes=True), inc="gp")
    P.op("gpsimd", lambda e: e.iota(g["KS8"][:], pattern=[[128, 8]], base=0,
                                    channel_multiplier=1,
                                    allow_small_or_imprecise_dtypes=True), inc="gp")
    P.wait("vector", "gp")
    P.op("vector", lambda e: e.tensor_scalar(
        out=g["MASKM"][:], in0=g["ATTS"][:], scalar1=g["cSLOT"][:], scalar2=None,
        op0=ALU.is_le), inc="dv")
    P.op("vector", lambda e: e.tensor_scalar(
        out=g["NEGT"][:], in0=g["MASKM"][:], scalar1=1.0e30, scalar2=-1.0e30,
        op0=ALU.mult, op1=ALU.add), inc="dv")
    P.wait("tensor", "dv")
    P.op("tensor", lambda e: e.matmul(g["PS1"][0:1, 0:128], g["cSLOT"][:],
                                      g["cEYEf"][:], start=True, stop=True), inc="mm")
    P.wait("vector", "mm")
    P.op("vector", lambda e: e.tensor_copy(out=g["ROW1"][:],
                                           in_=g["PS1"][0:1, 0:128]), inc="dv")
    P.wait("tensor", "dv")
    P.op("tensor", lambda e: e.matmul(g["PX"][:], g["cONEr"][:], g["ROW1"][:],
                                      start=True, stop=True), inc="mm")
    P.wait("vector", "mm")
    P.op("vector", lambda e: e.tensor_copy(out=g["QROW8"][:], in_=g["PX"][:]),
         inc="dv")
    for ck in range(NC):
        P.op("vector", lambda e, c_=ck: e.tensor_scalar(
            out=g["MTCH"][:, c_ * 128:(c_ + 1) * 128], in0=g["QROW8"][:],
            scalar1=g["KS8"][:, c_:c_ + 1], scalar2=None, op0=ALU.is_ge), inc="dv")

    if _P8STOP < 5:
        return
    P.wait("sync", "cc", vkv_cc)
    wo_pool = pool
    for h in range(NH):
        kvh = h // 2
        if psf.get("att"):
            P.wait("sync", psf["att"][0], psf["att"][1])
        vk = dma(P, "sync", ("q4", "q5")[h % 2], g["KBH"][:], _kvout_k_ap(g, kvh))
        vv = dma(P, "sync", ("q4", "q5")[h % 2], g["VBH"][:], _kvout_v_ap(g, kvh))
        qh = g["QR"][:, h * 128:(h + 1) * 128]
        if psf.get("att"):
            P.wait("tensor", psf["att"][0], psf["att"][1])
        P.wait("tensor", ("q4", "q5")[h % 2], vv)
        for hf in range(2):
            P.op("tensor", lambda e, q_=qh, f_=hf: e.matmul(
                g["PMA"][:, f_ * 512:(f_ + 1) * 512], q_,
                g["KBH"][:, f_ * 512:(f_ + 1) * 512], start=True, stop=True),
                inc="mm" if hf == 1 else None)
        P.wait("vector", "mm")
        P.op("vector", lambda e: e.tensor_mul(g["ATTS"][:], g["PMA"][:],
                                              g["MASKM"][:]), inc="dv")
        P.op("vector", lambda e: e.tensor_add(g["ATTS"][:], g["ATTS"][:],
                                              g["NEGT"][:]), inc="dv")
        vats = P.val("dv")
        psf["att"] = ("dv", vats)
        P.op("vector", lambda e: e.reduce_max(g["ST1"][:, 0:1], g["ATTS"][:],
                                              axis=AXX), inc="dv")
        P.wait("tensor", "dv")
        P.op("tensor", lambda e: e.matmul(g["PS1"][0:1, 0:128], g["ST1"][:, 0:1],
                                          g["cEYEf"][:], start=True, stop=True),
             inc="mm")
        P.wait("vector", "mm")
        P.op("vector", lambda e: e.tensor_copy(out=g["ROW1"][:],
                                               in_=g["PS1"][0:1, 0:128]), inc="dv")
        P.op("vector", lambda e: e.reduce_max(g["ST1"][0:1, 1:2], g["ROW1"][:],
                                              axis=AXX), inc="dv")
        P.wait("tensor", "dv")
        P.op("tensor", lambda e: e.matmul(g["PX"][:, 0:1], g["cONEr"][:],
                                          g["ST1"][0:1, 1:2], start=True, stop=True),
             inc="mm")
        P.wait("vector", "mm")
        P.op("vector", lambda e: e.tensor_scalar_mul(g["MHB8"][:], g["PX"][:, 0:1],
                                                     -1.0), inc="dv")
        vmh = P.val("dv")
        P.wait("scalar", "dv", vmh)
        P.op("scalar", lambda e, h_=h: e.activation(
            out=g["ATTS"][:], in_=g["ATTS"][:], func=AF.Exp, bias=g["MHB8"][:],
            accum_out=g["DEN8"][:, h_:h_ + 1]), inc="ac")
        P.wait("vector", "ac")
        P.op("vector", lambda e, h_=h: e.reciprocal(g["REC8"][:, h_:h_ + 1],
                                                    g["DEN8"][:, h_:h_ + 1]), inc="dv")
        for ck in range(NC):
            P.op("tensor", lambda e, c_=ck, q_=qh: e.matmul(
                g["PMB"][:, c_ * 128:(c_ + 1) * 128],
                g["KBH"][:, c_ * 128:(c_ + 1) * 128], q_, start=True, stop=True),
                inc="mm" if ck == NC - 1 else None)
        P.wait("scalar", "mm")
        for ck in range(NC):
            P.op("scalar", lambda e, c_=ck: e.activation(
                out=g["EHS"][:, c_ * 128:(c_ + 1) * 128],
                in_=g["PMB"][:, c_ * 128:(c_ + 1) * 128], func=AF.Exp,
                bias=g["MHB8"][:]), inc="ac")
        P.wait("vector", "ac")
        P.op("vector", lambda e: e.tensor_mul(g["ETS"][:], g["EHS"][:], g["MTCH"][:]),
             inc="dv")
        vets = P.val("dv")
        P.wait("tensor", "dv")
        P.op("tensor", lambda e, h_=h: e.matmul(g["PS1"][0:1, 0:128],
                                                g["REC8"][:, h_:h_ + 1], g["cEYEf"][:],
                                                start=True, stop=True), inc="mm")
        P.wait("vector", "mm")
        P.op("vector", lambda e: e.tensor_copy(out=g["ROW1"][:],
                                               in_=g["PS1"][0:1, 0:128]), inc="dv")
        P.wait("tensor", "dv")
        P.op("tensor", lambda e: e.matmul(g["PX"][:], g["cONEr"][:],
                                          g["ROW1"][:], start=True, stop=True),
             inc="mm")
        P.wait("vector", "mm")
        P.op("vector", lambda e: e.tensor_copy(out=g["RBC8"][:], in_=g["PX"][:]),
             inc="dv")
        P.wait("tensor", "dv", vets)
        for ck in range(NC):
            P.op("tensor", lambda e, c_=ck: e.matmul(
                g["PMC"][:, 0:128], g["VBH"][:, c_ * 128:(c_ + 1) * 128],
                g["ETS"][:, c_ * 128:(c_ + 1) * 128],
                start=(c_ == 0), stop=(c_ == NC - 1)),
                inc="mm" if ck == NC - 1 else None)
        P.wait("vector", "mm")
        vav = P.op("vector", lambda e, h_=h: e.tensor_mul(
            g["AVO"][:, h_ * 128:(h_ + 1) * 128], g["PMC"][:, 0:128], g["RBC8"][:]),
            inc="dv")
        psf["att"] = ("dv", vav)

    if _P8STOP < 6:
        return
    P.wait("tensor", "dv")
    wo_pool.prefetch(g["f_wo"][0, :, :], DCH * 128)
    for oh in range(DCH):
        if oh + 1 < DCH:
            wo_pool.prefetch(g["f_wo"][oh + 1, :, :], DCH * 128)
        bn, lv = wo_pool.take()
        pm = (g["PMA"][:, 0:128], g["PMA"][:, 512:640])[oh % 2]
        key = "p8o" + str(oh % 2)
        if psf.get(key):
            P.wait("tensor", psf[key][0], psf[key][1])
        P.wait("tensor", wo_pool.sems[bn], lv)
        for k in range(DCH):
            P.op("tensor", lambda e, k_=k, pm_=pm, b_=bn:
                 e.matmul(pm_, g[b_][:, k_ * 128:(k_ + 1) * 128],
                          g["AVO"][:, k_ * 128:(k_ + 1) * 128],
                          start=(k_ == 0), stop=(k_ == DCH - 1)),
                 inc="mm" if k == DCH - 1 else None)
        wo_pool.consumed(bn, "mm", P.val("mm"))
        P.wait("vector", "mm")
        vev = P.op("vector", lambda e, oh_=oh, pm_=pm: e.tensor_copy(
            out=g["AOF"][:, oh_ * 128:(oh_ + 1) * 128], in_=pm_), inc="dv")
        psf[key] = ("dv", vev)
    for ch in range(DCH):
        if psf.get("pxh"):
            P.wait("tensor", psf["pxh"][0], psf["pxh"][1])
        P.wait("tensor", "dv")
        P.op("tensor", lambda e, c_=ch: e.transpose(
            pxh, g["AOF"][:, c_ * 128:(c_ + 1) * 128], g["cEYEh"][:]), inc="mm")
        P.wait("vector", "mm")
        vcp = P.op("vector", lambda e, c_=ch: e.tensor_add(
            g["X1B"][:, c_ * 128:(c_ + 1) * 128],
            g["HID"][:, c_ * 128:(c_ + 1) * 128], pxh), inc="dv")
        psf["pxh"] = ("dv", vcp)
    l2w = _bias_ap(g, "f_ln2_w")
    P.wait("scalar", "dv")
    P.op("scalar", lambda e: e.activation(out=g["ATTV"][:], in_=g["X1B"][:],
                                          func=AF.Square, accum_out=g["ST1"][:, 4:5]),
         inc="ac")
    P.wait("vector", "ac")
    P.op("vector", lambda e: e.tensor_scalar_mul(g["ST1"][:, 4:5], g["ST1"][:, 4:5],
                                                 1.0 / D), inc="dv")
    P.wait("scalar", "dv")
    P.op("scalar", lambda e: e.activation(out=g["ST1"][:, 4:5], in_=g["ST1"][:, 4:5],
                                          func=AF.Ln, bias=_cst(g, EPS_RMS_B)),
         inc="ac")
    P.op("scalar", lambda e: e.activation(out=g["ST1"][:, 4:5], in_=g["ST1"][:, 4:5],
                                          func=AF.Exp, scale=-0.5), inc="ac")
    P.wait("vector", "ac")
    P.op("vector", lambda e: e.tensor_scalar_mul(g["P7A"][:, 0:D], g["X1B"][:],
                                                 g["ST1"][:, 4:5]), inc="dv")
    for ch in range(DCH):
        if psf.get("pxh"):
            P.wait("tensor", psf["pxh"][0], psf["pxh"][1])
        P.wait("tensor", "dv")
        P.op("tensor", lambda e, c_=ch: e.transpose(
            pxh, g["P7A"][:, c_ * 128:(c_ + 1) * 128], g["cEYEh"][:]), inc="mm")
        P.wait("vector", "mm")
        vcp = P.op("vector", lambda e, c_=ch: e.tensor_scalar_mul(
            g["H2"][:, c_ * 128:(c_ + 1) * 128], pxh, l2w[:, c_:c_ + 1]), inc="dv")
        psf["pxh"] = ("dv", vcp)

    if _P8STOP < 7:
        return
    P.wait("tensor", "dv")
    pool.prefetch(g["f_w_gate"][0, :, :], DCH * 128)
    pool2.prefetch(g["f_w_up"][0, :, :], DCH * 128)
    for of in range(ICH):
        if of + 1 < ICH:
            pool.prefetch(g["f_w_gate"][of + 1, :, :], DCH * 128)
            pool2.prefetch(g["f_w_up"][of + 1, :, :], DCH * 128)
        b1, v1 = pool.take()
        b2, v2 = pool2.take()
        pmg = (g["PMA"][:, 0:128], g["PMA"][:, 512:640])[of % 2]
        pmu = (g["PMB"][:, 0:128], g["PMB"][:, 512:640])[of % 2]
        keyg, keyu = "p8g" + str(of % 2), "p8u" + str(of % 2)
        for (bn, vv, pm_, key, po) in [(b1, v1, pmg, keyg, pool),
                                       (b2, v2, pmu, keyu, pool2)]:
            if psf.get(key):
                P.wait("tensor", psf[key][0], psf[key][1])
            P.wait("tensor", po.sems[bn], vv)
            for k in range(DCH):
                P.op("tensor", lambda e, k_=k, pm2=pm_, b_=bn:
                     e.matmul(pm2, g[b_][:, k_ * 128:(k_ + 1) * 128],
                              g["H2"][:, k_ * 128:(k_ + 1) * 128],
                              start=(k_ == 0), stop=(k_ == DCH - 1)),
                     inc="mm" if k == DCH - 1 else None)
            po.consumed(bn, "mm", P.val("mm"))
        P.wait("scalar", "mm")
        scr8 = (g["T8"], g["T8b"])[of % 2]
        if psf.get("scr8" + str(of % 2)):
            P.wait("scalar", psf["scr8" + str(of % 2)][0],
                   psf["scr8" + str(of % 2)][1])
        P.op("scalar", lambda e, pm_=pmg, s_=scr8:
             e.activation(out=s_[:], in_=pm_, func=AF.Silu), inc="ac")
        vsg = P.val("ac")
        psf[keyg] = ("ac", vsg)
        P.wait("vector", "ac", vsg)
        vgl = P.op("vector", lambda e, of_=of, pm_=pmu, s_=scr8: e.tensor_mul(
            g["GLB"][:, of_ * 128:(of_ + 1) * 128], s_[:], pm_), inc="dv")
        psf[keyu] = ("dv", vgl)
        psf["scr8" + str(of % 2)] = ("dv", vgl)

    if _P8STOP < 8:
        return
    dpool = g.get("_wpoold") or WPool(P, g, ["WD0", "WD1"], ["d0", "d1"])
    g["_wpoold"] = dpool
    dpool.prefetch(g["f_w_down"][0, :, :], ICH * 128)
    for oh in range(DCH):
        if oh + 1 < DCH:
            dpool.prefetch(g["f_w_down"][oh + 1, :, :], ICH * 128)
        bn, lv = dpool.take()
        pm = (g["PMC"][:, 0:128], g["PMC"][:, 512:640])[oh % 2]
        key = "p8d" + str(oh % 2)
        if psf.get(key):
            P.wait("tensor", psf[key][0], psf[key][1])
        P.wait("tensor", dpool.sems[bn], lv)
        if oh == 0:
            P.wait("tensor", "dv")
        for k in range(ICH):
            P.op("tensor", lambda e, k_=k, pm_=pm, b_=bn:
                 e.matmul(pm_, g[b_][:, k_ * 128:(k_ + 1) * 128],
                          g["GLB"][:, k_ * 128:(k_ + 1) * 128],
                          start=(k_ == 0), stop=(k_ == ICH - 1)),
                 inc="mm" if k == ICH - 1 else None)
        dpool.consumed(bn, "mm", P.val("mm"))
        P.wait("vector", "mm")
        vev = P.op("vector", lambda e, oh_=oh, pm_=pm: e.tensor_copy(
            out=g["MLPf"][:, oh_ * 128:(oh_ + 1) * 128], in_=pm_), inc="dv")
        psf[key] = ("dv", vev)
    for ch in range(DCH):
        if psf.get("pxh"):
            P.wait("tensor", psf["pxh"][0], psf["pxh"][1])
        P.wait("tensor", "dv")
        P.op("tensor", lambda e, c_=ch: e.transpose(
            pxh, g["MLPf"][:, c_ * 128:(c_ + 1) * 128], g["cEYEh"][:]), inc="mm")
        P.wait("vector", "mm")
        vcp = P.op("vector", lambda e, c_=ch: e.tensor_add(
            g["PROC"][:, c_ * 128:(c_ + 1) * 128],
            g["X1B"][:, c_ * 128:(c_ + 1) * 128], pxh), inc="dv")
        psf["pxh"] = ("dv", vcp)
    P.wait("scalar", "dv")
    P.op("scalar", lambda e: e.activation(out=g["ST1"][:, 5:6], in_=g["EXF"][:, 3:4],
                                          func=AF.Sigmoid), inc="ac")
    P.wait("vector", "ac")
    P.op("vector", lambda e: e.tensor_scalar(
        out=g["ST1"][:, 6:7], in0=g["ST1"][:, 5:6], scalar1=-1.0, scalar2=1.0,
        op0=ALU.mult, op1=ALU.add), inc="dv")
    P.op("vector", lambda e: e.tensor_scalar_mul(g["ATTV"][:], g["HID"][:],
                                                 g["ST1"][:, 6:7]), inc="dv")
    P.op("vector", lambda e: e.scalar_tensor_tensor(
        out=g["PROC"][:], in0=g["PROC"][:], scalar=g["ST1"][:, 5:6], in1=g["ATTV"][:],
        op0=ALU.mult, op1=ALU.add), inc="dv")
    P.op("vector", lambda e: e.tensor_add(g["ATTV"][:], g["XSEL"][:, 0:D],
                                          g["PROC"][:]), inc="dv")
    P.op("vector", lambda e: e.tensor_copy(out=g["EXI"][:, 0:1], in_=g["EXF"][:, 4:5]),
         inc="dv")
    vfin = P.val("dv")
    P.wait("scalar", "dv", vfin)
    dma(P, "scalar", "qo1", g["out_sel"], g["ATTV"][:])
    dma(P, "scalar", "qo2", g["out_rowid"], g["EXI"][:, 0:1])


def _kvin_k_ap(g):
    return _mk_ap(g["d_kvin"], 0, [[128, 128], [16384, NKV], [1, 128]])


def _kvin_v_ap(g):
    return _mk_ap(g["d_kvin"], 131072, [[1024, 128], [128, NKV], [1, 128]])


def _kvout_k_ap(g, kvh):
    return _mk_ap(g["d_kvout"], kvh * 16384, [[128, 128], [KVPACK, NC], [1, 128]])


def _kvout_v_ap(g, kvh):
    return _mk_ap(g["d_kvout"], 131072 + kvh * 128,
                  [[1024, 128], [KVPACK, NC], [1, 128]])



def kernel(**inputs):
    global _LAST_RES
    in_maps = _prepare_in_maps(inputs)
    nc, taps = _build_program(debug=_DEBUG_TAPS)
    res = run_bass_kernel_spmd(nc, in_maps, core_ids=list(range(NC)))
    _LAST_RES = res
    return _assemble([r for r in res.results])
